# version 17
# baseline (speedup 1.0000x reference)
from contextlib import ExitStack

import numpy as np
import ml_dtypes

import concourse.bass as bass
import concourse.mybir as mybir
from concourse.bass_utils import run_bass_kernel_spmd

F32 = mybir.dt.float32
BF16 = mybir.dt.bfloat16
F16 = mybir.dt.float16
AF = mybir.ActivationFunctionType
ALU = mybir.AluOpType
AXX = mybir.AxisListType.X

S, DM, H, HD = 2048, 1024, 16, 64
NT, NDCH = S // 128, DM // 128
NE, NSH, MIDR, MIDS = 16, 2, 256, 512
TB, NTB = 512, S // 512
NU = NE + 2 * NSH
TAB = 2 * S - 1
NGRP = 8

ENGS = ("tensor", "vector", "scalar", "sync", "gpsimd")


class Ph:

    def __init__(self, nc, sems, cnt):
        self.nc, self.sems, self.cnt = nc, sems, cnt
        self.items = {e: [] for e in ENGS}

    def wait(self, eng, ev):
        if ev is not None:
            self.items[eng].append(("w", ev))

    def waits(self, eng, evs):
        for ev in evs:
            self.wait(eng, ev)

    def op(self, eng, fn, inc=None, amt=1):
        ev = None
        if inc is not None:
            self.cnt[inc] = self.cnt.get(inc, 0) + amt
            ev = (inc, self.cnt[inc])
        self.items[eng].append(("o", fn, inc, amt))
        return ev

    def dma(self, eng, out, in_, inc, slow=False):
        return self.op(eng, lambda h, o=out, i=in_, s=slow: h.dma_start(
            out=o, in_=i, allow_slow_non_contiguous=s), inc=inc, amt=16)

    def drain(self, eng):
        self.items[eng].append(("o", lambda h: h.drain(), None, 1))

    def emit(self):
        with self.nc.Block() as block:
            for eng in ENGS:
                items = self.items[eng]
                if not items:
                    continue

                def run(h, items=items):
                    waited = {}
                    pending = {}

                    def flush():
                        for sname in sorted(pending):
                            v = pending[sname]
                            if waited.get(sname, -1) < v:
                                h.wait_ge(self.sems[sname], v)
                                waited[sname] = v
                        pending.clear()

                    for it in items:
                        if it[0] == "w":
                            sname, v = it[1]
                            if pending.get(sname, -1) < v:
                                pending[sname] = v
                        else:
                            flush()
                            _, fn, inc, amt = it
                            ins = fn(h)
                            if inc is not None:
                                ins.then_inc(self.sems[inc], amt)
                    flush()

                getattr(block, eng)(run)


def _ap(t, off, dims):
    return bass.AP(t, int(off), [[int(a), int(b)] for a, b in dims])


def build_nc(stop=None):
    nc = bass.Bass("TRN2", target_bir_lowering=False)

    x_in = nc.dram_tensor("x", [S, DM], F32, kind="ExternalInput")
    w_in = nc.dram_tensor("w_in", [DM, 3 * DM], F32, kind="ExternalInput")
    w_gate = nc.dram_tensor("w_gate", [DM, 2 * DM], F32, kind="ExternalInput")
    b_gate = nc.dram_tensor("b_gate", [2 * DM], F32, kind="ExternalInput")
    amp_in = nc.dram_tensor("amp", [H, 21], F32, kind="ExternalInput")
    sharp_in = nc.dram_tensor("sharp", [H, 21], F32, kind="ExternalInput")
    off_in = nc.dram_tensor("off", [H, 21], F32, kind="ExternalInput")
    router_in = nc.dram_tensor("router_w", [DM, NE], F32, kind="ExternalInput")
    wg_r = nc.dram_tensor("wg_r", [NE, DM, MIDR], F32, kind="ExternalInput")
    wu_r = nc.dram_tensor("wu_r", [NE, DM, MIDR], F32, kind="ExternalInput")
    wd_r = nc.dram_tensor("wd_r", [NE, MIDR, DM], F32, kind="ExternalInput")
    wg_s = nc.dram_tensor("wg_s", [NSH, DM, MIDS], F32, kind="ExternalInput")
    wu_s = nc.dram_tensor("wu_s", [NSH, DM, MIDS], F32, kind="ExternalInput")
    wd_s = nc.dram_tensor("wd_s", [NSH, MIDS, DM], F32, kind="ExternalInput")
    identb_d = nc.dram_tensor("identb", [128, 128], BF16, kind="ExternalInput")
    idf32_d = nc.dram_tensor("idf32", [128, 128], F32, kind="ExternalInput")
    onesf_d = nc.dram_tensor("onesf", [128, 128], F32, kind="ExternalInput")
    ind2_d = nc.dram_tensor("ind2", [128, 2], BF16, kind="ExternalInput")
    reduc_d = nc.dram_tensor("reduc", [128, H], F32, kind="ExternalInput")
    relbc_d = nc.dram_tensor("relbc", [128, TAB], F32, kind="ExternalInput")
    out_d = nc.dram_tensor("out", [S, DM], F32, kind="ExternalOutput")
    etab_d = nc.dram_tensor("etab_scr", [H, TAB], F16)
    comb_d = nc.dram_tensor("comb_scr", [NE, S], F32)
    rbc_d = nc.dram_tensor("rbc_scr", [2, 512], F32)
    sums_d = nc.dram_tensor("sums_scr", [2, 512], F32)
    wbf_d = nc.dram_tensor("wbf_scr", [NU * 3 * 128, 2048], BF16)

    cnt = {k: 0 for k in ("t", "v", "s", "dx", "dga", "dgb", "dwa", "dwb", "dda", "ddb", "dea", "deb", "dba", "dbb", "dwba", "dwbb", "ddba", "ddbb", "dra", "drb", "dec", "ded", "dst", "dgc", "dwc", "ddc", "dbc", "dwbc", "ddbc", "dxx", "dee", "dem")}

    with ExitStack() as octx:
        sems = {n: octx.enter_context(nc.semaphore(n)) for n in cnt}

        def SB(ctx, name, shape, dt):
            return ctx.enter_context(nc.sbuf_tensor(name, shape, dt))

        def PS(ctx, name, shape, dt):
            return ctx.enter_context(nc.psum_tensor(name, shape, dt))

        xt = SB(octx, "xt", [128, NT * DM], F32)
        hf = SB(octx, "hf", [128, NDCH * S], BF16)
        identb = SB(octx, "identb_s", [128, 128], BF16)
        idf32 = SB(octx, "idf32_s", [128, 128], F32)
        onesf = SB(octx, "onesf_s", [128, 128], F32)
        ind2 = SB(octx, "ind2_s", [128, 2], BF16)
        rw_im = SB(octx, "rw_im", [128, NDCH * NE], BF16)
        bg_sb = SB(octx, "bg_sb", [128, 16], F32)

        def tev():
            return ("t", cnt["t"])

        def vev():
            return ("v", cnt["v"])

        def sev():
            return ("s", cnt["s"])

        with ExitStack() as ctx:
            ph = Ph(nc, sems, cnt)
            acc = SB(ctx, "b0_acc", [128, TAB], F32)
            rel = SB(ctx, "b0_rel", [128, TAB], F32)
            dbuf = SB(ctx, "b0_d", [128, TAB], F32)
            qbuf = SB(ctx, "b0_q", [128, TAB], F32)
            tab = SB(ctx, "b0_tab", [16, TAB], F32)
            etab16 = SB(ctx, "b0_etab", [16, TAB], F16)
            reduc = SB(ctx, "b0_reduc", [128, H], F32)
            amp_pad = SB(ctx, "b0_amp", [16, 24], F32)
            shp_pad = SB(ctx, "b0_shp", [16, 24], F32)
            off_pad = SB(ctx, "b0_off", [16, 24], F32)
            amp_pp = SB(ctx, "b0_amppp", [128, 1], F32)
            shp_pp = SB(ctx, "b0_shppp", [128, 1], F32)
            off_pp = SB(ctx, "b0_offpp", [128, 1], F32)
            nshp = SB(ctx, "b0_nshp", [128, 1], F32)
            p0 = PS(ctx, "b0_p0", [16, 1024], F32)

            ph.dma("scalar",
                   _ap(xt, 0, [(NT * DM, 128), (DM, NT), (1, DM)]),
                   _ap(x_in, 0, [(DM, 128), (128 * DM, NT), (1, DM)]), "dxx")
            ph.dma("sync", identb[:, :], identb_d[:, :], "dx")
            ph.dma("sync", idf32[:, :], idf32_d[:, :], "dx")
            ph.dma("sync", onesf[:, :], onesf_d[:, :], "dx")
            ph.dma("sync", ind2[:, :], ind2_d[:, :], "dx")
            ph.dma("sync", reduc[:, :], reduc_d[:, :], "dx")
            ph.dma("sync", rel[:, :], relbc_d[:, :], "dx")
            ph.dma("sync", bg_sb[:, :],
                   _ap(b_gate, 0, [(1, 128), (128, 16)]), "dx", slow=True)
            ph.dma("gpsimd",
                   _ap(rw_im, 0, [(NDCH * NE, 128), (NE, NDCH), (1, NE)]),
                   _ap(router_in, 0, [(NE, 128), (128 * NE, NDCH), (1, NE)]),
                   "dga")
            ph.op("vector", lambda h: h.memset(amp_pad[:, :], 0.0))
            ph.op("vector", lambda h: h.memset(shp_pad[:, :], 0.0))
            ph.op("vector", lambda h: h.memset(off_pad[:, :], 0.0))
            ev_ms = ph.op("vector", lambda h: h.memset(acc[:, :], 0.0), inc="v")
            ph.wait("sync", ev_ms)
            for (dst, src) in ((amp_pad, amp_in), (shp_pad, sharp_in),
                               (off_pad, off_in)):
                ph.dma("sync", _ap(dst, 0, [(24, 16), (1, 21)]),
                       _ap(src, 0, [(21, 16), (1, 21)]), "dx", slow=True)
            ev_rel = ("dx", cnt["dx"])
            ev_loads = ev_rel
            ph.wait("sync", ev_rel)
            ev_acc = None
            for jg in range(3):
                if ev_acc is not None:
                    ph.wait("sync", ev_acc)
                evg = []
                for (pp, pad) in ((amp_pp, amp_pad), (shp_pp, shp_pad),
                                  (off_pp, off_pad)):
                    evg.append(ph.dma(
                        "sync", pp[:, :],
                        _ap(pad, jg * 8, [(24, 16), (1, 8)]), "dx"))
                ph.waits("vector", evg)
                ph.wait("vector", ev_rel)
                ev_d = ph.op("vector", lambda h: h.tensor_scalar(
                    out=dbuf[:, :], in0=rel[:, :], scalar1=off_pp[:, :],
                    scalar2=None, op0=ALU.subtract), inc="v")
                ph.wait("scalar", ev_d)
                ph.waits("scalar", evg)
                ph.op("scalar", lambda h: h.activation(
                    out=qbuf[:, :], in_=dbuf[:, :], func=AF.Square))
                ph.drain("scalar")
                ph.op("scalar", lambda h: h.activation(
                    out=nshp[:, :], in_=shp_pp[:, :], func=AF.Abs))
                ph.drain("scalar")
                ph.op("scalar", lambda h: h.mul(
                    out=nshp[:, :], in_=nshp[:, :], mul=-1.0))
                ph.drain("scalar")
                ev_e = ph.op("scalar", lambda h: h.activation(
                    out=dbuf[:, :], in_=qbuf[:, :], func=AF.Exp,
                    scale=nshp[:, :]), inc="s")
                ph.wait("vector", ev_e)
                ev_acc = ph.op("vector", lambda h: h.scalar_tensor_tensor(
                    out=acc[:, :], in0=dbuf[:, :], scalar=amp_pp[:, :],
                    in1=acc[:, :], op0=ALU.mult, op1=ALU.add), inc="v")
                ph.drain("vector")
            ph.wait("tensor", ev_acc)
            ph.wait("tensor", ev_loads)
            ev_tabslot = [None, None]
            for i in range(8):
                n = min(512, TAB - i * 512)
                if ev_tabslot[i % 2] is not None:
                    ph.wait("tensor", ev_tabslot[i % 2])
                ev_mm = ph.op("tensor", lambda h, i=i, n=n: h.matmul(
                    p0[0:16, (i % 2) * 512: (i % 2) * 512 + n],
                    reduc[:, 0:16], acc[:, i * 512: i * 512 + n],
                    start=True, stop=True), inc="t")
                ph.wait("vector", ev_mm)
                ev_tabslot[i % 2] = ph.op(
                    "vector", lambda h, i=i, n=n: h.tensor_copy(
                        out=tab[0:16, i * 512: i * 512 + n],
                        in_=p0[0:16, (i % 2) * 512: (i % 2) * 512 + n]),
                    inc="v")
            ph.wait("scalar", ev_tabslot[0])
            ph.wait("scalar", ev_tabslot[1])
            ev_et = ph.op("scalar", lambda h: h.activation(
                out=etab16[0:16, :], in_=tab[0:16, :], func=AF.Exp), inc="s")
            ph.wait("sync", ev_et)
            ph.dma("sync", etab_d[:, :], etab16[0:16, :], "dx")
            ph.emit()

        b0dx, b0dga, b0dxx = cnt["dx"], cnt["dga"], cnt["dxx"]

        def blanket(ph):
            for e in ENGS:
                ph.wait(e, ("dx", b0dx))
                ph.wait(e, ("dga", b0dga))
                ph.wait(e, ("dxx", b0dxx))

        def ln_phase(tag):
            with ExitStack() as ctx:
                ph = Ph(nc, sems, cnt)
                blanket(ph)
                stats = SB(ctx, f"ln{tag}_st", [128, NT * 12], F32)
                mv = SB(ctx, f"ln{tag}_mv", [128, NT * 2], F32)
                vtmp = SB(ctx, f"ln{tag}_vt", [128, NT], F32)
                vrec = SB(ctx, f"ln{tag}_vr", [128, NT], F32)
                rstd = SB(ctx, f"ln{tag}_rs", [128, NT], F32)
                negmr = SB(ctx, f"ln{tag}_nm", [128, NT], F32)
                htb = SB(ctx, f"ln{tag}_ht", [128, 2 * DM], BF16)
                p_h = PS(ctx, f"ln{tag}_ph", [128, 2 * DM], BF16)

                for t in range(NT):
                    for hh in range(2):
                        ph.op("vector", lambda h, t=t, hh=hh: h.bn_stats(
                            out=stats[:, t * 12 + hh * 6: t * 12 + hh * 6 + 6],
                            in_=xt[:, t * DM + hh * 512: t * DM + hh * 512 + 512]))
                ph.drain("vector")
                for t in range(NT):
                    ph.op("vector", lambda h, t=t: h.bn_aggr(
                        out=mv[:, t * 2: t * 2 + 2],
                        in_=stats[:, t * 12: t * 12 + 12]))
                ph.drain("vector")
                ph.op("vector", lambda h: h.tensor_scalar(
                    out=vtmp[:, :], in0=_ap(mv, 1, [(NT * 2, 128), (2, NT)]),
                    scalar1=1e-5, scalar2=None, op0=ALU.add))
                ph.drain("vector")
                ev_rec = ph.op("vector", lambda h: h.reciprocal(
                    out=vrec[:, :], in_=vtmp[:, :]), inc="v")
                ph.wait("scalar", ev_rec)
                ev_rstd = ph.op("scalar", lambda h: h.activation(
                    out=rstd[:, :], in_=vrec[:, :], func=AF.Sqrt), inc="s")
                ph.wait("vector", ev_rstd)
                ev_negmr = ph.op("vector", lambda h: h.scalar_tensor_tensor(
                    out=negmr[:, :], in0=_ap(mv, 0, [(NT * 2, 128), (2, NT)]),
                    scalar=-1.0, in1=rstd[:, :],
                    op0=ALU.mult, op1=ALU.mult), inc="v")
                ph.wait("scalar", ev_negmr)
                ph.drain("scalar")

                ev_slot = [None, None]
                for t in range(NT):
                    sl = t % 2
                    if ev_slot[sl] is not None:
                        ph.wait("scalar", ev_slot[sl])
                    ev_ap = ph.op("scalar", lambda h, t=t, sl=sl: h.activation(
                        out=htb[:, sl * DM: sl * DM + DM],
                        in_=xt[:, t * DM: t * DM + DM], func=AF.Identity,
                        bias=negmr[:, t: t + 1], scale=rstd[:, t: t + 1]),
                        inc="s")
                    ph.wait("tensor", ev_ap)
                    if ev_slot[sl] is not None:
                        ph.wait("tensor", ev_slot[sl])
                    for c in range(NDCH):
                        ph.op("tensor", lambda h, sl=sl, c=c: h.transpose(
                            p_h[:, sl * DM + c * 128: sl * DM + c * 128 + 128],
                            htb[:, sl * DM + c * 128: sl * DM + c * 128 + 128],
                            identb[:, :]),
                            inc="t" if c == NDCH - 1 else None)
                    ev_tr = tev()
                    ph.wait("vector", ev_tr)
                    ev_slot[sl] = ph.op("vector", lambda h, t=t, sl=sl: h.tensor_copy(
                        out=_ap(hf, t * 128,
                                [(NDCH * S, 128), (S, NDCH), (1, 128)]),
                        in_=p_h[:, sl * DM: sl * DM + DM]), inc="v")
                ph.emit()

        wb_ev = {}

        def unit_gu_src(u):
            if u < NE:
                return [(wg_r, u * DM * MIDR,
                         [(MIDR, 128), (128 * MIDR, NDCH), (1, 256)]),
                        (wu_r, u * DM * MIDR,
                         [(MIDR, 128), (128 * MIDR, NDCH), (1, 256)])]
            n, mh = (u - NE) // 2, (u - NE) % 2
            return [(wg_s, n * DM * MIDS + mh * 256,
                     [(MIDS, 128), (128 * MIDS, NDCH), (1, 256)]),
                    (wu_s, n * DM * MIDS + mh * 256,
                     [(MIDS, 128), (128 * MIDS, NDCH), (1, 256)])]

        def unit_d_src(u):
            if u < NE:
                return (wd_r, u * MIDR * DM,
                        [(DM, 128), (128 * DM, 2), (1, 1024)])
            n, mh = (u - NE) // 2, (u - NE) % 2
            return (wd_s, (n * MIDS + mh * 256) * DM,
                    [(DM, 128), (128 * DM, 2), (1, 1024)])

        def ffn_phase(tag, first_call):
            with ExitStack() as ctx:
                ph = Ph(nc, sems, cnt)
                blanket(ph)
                P = [PS(ctx, f"f{tag}_p{i}", [128, 1024], F32) for i in range(4)]
                p_all = SB(ctx, f"f{tag}_pall", [128, NU * 1024], BF16)
                gsil = SB(ctx, f"f{tag}_gs", [128, 2 * 1024], BF16)
                GSEM = ("dga", "dgb", "dgc")
                WSEM = ("dwa", "dwb", "dwc")
                DSEM = ("dda", "ddb", "ddc")
                BSEM = ("dba", "dbb", "dbc")
                WBSEM = ("dwba", "dwbb", "dwbc")
                DBSEM = ("ddba", "ddbb", "ddbc")
                WQUEUE = ("sync", "sync", "sync")
                t1 = SB(ctx, f"f{tag}_t1", [128, 1024], F32)
                bc_sb = SB(ctx, f"f{tag}_bc", [128, 3 * TB], F32)
                stage = SB(ctx, f"f{tag}_stg", [128, 2048], F32)
                lg = SB(ctx, f"f{tag}_lg", [128, NT * 16], F32)
                m8 = SB(ctx, f"f{tag}_m8", [128, NT * 8], F32)
                d21 = SB(ctx, f"f{tag}_d21", [128, NT], F32)
                e2 = SB(ctx, f"f{tag}_e2", [128, NT], F32)
                w1 = SB(ctx, f"f{tag}_w1", [128, NT], F32)
                w2 = SB(ctx, f"f{tag}_w2", [128, NT], F32)
                eqt = SB(ctx, f"f{tag}_eq", [128, 16], F32)
                comb = SB(ctx, f"f{tag}_cmb", [128, NT * 16], F32)
                comb_fm = SB(ctx, f"f{tag}_cfm", [16, S], F32)
                wgu = [SB(ctx, f"f{tag}_wgu{i}", [128, 2 * 2048], BF16)
                       for i in range(3)]
                wdb = [SB(ctx, f"f{tag}_wdb{i}", [128, 2048], BF16)
                       for i in range(3)]

                gu_dma, wd_dma, bc_dma = {}, {}, {}
                gu_free = [None, None, None]
                wd_free = [None, None, None]
                bc_free = [None, None, None]
                pre_issued = first_call

                ev_prev = None
                for t in range(NT):
                    if ev_prev is not None:
                        ph.wait("tensor", ev_prev)
                    for c in range(NDCH):
                        ph.op("tensor", lambda h, t=t, c=c: h.matmul(
                            P[0][:, 0:16],
                            hf[:, c * S + t * 128: c * S + t * 128 + 128],
                            rw_im[:, c * 16: c * 16 + 16],
                            start=(c == 0), stop=(c == NDCH - 1)),
                            inc="t" if c == NDCH - 1 else None)
                    ev_mm = tev()
                    ph.wait("vector", ev_mm)
                    ev_prev = ph.op("vector", lambda h, t=t: h.tensor_copy(
                        out=lg[:, t * 16: t * 16 + 16], in_=P[0][:, 0:16]),
                        inc="v")
                ph.drain("vector")
                for t in range(NT):
                    ph.op("vector", lambda h, t=t: h.max(
                        out=m8[:, t * 8: t * 8 + 8],
                        in_=lg[:, t * 16: t * 16 + 16]))
                ph.drain("vector")
                ev_d21 = ph.op("vector", lambda h: h.tensor_sub(
                    d21[:, :], _ap(m8, 1, [(NT * 8, 128), (8, NT)]),
                    _ap(m8, 0, [(NT * 8, 128), (8, NT)])), inc="v")
                ph.drain("vector")
                ph.wait("scalar", ev_d21)
                ev_e2 = ph.op("scalar", lambda h: h.activation(
                    out=e2[:, :], in_=d21[:, :], func=AF.Exp), inc="s")
                ph.wait("vector", ev_e2)
                ph.op("vector", lambda h: h.tensor_scalar(
                    out=w1[:, :], in0=e2[:, :], scalar1=1.0, scalar2=None,
                    op0=ALU.add))
                ph.drain("vector")
                ph.op("vector", lambda h: h.reciprocal(out=w2[:, :], in_=w1[:, :]))
                ph.drain("vector")
                ph.op("vector", lambda h: h.tensor_scalar(
                    out=w1[:, :], in0=w2[:, :], scalar1=-1.0, scalar2=1.0,
                    op0=ALU.mult, op1=ALU.add))
                ph.drain("vector")
                ev_ct = [None, None]
                ev_cf = None
                for t in range(NT):
                    ph.op("vector", lambda h, t=t: h.tensor_scalar(
                        out=eqt[:, :], in0=lg[:, t * 16: t * 16 + 16],
                        scalar1=m8[:, t * 8: t * 8 + 1], scalar2=None,
                        op0=ALU.is_equal))
                    ph.drain("vector")
                    ph.op("vector", lambda h, t=t: h.tensor_scalar(
                        out=comb[:, t * 16: t * 16 + 16], in0=eqt[:, :],
                        scalar1=w2[:, t: t + 1], scalar2=None, op0=ALU.mult))
                    ph.drain("vector")
                    ph.op("vector", lambda h, t=t: h.tensor_scalar(
                        out=eqt[:, :], in0=lg[:, t * 16: t * 16 + 16],
                        scalar1=m8[:, t * 8 + 1: t * 8 + 2], scalar2=None,
                        op0=ALU.is_equal))
                    ph.drain("vector")
                    ev_c = ph.op("vector", lambda h, t=t: h.scalar_tensor_tensor(
                        out=comb[:, t * 16: t * 16 + 16], in0=eqt[:, :],
                        scalar=w1[:, t: t + 1],
                        in1=comb[:, t * 16: t * 16 + 16],
                        op0=ALU.mult, op1=ALU.add), inc="v")
                    ph.drain("vector")
                    ph.wait("tensor", ev_c)
                    sl = 1 + (t % 2)
                    slot = (t // 2) % 8
                    if ev_ct[t % 2] is not None:
                        ph.wait("tensor", ev_ct[t % 2])
                    ev_t = ph.op("tensor", lambda h, t=t, sl=sl, slot=slot: h.transpose(
                        P[sl][0:16, slot * 128: slot * 128 + 128],
                        comb[:, t * 16: t * 16 + 16], idf32[:, :]), inc="t")
                    ph.wait("vector", ev_t)
                    ev_cf = ph.op("vector", lambda h, t=t, sl=sl, slot=slot: h.tensor_copy(
                        out=comb_fm[0:16, t * 128: t * 128 + 128],
                        in_=P[sl][0:16, slot * 128: slot * 128 + 128]), inc="v")
                    ev_ct[t % 2] = ev_cf
                ph.wait("sync", ev_cf)
                ev_comb_d = ph.dma("sync", comb_d[:, :], comb_fm[0:16, :], "dx")

                def issue_gu(tb, u, wonly=False, skipw=False):
                    buf = u % 3
                    sg = GSEM[buf]
                    sw = WSEM[buf]
                    fresh = first_call and tb == 0
                    if skipw:
                        pass
                    elif fresh:
                        if gu_free[buf] is not None:
                            ph.wait("gpsimd", gu_free[buf])
                        evs = []
                        for i, (ten, off, dims) in enumerate(unit_gu_src(u)):
                            evs.append(ph.dma(
                                "gpsimd",
                                _ap(wgu[buf], i * 2048,
                                    [(2 * 2048, 128), (256, NDCH), (1, 256)]),
                                _ap(ten, off, dims), sg))
                        gu_dma[(tb, u)] = evs
                    else:
                        qw = WQUEUE[buf]
                        if gu_free[buf] is not None:
                            ph.wait(qw, gu_free[buf])
                        ph.wait(qw, wb_ev[(u, 0)])
                        gu_dma[(tb, u)] = [ph.dma(
                            qw, wgu[buf][:, :],
                            _ap(wbf_d, (u * 3) * 128 * 2048,
                                [(2048, 128), (128 * 2048, 2), (1, 2048)]),
                            sw)]
                    if wonly:
                        return
                    if u < NE:
                        if bc_free[buf] is not None:
                            ph.wait("sync", bc_free[buf])
                        ph.wait("sync", ev_comb_d)
                        bc_dma[(tb, u)] = ph.dma(
                            "sync", bc_sb[:, buf * TB: buf * TB + TB],
                            _ap(comb_d, u * S + tb * TB, [(0, 128), (1, TB)]),
                            BSEM[buf])

                def issue_wd(tb, u):
                    buf = u % 3
                    sg = GSEM[buf]
                    sd = DSEM[buf]
                    fresh = first_call and tb == 0
                    if fresh:
                        if wd_free[buf] is not None:
                            ph.wait("gpsimd", wd_free[buf])
                        if gu_free[buf] is not None:
                            ph.wait("gpsimd", gu_free[buf])
                        ten, off, dims = unit_d_src(u)
                        wd_dma[(tb, u)] = [ph.dma(
                            "gpsimd", wdb[buf][:, :], _ap(ten, off, dims), sg)]
                    else:
                        qw = WQUEUE[buf]
                        if wd_free[buf] is not None:
                            ph.wait(qw, wd_free[buf])
                        ph.wait(qw, wb_ev[(u, 2)])
                        wd_dma[(tb, u)] = [ph.dma(
                            qw, wdb[buf][:, :],
                            _ap(wbf_d, (u * 3 + 2) * 128 * 2048,
                                [(2048, 128), (1, 2048)]), sd)]

                gsil_free = [None, None]
                ev_adds = vev()
                if first_call:
                    for u0 in range(3):
                        issue_gu(0, u0, wonly=True)

                for tb in range(NTB):
                    fresh = first_call and tb == 0
                    issue_gu(tb, 0, skipw=pre_issued)
                    issue_gu(tb, 1, skipw=pre_issued)
                    issue_gu(tb, 2, skipw=pre_issued)
                    pre_issued = False
                    last_pmult = None
                    last_sil = None
                    for u in range(NU):
                        buf = u % 2
                        wbuf = u % 3
                        pg, pu = P[2 * buf], P[2 * buf + 1]
                        ph.waits("tensor", gu_dma[(tb, u)])
                        if gsil_free[buf] is not None:
                            ph.wait("tensor", gsil_free[buf])
                        ph.wait("tensor", ev_adds)
                        for mc in range(2):
                            for c in range(NDCH):
                                ph.op("tensor", lambda h, buf=wbuf, mc=mc, c=c,
                                      pg=pg, tb=tb: h.matmul(
                                    pg[:, mc * 512: mc * 512 + 512],
                                    wgu[buf][:, c * 256 + mc * 128:
                                             c * 256 + mc * 128 + 128],
                                    hf[:, c * S + tb * TB: c * S + tb * TB + TB],
                                    start=(c == 0), stop=(c == NDCH - 1)),
                                    inc="t" if (mc == 1 and c == NDCH - 1) else None)
                        ev_g = tev()
                        for mc in range(2):
                            for c in range(NDCH):
                                ph.op("tensor", lambda h, buf=wbuf, mc=mc, c=c,
                                      pu=pu, tb=tb: h.matmul(
                                    pu[:, mc * 512: mc * 512 + 512],
                                    wgu[buf][:, 2048 + c * 256 + mc * 128:
                                             2048 + c * 256 + mc * 128 + 128],
                                    hf[:, c * S + tb * TB: c * S + tb * TB + TB],
                                    start=(c == 0), stop=(c == NDCH - 1)),
                                    inc="t" if (mc == 1 and c == NDCH - 1) else None)
                        ev_u = tev()
                        gu_free[wbuf] = ev_u
                        if fresh:
                            ph.wait("sync", ev_u)
                            wb_ev[(u, 0)] = ph.dma(
                                "sync",
                                _ap(wbf_d, (u * 3) * 128 * 2048,
                                    [(2048, 128), (128 * 2048, 2), (1, 2048)]),
                                wgu[wbuf][:, :], WBSEM[wbuf])
                            gu_free[wbuf] = wb_ev[(u, 0)]
                        ph.wait("scalar", ev_g)
                        if gsil_free[buf] is not None:
                            ph.wait("scalar", gsil_free[buf])
                        ev_sil = ph.op("scalar", lambda h, pg=pg, buf=buf: h.activation(
                            out=gsil[:, buf * 1024: buf * 1024 + 1024],
                            in_=pg[:, :], func=AF.Silu), inc="s")
                        last_sil = ev_sil
                        ph.wait("vector", ev_u)
                        ph.wait("vector", ev_sil)
                        if u < NE:
                            ph.wait("vector", bc_dma[(tb, u)])
                            ev_t1 = ph.op("vector", lambda h, pu=pu, buf=wbuf: h.tensor_tensor(
                                out=t1[:, :].rearrange("p (a b) -> p a b", a=2),
                                in0=pu[:, :].rearrange("p (a b) -> p a b", a=2),
                                in1=bc_sb[:, buf * TB: buf * TB + TB]
                                    .unsqueeze(1).to_broadcast([128, 2, TB]),
                                op=ALU.mult), inc="v")
                            bc_free[wbuf] = ev_t1
                            ph.drain("vector")
                            ev_p = ph.op("vector", lambda h, u=u, buf=buf: h.tensor_tensor(
                                out=p_all[:, u * 1024: u * 1024 + 1024],
                                in0=t1[:, :],
                                in1=gsil[:, buf * 1024: buf * 1024 + 1024],
                                op=ALU.mult), inc="v")
                        else:
                            ev_p = ph.op("vector", lambda h, u=u, pu=pu, buf=buf: h.tensor_tensor(
                                out=p_all[:, u * 1024: u * 1024 + 1024],
                                in0=pu[:, :],
                                in1=gsil[:, buf * 1024: buf * 1024 + 1024],
                                op=ALU.mult), inc="v")
                        ph.drain("vector")
                        gsil_free[buf] = ev_p
                        last_pmult = ev_p
                        if u + 3 < NU:
                            issue_gu(tb, u + 3)

                    issue_wd(tb, 0)
                    issue_wd(tb, 1)
                    issue_wd(tb, 2)
                    ph.wait("tensor", last_pmult)
                    ph.wait("tensor", last_sil)
                    for u in range(NU):
                        wbuf = u % 3
                        ph.waits("tensor", wd_dma[(tb, u)])
                        for dch in range(NDCH):
                            po = P[dch // 2]
                            col = (dch % 2) * 512
                            for mc in range(2):
                                ph.op("tensor", lambda h, buf=wbuf, dch=dch,
                                      mc=mc, po=po, col=col, u=u: h.matmul(
                                    po[:, col: col + 512],
                                    wdb[buf][:, mc * 1024 + dch * 128:
                                             mc * 1024 + dch * 128 + 128],
                                    p_all[:, u * 1024 + mc * 512:
                                          u * 1024 + mc * 512 + 512],
                                    start=(u == 0 and mc == 0),
                                    stop=(u == NU - 1 and mc == 1)),
                                    inc="t" if (dch == NDCH - 1 and mc == 1) else None)
                        ev_wdused = tev()
                        wd_free[wbuf] = ev_wdused
                        if fresh:
                            ph.wait("sync", ev_wdused)
                            wb_ev[(u, 2)] = ph.dma(
                                "sync",
                                _ap(wbf_d, (u * 3 + 2) * 128 * 2048,
                                    [(2048, 128), (1, 2048)]),
                                wdb[wbuf][:, :], DBSEM[wbuf])
                            wd_free[wbuf] = wb_ev[(u, 2)]
                        if u + 3 < NU:
                            issue_wd(tb, u + 3)
                    ev_down = tev()

                    ev_adds_l = None
                    ev_tr_prev = None
                    for dh in range(2):
                        ph.wait("scalar", ev_down)
                        if ev_tr_prev is not None:
                            ph.wait("scalar", ev_tr_prev)
                        ev_st = []
                        for j in range(2):
                            ev_st.append(ph.op(
                                "scalar", lambda h, dh=dh, j=j: h.activation(
                                    out=stage[:, j * 1024: j * 1024 + 1024],
                                    in_=P[2 * dh + j][:, :], func=AF.Identity,
                                    scale=0.5), inc="s"))
                        for j in range(2):
                            ph.wait("tensor", ev_st[j])
                            for sub in range(4):
                                for dl in range(2):
                                    ph.op("tensor",
                                          lambda h, j=j, sub=sub, dl=dl, dh=dh: h.transpose(
                                        P[2 * dh + j][:, (sub * 2 + dl) * 128:
                                                      (sub * 2 + dl) * 128 + 128],
                                        stage[:, j * 1024 + dl * 512 + sub * 128:
                                              j * 1024 + dl * 512 + sub * 128 + 128],
                                        idf32[:, :]),
                                        inc="t" if (sub == 3 and dl == 1) else None)
                            ev_tr = tev()
                            ev_tr_prev = ev_tr
                            ph.wait("vector", ev_tr)
                            for sub in range(4):
                                tile = tb * 4 + sub
                                for dl in range(2):
                                    dch = (2 * dh + j) * 2 + dl
                                    ev_adds_l = ph.op(
                                        "vector", lambda h, tile=tile, dch=dch,
                                        j=j, sub=sub, dl=dl, dh=dh: h.tensor_tensor(
                                            out=xt[:, tile * DM + dch * 128:
                                                   tile * DM + dch * 128 + 128],
                                            in0=xt[:, tile * DM + dch * 128:
                                                   tile * DM + dch * 128 + 128],
                                            in1=P[2 * dh + j][:, (sub * 2 + dl) * 128:
                                                              (sub * 2 + dl) * 128 + 128],
                                            op=ALU.add), inc="v")
                            ph.drain("vector")
                    ev_adds = ev_adds_l
                ph.emit()

        def attn_phase():
            with ExitStack() as ctx:
                ph = Ph(nc, sems, cnt)
                blanket(ph)
                etab_ready = ("dx", b0dx)
                af = SB(ctx, "at_af", [128, NDCH * S], BF16)
                kT = SB(ctx, "at_kT", [128, S], BF16)
                vT = SB(ctx, "at_vT", [128, S], BF16)
                qT = SB(ctx, "at_qT", [128, S], BF16)
                kM = SB(ctx, "at_kM", [128, S], BF16)
                qM = SB(ctx, "at_qM", [128, S], BF16)
                vtm = SB(ctx, "at_vtm", [128, 16 * 65], BF16)
                pt = SB(ctx, "at_pt", [128, 2 * 2048], BF16)
                pexp = SB(ctx, "at_pexp", [128, 2 * 2048], BF16)
                ebuf = [SB(ctx, f"at_e{i}", [128, 2432], F16) for i in range(2)]
                win = [SB(ctx, f"at_w{i}", [128, 3 * 1024], BF16) for i in range(2)]
                wgp = win
                sigb = SB(ctx, "at_sig", [128, 2 * 512], BF16)
                gsg = SB(ctx, "at_gsg", [128, 2 * 512], F32)
                bca = SB(ctx, "at_bca", [128, 1024], F16)
                att_sb = SB(ctx, "at_attsb", [128, 2 * 512], F32)
                stag = SB(ctx, "at_stag", [64, 2 * 512], BF16)
                negmh = SB(ctx, "at_ngm", [128, 16], F32)
                mtmp = SB(ctx, "at_mt", [2, 16], F32)
                srec = SB(ctx, "at_srec", [128, 8], F32)
                rrec = SB(ctx, "at_rrec", [128, 8], F32)
                trow = SB(ctx, "at_tr", [1, 4], F32)
                srow = SB(ctx, "at_sr", [1, 4], F32)
                p_s = PS(ctx, "at_ps", [128, 2048], F32)
                p_pr = PS(ctx, "at_ppr", [128, 1024], F32)
                p_v = PS(ctx, "at_pv", [128, 1024], BF16)
                p_bc = PS(ctx, "at_pbc", [128, 8], F32)

                ev_ones = ph.op("vector", lambda h: h.memset(
                    _ap(vtm, 64, [(16 * 65, 128), (65, 16), (1, 1)]), 1.0),
                    inc="v")

                win_dma = {}
                win_free = [None, None]

                def issue_win(g):
                    buf = g % 2
                    if win_free[buf] is not None:
                        ph.wait("gpsimd", win_free[buf])
                    evs = []
                    for part in range(3):
                        evs.append(ph.dma(
                            "gpsimd",
                            _ap(win[buf], part * 1024,
                                [(3 * 1024, 128), (128, NDCH), (1, 128)]),
                            _ap(w_in, part * 1024 + g * 128,
                                [(3 * DM, 128), (128 * 3 * DM, NDCH), (1, 128)]),
                            "dga" if buf == 0 else "dgb"))
                    win_dma[g] = evs

                EQ = []
                for _g in range(NGRP):
                    for _hl in range(2):
                        for _qg in range(4):
                            EQ.append((_g * 2 + _hl, _qg))
                ESEM = ("dea", "deb")
                EQUEUE = ("sync", "scalar")
                e_dma_ev = [None] * len(EQ)
                e_free_ev = [None] * 2
                eq_ptr = [0]

                def issue_next_equad():
                    i = eq_ptr[0]
                    if i >= len(EQ):
                        return
                    eq_ptr[0] = i + 1
                    h_, qg = EQ[i]
                    buf = i % 2
                    qu = EQUEUE[buf]
                    if e_free_ev[buf] is not None:
                        ph.wait(qu, e_free_ev[buf])
                    ph.wait(qu, etab_ready)
                    e_dma_ev[i] = ph.dma(
                        qu, ebuf[buf][:, :],
                        _ap(etab_d, h_ * TAB + 512 * qg,
                            [(1, 128), (1, 2432)]), ESEM[buf])

                issue_win(0)
                issue_win(1)
                for _ in range(2):
                    issue_next_equad()
                eq_use = [0]
                ev_attn_done = []
                odd_dma_evs = []
                ev_vtmfree = None
                pending = [None]
                pend_b = [None]
                pbc_free = [None, None]

                def flush_a():
                    if pending[0] is None:
                        return
                    (par, fh, fqg, fhl, ev_evc) = pending[0]
                    pending[0] = None
                    ph.wait("tensor", ev_evc)
                    ph.wait("tensor", pbc_free[par])
                    for j in range(4):
                        ph.op("tensor", lambda h, par=par, j=j: h.transpose(
                            p_bc[:, par * 4 + j: par * 4 + j + 1],
                            bass.AP(att_sb, 64 * 1024 + par * 512 + j,
                                    [[1024, 1], [4, 128]]),
                            idf32[64:65, 64:65]),
                            inc="t" if j == 3 else None)
                    ev_tr4 = tev()
                    ph.wait("vector", ev_tr4)
                    ev_rin = ph.op("vector", lambda h, par=par: h.reciprocal(
                        out=rrec[:, par * 4: par * 4 + 4],
                        in_=p_bc[:, par * 4: par * 4 + 4]), inc="v")
                    pbc_free[par] = ev_rin
                    dsem = "dra" if par == 0 else "drb"
                    ph.wait("gpsimd", att_ev[par])
                    ph.wait("gpsimd", ev_rin)
                    ev_st2 = ph.dma(
                        "gpsimd", _ap(rbc_d, par * 512, [(4, 128), (1, 4)]),
                        rrec[:, par * 4: par * 4 + 4], dsem)
                    ph.wait("gpsimd", ev_st2)
                    ev_bc = ph.dma(
                        "gpsimd", bca[:, par * 512: par * 512 + 512],
                        _ap(rbc_d, par * 512, [(0, 128), (1, 512)]), dsem)
                    pend_b[0] = (par, fh, fqg, fhl, ev_evc, ev_bc)

                def flush_b():
                    if pend_b[0] is None:
                        return
                    (par, fh, fqg, fhl, ev_evc, ev_bc) = pend_b[0]
                    pend_b[0] = None
                    ph.wait("vector", ev_bc)
                    ph.wait("vector", ev_evc)
                    if fhl == 0:
                        ev_at = ph.op(
                            "vector", lambda h, fh=fh, fqg=fqg,
                            par=par: h.tensor_tensor(
                                out=af[0:64, (fh // 2) * S + fqg * 512:
                                       (fh // 2) * S + fqg * 512 + 512],
                                in0=att_sb[0:64, par * 512: par * 512 + 512],
                                in1=bca[0:64, par * 512: par * 512 + 512],
                                op=ALU.mult), inc="v")
                        ev_attn_done.append(ev_at)
                    else:
                        ssl = fqg % 2
                        if len(odd_dma_evs) >= 2:
                            ph.wait("vector", odd_dma_evs[-1])
                        ev_at = ph.op(
                            "vector", lambda h, ssl=ssl, par=par: h.tensor_tensor(
                                out=stag[0:64, ssl * 512: ssl * 512 + 512],
                                in0=att_sb[0:64, par * 512: par * 512 + 512],
                                in1=bca[0:64, par * 512: par * 512 + 512],
                                op=ALU.mult), inc="v")
                        ph.wait("gpsimd", ev_at)
                        odd_dma_evs.append(ph.dma(
                            "gpsimd",
                            af[64:128, (fh // 2) * S + fqg * 512:
                               (fh // 2) * S + fqg * 512 + 512],
                            stag[0:64, ssl * 512: ssl * 512 + 512], "dst"))
                    att_ev[par] = ev_at

                def flush_pending():
                    flush_a()
                    flush_b()
                expAB = [None, None]
                mult_ev = [[None, None], [None, None]]
                pvlast = [None, None]
                pat_free = [None, None]
                att_ev = [None, None]

                for g in range(NGRP):
                    wbuf = g % 2
                    flush_pending()
                    ph.waits("tensor", win_dma[g])
                    ph.wait("tensor", pat_free[0])
                    ph.wait("tensor", pat_free[1])
                    ev_evac = [None, None]
                    for part in range(3):
                        for tch in range(4):
                            sl = tch % 2
                            if ev_evac[sl] is not None:
                                ph.wait("tensor", ev_evac[sl])
                            for c in range(NDCH):
                                ph.op("tensor", lambda h, part=part, tch=tch,
                                      c=c, wbuf=wbuf, sl=sl: h.matmul(
                                    p_pr[:, sl * 512: sl * 512 + 512],
                                    win[wbuf][:, part * 1024 + c * 128:
                                              part * 1024 + c * 128 + 128],
                                    hf[:, c * S + tch * 512:
                                       c * S + tch * 512 + 512],
                                    start=(c == 0), stop=(c == NDCH - 1)),
                                    inc="t" if c == NDCH - 1 else None)
                            ev_mm = tev()
                            if part == 0:
                                ph.wait("scalar", ev_mm)
                                ev_evac[sl] = ph.op(
                                    "scalar", lambda h, tch=tch, sl=sl: h.copy(
                                        out=kT[:, tch * 512: tch * 512 + 512],
                                        in_=p_pr[:, sl * 512: sl * 512 + 512]),
                                    inc="s")
                            elif part == 1:
                                ph.wait("scalar", ev_mm)
                                ev_evac[sl] = ph.op(
                                    "scalar", lambda h, tch=tch, sl=sl: h.copy(
                                        out=vT[:, tch * 512: tch * 512 + 512],
                                        in_=p_pr[:, sl * 512: sl * 512 + 512]),
                                    inc="s")
                            else:
                                ph.wait("vector", ev_mm)
                                ev_evac[sl] = ph.op(
                                    "vector", lambda h, tch=tch, sl=sl: h.tensor_scalar(
                                        out=_ap(qT, 2047 - tch * 512,
                                                [(S, 128), (-1, 512)]),
                                        in0=p_pr[:, sl * 512: sl * 512 + 512],
                                        scalar1=0.125, scalar2=None,
                                        op0=ALU.mult), inc="v")
                    ev_proj_s = sev()
                    ev_proj_v = vev()
                    pat_free[0] = ev_evac[0]
                    pat_free[1] = ev_evac[1]
                    win_free[wbuf] = tev()
                    if g + 2 < NGRP:
                        issue_win(g + 2)
                    ph.wait("scalar", tev())
                    ph.wait("scalar", ev_proj_s)
                    ph.wait("scalar", ev_proj_v)
                    for (dst, srct) in ((kM, kT), (qM, qT)):
                        for bnd in range(2):
                            ph.dma("scalar", dst[bnd * 64:(bnd + 1) * 64, :],
                                   srct[(1 - bnd) * 64:(2 - bnd) * 64, :],
                                   "dem")
                    ev_mirror = ("dem", cnt["dem"])

                    ph.wait("scalar", ev_proj_v)
                    ph.wait("scalar", pvlast[0])
                    ph.wait("scalar", pvlast[1])
                    sqb = pt
                    ev_sq = [None, None]
                    ev_sq[0] = ph.op("scalar", lambda h: h.activation(
                        out=sqb[:, 0:S], in_=qT[:, :], func=AF.Square), inc="s")
                    ev_sq[1] = ph.op("scalar", lambda h: h.activation(
                        out=sqb[:, S: 2 * S], in_=kT[:, :], func=AF.Square),
                        inc="s")
                    red_ev = [expAB[0], expAB[0], expAB[1], expAB[1]]
                    for src in range(2):
                        ph.wait("tensor", ev_sq[src])
                        for i in range(4):
                            ph.wait("tensor", red_ev[i])
                            ev_nm = ph.op("tensor", lambda h, i=i, src=src:
                                          h.matmul(
                                p_s[0:2, i * 512: i * 512 + 512],
                                ind2[:, :],
                                sqb[:, src * S + i * 512:
                                    src * S + i * 512 + 512],
                                start=True, stop=True), inc="t")
                            ph.wait("vector", ev_nm)
                            red_ev[i] = ph.op(
                                "vector", lambda h, i=i, src=src: h.reduce_max(
                                    out=mtmp[0:2, src * 4 + i: src * 4 + i + 1],
                                    in_=p_s[0:2, i * 512: i * 512 + 512],
                                    axis=AXX), inc="v")
                    expAB[0] = red_ev[1]
                    expAB[1] = red_ev[3]
                    ph.drain("vector")
                    ph.op("vector", lambda h: h.reduce_max(
                        out=mtmp[0:2, 8:9], in_=mtmp[0:2, 0:4], axis=AXX))
                    ph.drain("vector")
                    ph.op("vector", lambda h: h.reduce_max(
                        out=mtmp[0:2, 9:10], in_=mtmp[0:2, 4:8], axis=AXX))
                    ph.drain("vector")
                    ev_t2 = ph.op("vector", lambda h: h.tensor_mul(
                        mtmp[0:2, 10:11], mtmp[0:2, 8:9], mtmp[0:2, 9:10]),
                        inc="v")
                    ph.wait("tensor", ev_t2)
                    ph.wait("tensor", expAB[0])
                    ev_trp = ph.op("tensor", lambda h: h.transpose(
                        p_s[0:1, 0:2], mtmp[0:2, 10:11], idf32[0:2, 0:2]),
                        inc="t")
                    ph.wait("vector", ev_trp)
                    ev_trc = ph.op("vector", lambda h: h.tensor_copy(
                        out=trow[0:1, 0:2], in_=p_s[0:1, 0:2]), inc="v")
                    ph.wait("scalar", ev_trc)
                    ev_sqr = ph.op("scalar", lambda h: h.activation(
                        out=srow[0:1, 0:2], in_=trow[0:1, 0:2], func=AF.Sqrt),
                        inc="s")
                    ph.wait("vector", ev_sqr)
                    ev_trow = ph.op("vector", lambda h: h.tensor_scalar(
                        out=trow[0:1, 2:4], in0=srow[0:1, 0:2],
                        scalar1=-1.0, scalar2=-0.5,
                        op0=ALU.mult, op1=ALU.add), inc="v")
                    ph.wait("tensor", ev_trow)
                    ph.wait("tensor", ev_trc)
                    ev_bcm = ph.op("tensor", lambda h: h.matmul(
                        p_s[:, 0:2], onesf[0:1, :], trow[0:1, 2:4],
                        start=True, stop=True), inc="t")
                    ph.wait("scalar", ev_bcm)
                    ev_negmh = ph.op("scalar", lambda h, g=g: h.copy(
                        out=negmh[:, g * 2: g * 2 + 2], in_=p_s[:, 0:2]),
                        inc="s")
                    ph.wait("tensor", ev_negmh)
                    expAB[0] = None

                    for hl in range(2):
                        h_ = g * 2 + hl
                        base = hl * 64
                        if ev_vtmfree is not None:
                            ph.wait("tensor", ev_vtmfree)
                        ph.wait("tensor", ev_proj_s)
                        ph.wait("tensor", ev_proj_v)
                        for c in range(16):
                            ph.op("tensor", lambda h, c=c, base=base: h.transpose(
                                p_v[:, c * 64: c * 64 + 64],
                                vT[base: base + 64, c * 128: c * 128 + 128],
                                identb[base: base + 64, base: base + 64]),
                                inc="t" if c == 15 else None)
                        ev_vtr = tev()
                        ph.wait("vector", ev_vtr)
                        ph.wait("vector", ev_ones)
                        ev_vtm = ph.op("vector", lambda h: h.tensor_copy(
                            out=_ap(vtm, 0, [(16 * 65, 128), (65, 16), (1, 64)]),
                            in_=_ap(p_v, 0, [(1024, 128), (64, 16), (1, 64)])),
                            inc="v")
                        ev_vtmfree = ev_vtm

                        for qg in range(4):
                            hgi = h_ * 4 + qg
                            par = hgi % 2
                            ev_s = [[None, None] for _ in range(4)]
                            ev_exp = [[None, None] for _ in range(4)]
                            ev_pv = [None] * 4

                            def emit_pv(q, qg=qg, par=par, ev_pv=ev_pv,
                                        ev_vtm=ev_vtm):
                                for j in range(4):
                                    c = q * 4 + j
                                    ph.wait("tensor", mult_ev[q % 2][j // 2])
                                    if c == 0:
                                        ph.wait("tensor", pat_free[par])
                                        ph.wait("tensor", ev_vtm)
                                    ph.op("tensor", lambda h, c=c, q=q, j=j,
                                          par=par: h.matmul(
                                        p_pr[0:65, par * 512: par * 512 + 512],
                                        vtm[:, c * 65: c * 65 + 65],
                                        pt[:, (q % 2) * 2048 + j * 512:
                                           (q % 2) * 2048 + j * 512 + 512],
                                        start=(c == 0), stop=(c == 15)),
                                        inc="t" if j == 3 else None)
                                ev_pv[q] = tev()
                                pvlast[q % 2] = ev_pv[q]

                            mbase = 64 - base
                            for q in range(4):
                                ph.wait("tensor", expAB[0])
                                ph.wait("tensor", ev_mirror)
                                for j in range(2):
                                    c = q * 4 + j
                                    kk = kT if j % 2 == 0 else kM
                                    qq = qT if j % 2 == 0 else qM
                                    bb = base if j % 2 == 0 else mbase
                                    ph.op("tensor", lambda h, c=c, bb=bb,
                                          kk=kk, qq=qq, qg=qg: h.matmul(
                                        p_s[:, (c % 4) * 512: (c % 4) * 512 + 512],
                                        kk[bb: bb + 64, c * 128: c * 128 + 128],
                                        qq[bb: bb + 64, qg * 512: qg * 512 + 512],
                                        start=True, stop=True),
                                        inc="t" if j == 1 else None)
                                ev_sA = tev()
                                ph.wait("tensor", expAB[1])
                                for j in range(2, 4):
                                    c = q * 4 + j
                                    kk = kT if j % 2 == 0 else kM
                                    qq = qT if j % 2 == 0 else qM
                                    bb = base if j % 2 == 0 else mbase
                                    ph.op("tensor", lambda h, c=c, bb=bb,
                                          kk=kk, qq=qq, qg=qg: h.matmul(
                                        p_s[:, (c % 4) * 512: (c % 4) * 512 + 512],
                                        kk[bb: bb + 64, c * 128: c * 128 + 128],
                                        qq[bb: bb + 64, qg * 512: qg * 512 + 512],
                                        start=True, stop=True),
                                        inc="t" if j == 3 else None)
                                ev_sB = tev()
                                if q == 1:
                                    flush_a()
                                if q == 2:
                                    emit_pv(0)
                                    emit_pv(1)
                                for half_i, ev_sx in ((0, ev_sA), (1, ev_sB)):
                                    ph.wait("scalar", ev_sx)
                                    ph.wait("scalar", ev_negmh)
                                    ph.wait("scalar", mult_ev[q % 2][half_i])
                                    ev_e_ = ph.op(
                                        "scalar", lambda h, q=q, half_i=half_i,
                                        hh=h_: h.activation(
                                            out=pexp[:, (q % 2) * 2048 +
                                                     half_i * 1024:
                                                     (q % 2) * 2048 +
                                                     half_i * 1024 + 1024],
                                            in_=p_s[:, half_i * 1024:
                                                    half_i * 1024 + 1024],
                                            func=AF.Exp,
                                            bias=negmh[:, hh: hh + 1]),
                                        inc="s")
                                    expAB[half_i] = ev_e_
                                    ev_exp[q][half_i] = ev_e_
                                ei = eq_use[0]
                                ebuf_i = ebuf[ei % 2]
                                for j in range(4):
                                    ph.wait("vector", ev_exp[q][j // 2])
                                    ph.wait("vector", e_dma_ev[ei])
                                    ph.wait("vector", pvlast[q % 2])
                                    mev = ph.op(
                                        "vector", lambda h, q=q, j=j,
                                        ebuf_i=ebuf_i: h.tensor_tensor(
                                            out=pt[:, (q % 2) * 2048 + j * 512:
                                                   (q % 2) * 2048 + j * 512
                                                   + 512],
                                            in0=pexp[:, (q % 2) * 2048 +
                                                     j * 512:
                                                     (q % 2) * 2048 +
                                                     j * 512 + 512],
                                            in1=ebuf_i[:, q * 512 + j * 128:
                                                       q * 512 + j * 128 + 512],
                                            op=ALU.mult), inc="v")
                                    if j % 2 == 1:
                                        mult_ev[q % 2][j // 2] = mev
                                if q == 3:
                                    eq_use[0] = ei + 1
                                    e_free_ev[ei % 2] = mult_ev[q % 2][1]
                                    issue_next_equad()
                            emit_pv(2)
                            emit_pv(3)
                            flush_b()
                            ph.wait("scalar", ev_pv[3])
                            ph.wait("scalar", att_ev[par])
                            ev_evc = ph.op("scalar", lambda h, par=par: h.copy(
                                out=att_sb[0:65, par * 512: par * 512 + 512],
                                in_=p_pr[0:65, par * 512: par * 512 + 512]),
                                inc="s")
                            pat_free[par] = ev_evc
                            pending[0] = (par, h_, qg, hl, ev_evc)

                flush_pending()
                wg_dma = {}
                wg_free = [None, None]

                def issue_wg(ja):
                    buf = ja % 2
                    if wg_free[buf] is not None:
                        ph.wait("gpsimd", wg_free[buf])
                    if win_free[buf] is not None:
                        ph.wait("gpsimd", win_free[buf])
                    evs = []
                    for i, j in enumerate((ja, ja + 8)):
                        evs.append(ph.dma(
                            "gpsimd",
                            _ap(wgp[buf], i * 1024,
                                [(3 * 1024, 128), (128, NDCH), (1, 128)]),
                            _ap(w_gate, j * 128,
                                [(2 * DM, 128), (128 * 2 * DM, NDCH), (1, 128)]),
                            "dga" if buf == 0 else "dgb"))
                    wg_dma[ja] = evs

                issue_wg(0)
                issue_wg(1)
                ph.waits("tensor", ev_attn_done)
                ph.waits("tensor", odd_dma_evs)
                ph.wait("tensor", expAB[0])
                ph.wait("tensor", expAB[1])
                ph.wait("tensor", pat_free[0])
                ph.wait("tensor", pat_free[1])
                ev_gsl = [None, None]
                ev_ppr_free = None
                for ja in range(8):
                    buf = ja % 2
                    ph.waits("tensor", wg_dma[ja])
                    for tch in range(4):
                        if ev_gsl[0] is not None:
                            ph.wait("tensor", ev_gsl[0])
                        if ev_gsl[1] is not None:
                            ph.wait("tensor", ev_gsl[1])
                        for i in range(2):
                            for c in range(NDCH):
                                ph.op("tensor", lambda h, ja=ja, tch=tch, i=i,
                                      c=c, buf=buf: h.matmul(
                                    p_s[:, i * 512: i * 512 + 512],
                                    wgp[buf][:, i * 1024 + c * 128:
                                             i * 1024 + c * 128 + 128],
                                    af[:, c * S + tch * 512:
                                       c * S + tch * 512 + 512],
                                    start=(c == 0), stop=(c == NDCH - 1)),
                                    inc="t" if c == NDCH - 1 else None)
                        ev_ag = tev()
                        ssl = tch % 2
                        ph.wait("scalar", ev_ag)
                        if ev_gsl[ssl] is not None:
                            ph.wait("scalar", ev_gsl[ssl])
                        ev_sig = ph.op("scalar", lambda h, ja=ja, ssl=ssl: h.activation(
                            out=sigb[:, ssl * 512: ssl * 512 + 512],
                            in_=p_s[:, 512:1024], func=AF.Sigmoid,
                            bias=bg_sb[:, 8 + ja: 9 + ja]), inc="s")
                        ph.wait("vector", ev_sig)
                        ev_gtd = ph.op("vector", lambda h, ja=ja, ssl=ssl: h.scalar_tensor_tensor(
                            out=_ap(gsg, ssl * 512 + 511,
                                    [(2 * 512, 128), (-1, 512)]),
                            in0=p_s[:, 0:512], scalar=bg_sb[:, ja: ja + 1],
                            in1=sigb[:, ssl * 512: ssl * 512 + 512],
                            op0=ALU.add, op1=ALU.mult), inc="v")
                        ph.drain("vector")
                        ev_gsl[ssl] = ev_gtd
                        ph.wait("tensor", ev_gtd)
                        if ev_ppr_free is not None:
                            ph.wait("tensor", ev_ppr_free)
                        for sub in range(4):
                            ph.op("tensor", lambda h, sub=sub, ssl=ssl: h.transpose(
                                p_pr[:, (ssl * 4 + sub) * 128:
                                     (ssl * 4 + sub) * 128 + 128],
                                gsg[:, ssl * 512 + sub * 128:
                                    ssl * 512 + sub * 128 + 128],
                                idf32[:, :]),
                                inc="t" if sub == 3 else None)
                        ev_tr = tev()
                        ph.wait("vector", ev_tr)
                        ev_add = None
                        for sub in range(4):
                            tile = (3 - tch) * 4 + sub
                            ev_add = ph.op("vector", lambda h, tile=tile, ja=ja,
                                           sub=sub, ssl=ssl: h.tensor_tensor(
                                out=xt[:, tile * DM + ja * 128:
                                       tile * DM + ja * 128 + 128],
                                in0=xt[:, tile * DM + ja * 128:
                                       tile * DM + ja * 128 + 128],
                                in1=p_pr[:, (ssl * 4 + sub) * 128:
                                         (ssl * 4 + sub) * 128 + 128],
                                op=ALU.add), inc="v")
                        ph.drain("vector")
                        ev_ppr_free = ev_add
                    wg_free[buf] = tev()
                    if ja + 2 < 8:
                        issue_wg(ja + 2)
                ph.emit()

        ln_phase(0)
        ffn_phase(0, first_call=True)
        if stop != "ffn1":
            ln_phase(1)
            attn_phase()
            if stop != "attn":
                ln_phase(2)
                ffn_phase(1, first_call=False)

        with ExitStack() as ctx:
            ph = Ph(nc, sems, cnt)
            ev = ph.dma("sync",
                        _ap(out_d, 0, [(DM, 128), (128 * DM, NT), (1, DM)]),
                        _ap(xt, 0, [(NT * DM, 128), (DM, NT), (1, DM)]), "dx")
            ph.wait("sync", ev)
            ph.op("sync", lambda h: h.nop())
            ph.emit()

    return nc


_CACHE = {}


def _consts():
    if "c" not in _CACHE:
        rel = (np.arange(TAB, dtype=np.float32) - (S - 1))
        ind2 = np.zeros((128, 2), dtype=ml_dtypes.bfloat16)
        ind2[0:64, 0] = 1
        ind2[64:128, 1] = 1
        reduc = np.zeros((128, H), dtype=np.float32)
        for p in range(128):
            reduc[p, p // 8] = 1.0
        _CACHE["c"] = {
            "identb": np.eye(128, dtype=ml_dtypes.bfloat16),
            "idf32": np.eye(128, dtype=np.float32),
            "onesf": np.ones((128, 128), dtype=np.float32),
            "ind2": ind2,
            "reduc": reduc,
            "relbc": np.broadcast_to(rel, (128, TAB)).copy(),
        }
    return _CACHE["c"]


def kernel(L0_attn_x, L0_attn_w_in, L0_attn_w_gate, L0_attn_b_gate,
           L0_attn_amplitudes, L0_attn_sharpness, L0_attn_offsets,
           L0_ffn_router_w, L0_ffn_wg_r, L0_ffn_wu_r, L0_ffn_wd_r,
           L0_ffn_wg_s, L0_ffn_wu_s, L0_ffn_wd_s):
    if "nc" not in _CACHE:
        _CACHE["nc"] = build_nc()
    nc = _CACHE["nc"]
    c = _consts()
    x = np.ascontiguousarray(np.asarray(L0_attn_x, dtype=np.float32))
    shared = {
        "w_in": np.ascontiguousarray(np.asarray(L0_attn_w_in, np.float32)),
        "w_gate": np.ascontiguousarray(np.asarray(L0_attn_w_gate, np.float32)),
        "b_gate": np.ascontiguousarray(np.asarray(L0_attn_b_gate, np.float32)),
        "amp": np.ascontiguousarray(np.asarray(L0_attn_amplitudes, np.float32)),
        "sharp": np.ascontiguousarray(np.asarray(L0_attn_sharpness, np.float32)),
        "off": np.ascontiguousarray(np.asarray(L0_attn_offsets, np.float32)),
        "router_w": np.ascontiguousarray(np.asarray(L0_ffn_router_w, np.float32)),
        "wg_r": np.ascontiguousarray(np.asarray(L0_ffn_wg_r, np.float32)),
        "wu_r": np.ascontiguousarray(np.asarray(L0_ffn_wu_r, np.float32)),
        "wd_r": np.ascontiguousarray(np.asarray(L0_ffn_wd_r, np.float32)),
        "wg_s": np.ascontiguousarray(np.asarray(L0_ffn_wg_s, np.float32)),
        "wu_s": np.ascontiguousarray(np.asarray(L0_ffn_wu_s, np.float32)),
        "wd_s": np.ascontiguousarray(np.asarray(L0_ffn_wd_s, np.float32)),
    }
    shared.update(c)
    in_maps = []
    for core in range(8):
        m = dict(shared)
        m["x"] = np.ascontiguousarray(x[core])
        in_maps.append(m)
    res = run_bass_kernel_spmd(nc, in_maps, core_ids=list(range(8)))
    return np.stack([r["out"] for r in res.results], axis=0)


# version 18
# speedup vs baseline: 1.0235x; 1.0235x over previous
from contextlib import ExitStack

import numpy as np
import ml_dtypes

import concourse.bass as bass
import concourse.mybir as mybir
from concourse.bass_utils import run_bass_kernel_spmd

F32 = mybir.dt.float32
BF16 = mybir.dt.bfloat16
F16 = mybir.dt.float16
AF = mybir.ActivationFunctionType
ALU = mybir.AluOpType
AXX = mybir.AxisListType.X

S, DM, H, HD = 2048, 1024, 16, 64
NT, NDCH = S // 128, DM // 128
NE, NSH, MIDR, MIDS = 16, 2, 256, 512
TB, NTB = 512, S // 512
NU = NE + 2 * NSH
TAB = 2 * S - 1
NGRP = 8

ENGS = ("tensor", "vector", "scalar", "sync", "gpsimd")


class Ph:

    def __init__(self, nc, sems, cnt):
        self.nc, self.sems, self.cnt = nc, sems, cnt
        self.items = {e: [] for e in ENGS}

    def wait(self, eng, ev):
        if ev is not None:
            self.items[eng].append(("w", ev))

    def waits(self, eng, evs):
        for ev in evs:
            self.wait(eng, ev)

    def op(self, eng, fn, inc=None, amt=1):
        ev = None
        if inc is not None:
            self.cnt[inc] = self.cnt.get(inc, 0) + amt
            ev = (inc, self.cnt[inc])
        self.items[eng].append(("o", fn, inc, amt))
        return ev

    def dma(self, eng, out, in_, inc, slow=False):
        return self.op(eng, lambda h, o=out, i=in_, s=slow: h.dma_start(
            out=o, in_=i, allow_slow_non_contiguous=s), inc=inc, amt=16)

    def drain(self, eng):
        self.items[eng].append(("o", lambda h: h.drain(), None, 1))

    def emit(self):
        with self.nc.Block() as block:
            for eng in ENGS:
                items = self.items[eng]
                if not items:
                    continue

                def run(h, items=items):
                    waited = {}
                    pending = {}

                    def flush():
                        for sname in sorted(pending):
                            v = pending[sname]
                            if waited.get(sname, -1) < v:
                                h.wait_ge(self.sems[sname], v)
                                waited[sname] = v
                        pending.clear()

                    for it in items:
                        if it[0] == "w":
                            sname, v = it[1]
                            if pending.get(sname, -1) < v:
                                pending[sname] = v
                        else:
                            flush()
                            _, fn, inc, amt = it
                            ins = fn(h)
                            if inc is not None:
                                ins.then_inc(self.sems[inc], amt)
                    flush()

                getattr(block, eng)(run)


def _ap(t, off, dims):
    return bass.AP(t, int(off), [[int(a), int(b)] for a, b in dims])


def build_nc(stop=None):
    nc = bass.Bass("TRN2", target_bir_lowering=False)

    x_in = nc.dram_tensor("x", [S, DM], F32, kind="ExternalInput")
    w_in = nc.dram_tensor("w_in", [DM, 3 * DM], F32, kind="ExternalInput")
    w_gate = nc.dram_tensor("w_gate", [DM, 2 * DM], F32, kind="ExternalInput")
    b_gate = nc.dram_tensor("b_gate", [2 * DM], F32, kind="ExternalInput")
    amp_in = nc.dram_tensor("amp", [H, 21], F32, kind="ExternalInput")
    sharp_in = nc.dram_tensor("sharp", [H, 21], F32, kind="ExternalInput")
    off_in = nc.dram_tensor("off", [H, 21], F32, kind="ExternalInput")
    router_in = nc.dram_tensor("router_w", [DM, NE], F32, kind="ExternalInput")
    wg_r = nc.dram_tensor("wg_r", [NE, DM, MIDR], F32, kind="ExternalInput")
    wu_r = nc.dram_tensor("wu_r", [NE, DM, MIDR], F32, kind="ExternalInput")
    wd_r = nc.dram_tensor("wd_r", [NE, MIDR, DM], F32, kind="ExternalInput")
    wg_s = nc.dram_tensor("wg_s", [NSH, DM, MIDS], F32, kind="ExternalInput")
    wu_s = nc.dram_tensor("wu_s", [NSH, DM, MIDS], F32, kind="ExternalInput")
    wd_s = nc.dram_tensor("wd_s", [NSH, MIDS, DM], F32, kind="ExternalInput")
    identb_d = nc.dram_tensor("identb", [128, 128], BF16, kind="ExternalInput")
    idf32_d = nc.dram_tensor("idf32", [128, 128], F32, kind="ExternalInput")
    onesf_d = nc.dram_tensor("onesf", [128, 128], F32, kind="ExternalInput")
    ind2_d = nc.dram_tensor("ind2", [128, 2], BF16, kind="ExternalInput")
    reduc_d = nc.dram_tensor("reduc", [128, H], F32, kind="ExternalInput")
    relbc_d = nc.dram_tensor("relbc", [128, TAB], F32, kind="ExternalInput")
    out_d = nc.dram_tensor("out", [S, DM], F32, kind="ExternalOutput")
    etab_d = nc.dram_tensor("etab_scr", [H, TAB], F16)
    comb_d = nc.dram_tensor("comb_scr", [NE, S], F32)
    rbc_d = nc.dram_tensor("rbc_scr", [2, 512], F32)
    sums_d = nc.dram_tensor("sums_scr", [2, 512], F32)
    wbf_d = nc.dram_tensor("wbf_scr", [NU * 3 * 128, 2048], BF16)

    cnt = {k: 0 for k in ("t", "v", "s", "dx", "dga", "dgb", "dwa", "dwb", "dda", "ddb", "dea", "deb", "dba", "dbb", "dwba", "dwbb", "ddba", "ddbb", "dra", "drb", "dec", "ded", "dst", "dgc", "dwc", "ddc", "dbc", "dwbc", "ddbc", "dxx", "dee", "dem")}

    with ExitStack() as octx:
        sems = {n: octx.enter_context(nc.semaphore(n)) for n in cnt}

        def SB(ctx, name, shape, dt):
            return ctx.enter_context(nc.sbuf_tensor(name, shape, dt))

        def PS(ctx, name, shape, dt):
            return ctx.enter_context(nc.psum_tensor(name, shape, dt))

        xt = SB(octx, "xt", [128, NT * DM], F32)
        hf = SB(octx, "hf", [128, NDCH * S], BF16)
        identb = SB(octx, "identb_s", [128, 128], BF16)
        idf32 = SB(octx, "idf32_s", [128, 128], F32)
        onesf = SB(octx, "onesf_s", [128, 128], F32)
        ind2 = SB(octx, "ind2_s", [128, 2], BF16)
        rw_im = SB(octx, "rw_im", [128, NDCH * NE], BF16)
        bg_sb = SB(octx, "bg_sb", [128, 16], F32)

        def tev():
            return ("t", cnt["t"])

        def vev():
            return ("v", cnt["v"])

        def sev():
            return ("s", cnt["s"])

        with ExitStack() as ctx:
            ph = Ph(nc, sems, cnt)
            acc = SB(ctx, "b0_acc", [128, TAB], F32)
            rel = SB(ctx, "b0_rel", [128, TAB], F32)
            dbuf = SB(ctx, "b0_d", [128, TAB], F32)
            qbuf = SB(ctx, "b0_q", [128, TAB], F32)
            tab = SB(ctx, "b0_tab", [16, TAB], F32)
            etab16 = SB(ctx, "b0_etab", [16, TAB], F16)
            reduc = SB(ctx, "b0_reduc", [128, H], F32)
            amp_pad = SB(ctx, "b0_amp", [16, 24], F32)
            shp_pad = SB(ctx, "b0_shp", [16, 24], F32)
            off_pad = SB(ctx, "b0_off", [16, 24], F32)
            amp_pp = SB(ctx, "b0_amppp", [128, 1], F32)
            shp_pp = SB(ctx, "b0_shppp", [128, 1], F32)
            off_pp = SB(ctx, "b0_offpp", [128, 1], F32)
            nshp = SB(ctx, "b0_nshp", [128, 1], F32)
            p0 = PS(ctx, "b0_p0", [16, 1024], F32)

            ph.dma("scalar",
                   _ap(xt, 0, [(NT * DM, 128), (DM, NT), (1, DM)]),
                   _ap(x_in, 0, [(DM, 128), (128 * DM, NT), (1, DM)]), "dxx")
            ph.dma("sync", identb[:, :], identb_d[:, :], "dx")
            ph.dma("sync", idf32[:, :], idf32_d[:, :], "dx")
            ph.dma("sync", onesf[:, :], onesf_d[:, :], "dx")
            ph.dma("sync", ind2[:, :], ind2_d[:, :], "dx")
            ph.dma("sync", reduc[:, :], reduc_d[:, :], "dx")
            ph.dma("sync", rel[:, :], relbc_d[:, :], "dx")
            ph.dma("sync", bg_sb[:, :],
                   _ap(b_gate, 0, [(1, 128), (128, 16)]), "dx", slow=True)
            ph.dma("gpsimd",
                   _ap(rw_im, 0, [(NDCH * NE, 128), (NE, NDCH), (1, NE)]),
                   _ap(router_in, 0, [(NE, 128), (128 * NE, NDCH), (1, NE)]),
                   "dga")
            ph.op("vector", lambda h: h.memset(amp_pad[:, :], 0.0))
            ph.op("vector", lambda h: h.memset(shp_pad[:, :], 0.0))
            ph.op("vector", lambda h: h.memset(off_pad[:, :], 0.0))
            ev_ms = ph.op("vector", lambda h: h.memset(acc[:, :], 0.0), inc="v")
            ph.wait("sync", ev_ms)
            for (dst, src) in ((amp_pad, amp_in), (shp_pad, sharp_in),
                               (off_pad, off_in)):
                ph.dma("sync", _ap(dst, 0, [(24, 16), (1, 21)]),
                       _ap(src, 0, [(21, 16), (1, 21)]), "dx", slow=True)
            ev_rel = ("dx", cnt["dx"])
            ev_loads = ev_rel
            ph.wait("sync", ev_rel)
            ev_acc = None
            for jg in range(3):
                if ev_acc is not None:
                    ph.wait("sync", ev_acc)
                evg = []
                for (pp, pad) in ((amp_pp, amp_pad), (shp_pp, shp_pad),
                                  (off_pp, off_pad)):
                    evg.append(ph.dma(
                        "sync", pp[:, :],
                        _ap(pad, jg * 8, [(24, 16), (1, 8)]), "dx"))
                ph.waits("vector", evg)
                ph.wait("vector", ev_rel)
                ev_d = ph.op("vector", lambda h: h.tensor_scalar(
                    out=dbuf[:, :], in0=rel[:, :], scalar1=off_pp[:, :],
                    scalar2=None, op0=ALU.subtract), inc="v")
                ph.wait("scalar", ev_d)
                ph.waits("scalar", evg)
                ph.op("scalar", lambda h: h.activation(
                    out=qbuf[:, :], in_=dbuf[:, :], func=AF.Square))
                ph.drain("scalar")
                ph.op("scalar", lambda h: h.activation(
                    out=nshp[:, :], in_=shp_pp[:, :], func=AF.Abs))
                ph.drain("scalar")
                ph.op("scalar", lambda h: h.mul(
                    out=nshp[:, :], in_=nshp[:, :], mul=-1.0))
                ph.drain("scalar")
                ev_e = ph.op("scalar", lambda h: h.activation(
                    out=dbuf[:, :], in_=qbuf[:, :], func=AF.Exp,
                    scale=nshp[:, :]), inc="s")
                ph.wait("vector", ev_e)
                ev_acc = ph.op("vector", lambda h: h.scalar_tensor_tensor(
                    out=acc[:, :], in0=dbuf[:, :], scalar=amp_pp[:, :],
                    in1=acc[:, :], op0=ALU.mult, op1=ALU.add), inc="v")
                ph.drain("vector")
            ph.wait("tensor", ev_acc)
            ph.wait("tensor", ev_loads)
            ev_tabslot = [None, None]
            for i in range(8):
                n = min(512, TAB - i * 512)
                if ev_tabslot[i % 2] is not None:
                    ph.wait("tensor", ev_tabslot[i % 2])
                ev_mm = ph.op("tensor", lambda h, i=i, n=n: h.matmul(
                    p0[0:16, (i % 2) * 512: (i % 2) * 512 + n],
                    reduc[:, 0:16], acc[:, i * 512: i * 512 + n],
                    start=True, stop=True), inc="t")
                ph.wait("vector", ev_mm)
                ev_tabslot[i % 2] = ph.op(
                    "vector", lambda h, i=i, n=n: h.tensor_copy(
                        out=tab[0:16, i * 512: i * 512 + n],
                        in_=p0[0:16, (i % 2) * 512: (i % 2) * 512 + n]),
                    inc="v")
            ph.wait("scalar", ev_tabslot[0])
            ph.wait("scalar", ev_tabslot[1])
            ev_et = ph.op("scalar", lambda h: h.activation(
                out=etab16[0:16, :], in_=tab[0:16, :], func=AF.Exp), inc="s")
            ph.wait("sync", ev_et)
            ph.dma("sync", etab_d[:, :], etab16[0:16, :], "dx")
            ph.emit()

        b0dx, b0dga, b0dxx = cnt["dx"], cnt["dga"], cnt["dxx"]

        def blanket(ph):
            for e in ENGS:
                ph.wait(e, ("dx", b0dx))
                ph.wait(e, ("dga", b0dga))
                ph.wait(e, ("dxx", b0dxx))

        def ln_phase(tag):
            with ExitStack() as ctx:
                ph = Ph(nc, sems, cnt)
                blanket(ph)
                stats = SB(ctx, f"ln{tag}_st", [128, NT * 12], F32)
                mv = SB(ctx, f"ln{tag}_mv", [128, NT * 2], F32)
                vtmp = SB(ctx, f"ln{tag}_vt", [128, NT], F32)
                vrec = SB(ctx, f"ln{tag}_vr", [128, NT], F32)
                rstd = SB(ctx, f"ln{tag}_rs", [128, NT], F32)
                negmr = SB(ctx, f"ln{tag}_nm", [128, NT], F32)
                htb = SB(ctx, f"ln{tag}_ht", [128, 2 * DM], BF16)
                p_h = PS(ctx, f"ln{tag}_ph", [128, 2 * DM], BF16)

                for t in range(NT):
                    for hh in range(2):
                        ph.op("vector", lambda h, t=t, hh=hh: h.bn_stats(
                            out=stats[:, t * 12 + hh * 6: t * 12 + hh * 6 + 6],
                            in_=xt[:, t * DM + hh * 512: t * DM + hh * 512 + 512]))
                ph.drain("vector")
                for t in range(NT):
                    ph.op("vector", lambda h, t=t: h.bn_aggr(
                        out=mv[:, t * 2: t * 2 + 2],
                        in_=stats[:, t * 12: t * 12 + 12]))
                ph.drain("vector")
                ph.op("vector", lambda h: h.tensor_scalar(
                    out=vtmp[:, :], in0=_ap(mv, 1, [(NT * 2, 128), (2, NT)]),
                    scalar1=1e-5, scalar2=None, op0=ALU.add))
                ph.drain("vector")
                ev_rec = ph.op("vector", lambda h: h.reciprocal(
                    out=vrec[:, :], in_=vtmp[:, :]), inc="v")
                ph.wait("scalar", ev_rec)
                ev_rstd = ph.op("scalar", lambda h: h.activation(
                    out=rstd[:, :], in_=vrec[:, :], func=AF.Sqrt), inc="s")
                ph.wait("vector", ev_rstd)
                ev_negmr = ph.op("vector", lambda h: h.scalar_tensor_tensor(
                    out=negmr[:, :], in0=_ap(mv, 0, [(NT * 2, 128), (2, NT)]),
                    scalar=-1.0, in1=rstd[:, :],
                    op0=ALU.mult, op1=ALU.mult), inc="v")
                ph.wait("scalar", ev_negmr)
                ph.drain("scalar")

                ev_slot = [None, None]
                for t in range(NT):
                    sl = t % 2
                    if ev_slot[sl] is not None:
                        ph.wait("scalar", ev_slot[sl])
                    ev_ap = ph.op("scalar", lambda h, t=t, sl=sl: h.activation(
                        out=htb[:, sl * DM: sl * DM + DM],
                        in_=xt[:, t * DM: t * DM + DM], func=AF.Identity,
                        bias=negmr[:, t: t + 1], scale=rstd[:, t: t + 1]),
                        inc="s")
                    ph.wait("tensor", ev_ap)
                    if ev_slot[sl] is not None:
                        ph.wait("tensor", ev_slot[sl])
                    for c in range(NDCH):
                        ph.op("tensor", lambda h, sl=sl, c=c: h.transpose(
                            p_h[:, sl * DM + c * 128: sl * DM + c * 128 + 128],
                            htb[:, sl * DM + c * 128: sl * DM + c * 128 + 128],
                            identb[:, :]),
                            inc="t" if c == NDCH - 1 else None)
                    ev_tr = tev()
                    ph.wait("vector", ev_tr)
                    ev_slot[sl] = ph.op("vector", lambda h, t=t, sl=sl: h.tensor_copy(
                        out=_ap(hf, t * 128,
                                [(NDCH * S, 128), (S, NDCH), (1, 128)]),
                        in_=p_h[:, sl * DM: sl * DM + DM]), inc="v")
                ph.emit()

        wb_ev = {}

        def unit_gu_src(u):
            if u < NE:
                return [(wg_r, u * DM * MIDR,
                         [(MIDR, 128), (128 * MIDR, NDCH), (1, 256)]),
                        (wu_r, u * DM * MIDR,
                         [(MIDR, 128), (128 * MIDR, NDCH), (1, 256)])]
            n, mh = (u - NE) // 2, (u - NE) % 2
            return [(wg_s, n * DM * MIDS + mh * 256,
                     [(MIDS, 128), (128 * MIDS, NDCH), (1, 256)]),
                    (wu_s, n * DM * MIDS + mh * 256,
                     [(MIDS, 128), (128 * MIDS, NDCH), (1, 256)])]

        def unit_d_src(u):
            if u < NE:
                return (wd_r, u * MIDR * DM,
                        [(DM, 128), (128 * DM, 2), (1, 1024)])
            n, mh = (u - NE) // 2, (u - NE) % 2
            return (wd_s, (n * MIDS + mh * 256) * DM,
                    [(DM, 128), (128 * DM, 2), (1, 1024)])

        def ffn_phase(tag, first_call):
            with ExitStack() as ctx:
                ph = Ph(nc, sems, cnt)
                blanket(ph)
                P = [PS(ctx, f"f{tag}_p{i}", [128, 1024], F32) for i in range(4)]
                p_all = SB(ctx, f"f{tag}_pall", [128, NU * 1024], BF16)
                gsil = SB(ctx, f"f{tag}_gs", [128, 2 * 1024], BF16)
                GSEM = ("dga", "dgb", "dgc")
                WSEM = ("dwa", "dwb", "dwc")
                DSEM = ("dda", "ddb", "ddc")
                BSEM = ("dba", "dbb", "dbc")
                WBSEM = ("dwba", "dwbb", "dwbc")
                DBSEM = ("ddba", "ddbb", "ddbc")
                WQUEUE = ("sync", "sync", "sync")
                t1 = SB(ctx, f"f{tag}_t1", [128, 1024], F32)
                bc_sb = SB(ctx, f"f{tag}_bc", [128, 3 * TB], F32)
                stage = SB(ctx, f"f{tag}_stg", [128, 2048], F32)
                lg = SB(ctx, f"f{tag}_lg", [128, NT * 16], F32)
                m8 = SB(ctx, f"f{tag}_m8", [128, NT * 8], F32)
                d21 = SB(ctx, f"f{tag}_d21", [128, NT], F32)
                e2 = SB(ctx, f"f{tag}_e2", [128, NT], F32)
                w1 = SB(ctx, f"f{tag}_w1", [128, NT], F32)
                w2 = SB(ctx, f"f{tag}_w2", [128, NT], F32)
                eqt = SB(ctx, f"f{tag}_eq", [128, 16], F32)
                comb = SB(ctx, f"f{tag}_cmb", [128, NT * 16], F32)
                comb_fm = SB(ctx, f"f{tag}_cfm", [16, S], F32)
                wgu = [SB(ctx, f"f{tag}_wgu{i}", [128, 2 * 2048], BF16)
                       for i in range(3)]
                wdb = [SB(ctx, f"f{tag}_wdb{i}", [128, 2048], BF16)
                       for i in range(3)]

                gu_dma, wd_dma, bc_dma = {}, {}, {}
                gu_free = [None, None, None]
                wd_free = [None, None, None]
                bc_free = [None, None, None]
                pre_issued = first_call

                ev_cp = [None] * 4
                for t in range(NT):
                    bk = t % 4
                    pb = P[bk // 2]
                    sl = (bk % 2) * 512 + (t // 4) * 16
                    ph.wait("tensor", ev_cp[bk])
                    for c in range(NDCH):
                        ph.op("tensor", lambda h, t=t, c=c, sl=sl, pb=pb: h.matmul(
                            pb[:, sl: sl + 16],
                            hf[:, c * S + t * 128: c * S + t * 128 + 128],
                            rw_im[:, c * 16: c * 16 + 16],
                            start=(c == 0), stop=(c == NDCH - 1)),
                            inc="t" if c == NDCH - 1 else None)
                    ev_mm = tev()
                    ph.wait("vector", ev_mm)
                    ev_cp[bk] = ph.op(
                        "vector", lambda h, t=t, sl=sl, pb=pb: h.tensor_copy(
                            out=lg[:, t * 16: t * 16 + 16], in_=pb[:, sl: sl + 16]),
                        inc="v")
                ph.drain("vector")
                for t in range(NT):
                    ph.op("vector", lambda h, t=t: h.max(
                        out=m8[:, t * 8: t * 8 + 8],
                        in_=lg[:, t * 16: t * 16 + 16]))
                ph.drain("vector")
                ev_d21 = ph.op("vector", lambda h: h.tensor_sub(
                    d21[:, :], _ap(m8, 1, [(NT * 8, 128), (8, NT)]),
                    _ap(m8, 0, [(NT * 8, 128), (8, NT)])), inc="v")
                ph.drain("vector")
                ph.wait("scalar", ev_d21)
                ev_e2 = ph.op("scalar", lambda h: h.activation(
                    out=e2[:, :], in_=d21[:, :], func=AF.Exp), inc="s")
                ph.wait("vector", ev_e2)
                ph.op("vector", lambda h: h.tensor_scalar(
                    out=w1[:, :], in0=e2[:, :], scalar1=1.0, scalar2=None,
                    op0=ALU.add))
                ph.drain("vector")
                ph.op("vector", lambda h: h.reciprocal(out=w2[:, :], in_=w1[:, :]))
                ph.drain("vector")
                ph.op("vector", lambda h: h.tensor_scalar(
                    out=w1[:, :], in0=w2[:, :], scalar1=-1.0, scalar2=1.0,
                    op0=ALU.mult, op1=ALU.add))
                ph.drain("vector")
                ev_ct = [None, None]
                ev_cf = None
                for t in range(NT):
                    ph.op("vector", lambda h, t=t: h.tensor_scalar(
                        out=eqt[:, :], in0=lg[:, t * 16: t * 16 + 16],
                        scalar1=m8[:, t * 8: t * 8 + 1], scalar2=None,
                        op0=ALU.is_equal))
                    ph.drain("vector")
                    ph.op("vector", lambda h, t=t: h.tensor_scalar(
                        out=comb[:, t * 16: t * 16 + 16], in0=eqt[:, :],
                        scalar1=w2[:, t: t + 1], scalar2=None, op0=ALU.mult))
                    ph.drain("vector")
                    ph.op("vector", lambda h, t=t: h.tensor_scalar(
                        out=eqt[:, :], in0=lg[:, t * 16: t * 16 + 16],
                        scalar1=m8[:, t * 8 + 1: t * 8 + 2], scalar2=None,
                        op0=ALU.is_equal))
                    ph.drain("vector")
                    ev_c = ph.op("vector", lambda h, t=t: h.scalar_tensor_tensor(
                        out=comb[:, t * 16: t * 16 + 16], in0=eqt[:, :],
                        scalar=w1[:, t: t + 1],
                        in1=comb[:, t * 16: t * 16 + 16],
                        op0=ALU.mult, op1=ALU.add), inc="v")
                    ph.drain("vector")
                    ph.wait("tensor", ev_c)
                    sl = 1 + (t % 2)
                    slot = (t // 2) % 8
                    if ev_ct[t % 2] is not None:
                        ph.wait("tensor", ev_ct[t % 2])
                    ev_t = ph.op("tensor", lambda h, t=t, sl=sl, slot=slot: h.transpose(
                        P[sl][0:16, slot * 128: slot * 128 + 128],
                        comb[:, t * 16: t * 16 + 16], idf32[:, :]), inc="t")
                    ph.wait("vector", ev_t)
                    ev_cf = ph.op("vector", lambda h, t=t, sl=sl, slot=slot: h.tensor_copy(
                        out=comb_fm[0:16, t * 128: t * 128 + 128],
                        in_=P[sl][0:16, slot * 128: slot * 128 + 128]), inc="v")
                    ev_ct[t % 2] = ev_cf
                ph.wait("sync", ev_cf)
                ev_comb_d = ph.dma("sync", comb_d[:, :], comb_fm[0:16, :], "dx")

                def issue_gu(tb, u, wonly=False, skipw=False):
                    buf = u % 3
                    sg = GSEM[buf]
                    sw = WSEM[buf]
                    fresh = first_call and tb == 0
                    if skipw:
                        pass
                    elif fresh:
                        if gu_free[buf] is not None:
                            ph.wait("gpsimd", gu_free[buf])
                        evs = []
                        for i, (ten, off, dims) in enumerate(unit_gu_src(u)):
                            evs.append(ph.dma(
                                "gpsimd",
                                _ap(wgu[buf], i * 2048,
                                    [(2 * 2048, 128), (256, NDCH), (1, 256)]),
                                _ap(ten, off, dims), sg))
                        gu_dma[(tb, u)] = evs
                    else:
                        qw = WQUEUE[buf]
                        if gu_free[buf] is not None:
                            ph.wait(qw, gu_free[buf])
                        ph.wait(qw, wb_ev[(u, 0)])
                        gu_dma[(tb, u)] = [ph.dma(
                            qw, wgu[buf][:, :],
                            _ap(wbf_d, (u * 3) * 128 * 2048,
                                [(2048, 128), (128 * 2048, 2), (1, 2048)]),
                            sw)]
                    if wonly:
                        return
                    if u < NE:
                        if bc_free[buf] is not None:
                            ph.wait("sync", bc_free[buf])
                        ph.wait("sync", ev_comb_d)
                        bc_dma[(tb, u)] = ph.dma(
                            "sync", bc_sb[:, buf * TB: buf * TB + TB],
                            _ap(comb_d, u * S + tb * TB, [(0, 128), (1, TB)]),
                            BSEM[buf])

                def issue_wd(tb, u):
                    buf = u % 3
                    sg = GSEM[buf]
                    sd = DSEM[buf]
                    fresh = first_call and tb == 0
                    if fresh:
                        if wd_free[buf] is not None:
                            ph.wait("gpsimd", wd_free[buf])
                        if gu_free[buf] is not None:
                            ph.wait("gpsimd", gu_free[buf])
                        ten, off, dims = unit_d_src(u)
                        wd_dma[(tb, u)] = [ph.dma(
                            "gpsimd", wdb[buf][:, :], _ap(ten, off, dims), sg)]
                    else:
                        qw = WQUEUE[buf]
                        if wd_free[buf] is not None:
                            ph.wait(qw, wd_free[buf])
                        ph.wait(qw, wb_ev[(u, 2)])
                        wd_dma[(tb, u)] = [ph.dma(
                            qw, wdb[buf][:, :],
                            _ap(wbf_d, (u * 3 + 2) * 128 * 2048,
                                [(2048, 128), (1, 2048)]), sd)]

                gsil_free = [None, None]
                ev_adds = vev()
                if first_call:
                    for u0 in range(3):
                        issue_gu(0, u0, wonly=True)

                for tb in range(NTB):
                    fresh = first_call and tb == 0
                    issue_gu(tb, 0, skipw=pre_issued)
                    issue_gu(tb, 1, skipw=pre_issued)
                    issue_gu(tb, 2, skipw=pre_issued)
                    pre_issued = False
                    last_pmult = None
                    last_sil = None
                    for u in range(NU):
                        buf = u % 2
                        wbuf = u % 3
                        pg, pu = P[2 * buf], P[2 * buf + 1]
                        ph.waits("tensor", gu_dma[(tb, u)])
                        if gsil_free[buf] is not None:
                            ph.wait("tensor", gsil_free[buf])
                        ph.wait("tensor", ev_adds)
                        for mc in range(2):
                            for c in range(NDCH):
                                ph.op("tensor", lambda h, buf=wbuf, mc=mc, c=c,
                                      pg=pg, tb=tb: h.matmul(
                                    pg[:, mc * 512: mc * 512 + 512],
                                    wgu[buf][:, c * 256 + mc * 128:
                                             c * 256 + mc * 128 + 128],
                                    hf[:, c * S + tb * TB: c * S + tb * TB + TB],
                                    start=(c == 0), stop=(c == NDCH - 1)),
                                    inc="t" if (mc == 1 and c == NDCH - 1) else None)
                        ev_g = tev()
                        for mc in range(2):
                            for c in range(NDCH):
                                ph.op("tensor", lambda h, buf=wbuf, mc=mc, c=c,
                                      pu=pu, tb=tb: h.matmul(
                                    pu[:, mc * 512: mc * 512 + 512],
                                    wgu[buf][:, 2048 + c * 256 + mc * 128:
                                             2048 + c * 256 + mc * 128 + 128],
                                    hf[:, c * S + tb * TB: c * S + tb * TB + TB],
                                    start=(c == 0), stop=(c == NDCH - 1)),
                                    inc="t" if (mc == 1 and c == NDCH - 1) else None)
                        ev_u = tev()
                        gu_free[wbuf] = ev_u
                        if fresh:
                            ph.wait("sync", ev_u)
                            wb_ev[(u, 0)] = ph.dma(
                                "sync",
                                _ap(wbf_d, (u * 3) * 128 * 2048,
                                    [(2048, 128), (128 * 2048, 2), (1, 2048)]),
                                wgu[wbuf][:, :], WBSEM[wbuf])
                            gu_free[wbuf] = wb_ev[(u, 0)]
                        ph.wait("scalar", ev_g)
                        if gsil_free[buf] is not None:
                            ph.wait("scalar", gsil_free[buf])
                        ev_sil = ph.op("scalar", lambda h, pg=pg, buf=buf: h.activation(
                            out=gsil[:, buf * 1024: buf * 1024 + 1024],
                            in_=pg[:, :], func=AF.Silu), inc="s")
                        last_sil = ev_sil
                        ph.wait("vector", ev_u)
                        ph.wait("vector", ev_sil)
                        if u < NE:
                            ph.wait("vector", bc_dma[(tb, u)])
                            ev_t1 = ph.op("vector", lambda h, pu=pu, buf=wbuf: h.tensor_tensor(
                                out=t1[:, :].rearrange("p (a b) -> p a b", a=2),
                                in0=pu[:, :].rearrange("p (a b) -> p a b", a=2),
                                in1=bc_sb[:, buf * TB: buf * TB + TB]
                                    .unsqueeze(1).to_broadcast([128, 2, TB]),
                                op=ALU.mult), inc="v")
                            bc_free[wbuf] = ev_t1
                            ph.drain("vector")
                            ev_p = ph.op("vector", lambda h, u=u, buf=buf: h.tensor_tensor(
                                out=p_all[:, u * 1024: u * 1024 + 1024],
                                in0=t1[:, :],
                                in1=gsil[:, buf * 1024: buf * 1024 + 1024],
                                op=ALU.mult), inc="v")
                        else:
                            ev_p = ph.op("vector", lambda h, u=u, pu=pu, buf=buf: h.tensor_tensor(
                                out=p_all[:, u * 1024: u * 1024 + 1024],
                                in0=pu[:, :],
                                in1=gsil[:, buf * 1024: buf * 1024 + 1024],
                                op=ALU.mult), inc="v")
                        ph.drain("vector")
                        gsil_free[buf] = ev_p
                        last_pmult = ev_p
                        if u + 3 < NU:
                            issue_gu(tb, u + 3)

                    issue_wd(tb, 0)
                    issue_wd(tb, 1)
                    issue_wd(tb, 2)
                    ph.wait("tensor", last_pmult)
                    ph.wait("tensor", last_sil)
                    for u in range(NU):
                        wbuf = u % 3
                        ph.waits("tensor", wd_dma[(tb, u)])
                        for dch in range(NDCH):
                            po = P[dch // 2]
                            col = (dch % 2) * 512
                            for mc in range(2):
                                ph.op("tensor", lambda h, buf=wbuf, dch=dch,
                                      mc=mc, po=po, col=col, u=u: h.matmul(
                                    po[:, col: col + 512],
                                    wdb[buf][:, mc * 1024 + dch * 128:
                                             mc * 1024 + dch * 128 + 128],
                                    p_all[:, u * 1024 + mc * 512:
                                          u * 1024 + mc * 512 + 512],
                                    start=(u == 0 and mc == 0),
                                    stop=(u == NU - 1 and mc == 1)),
                                    inc="t" if (dch == NDCH - 1 and mc == 1) else None)
                        ev_wdused = tev()
                        wd_free[wbuf] = ev_wdused
                        if fresh:
                            ph.wait("sync", ev_wdused)
                            wb_ev[(u, 2)] = ph.dma(
                                "sync",
                                _ap(wbf_d, (u * 3 + 2) * 128 * 2048,
                                    [(2048, 128), (1, 2048)]),
                                wdb[wbuf][:, :], DBSEM[wbuf])
                            wd_free[wbuf] = wb_ev[(u, 2)]
                        if u + 3 < NU:
                            issue_wd(tb, u + 3)
                    ev_down = tev()

                    ev_adds_l = None
                    ev_tr_prev = None
                    for dh in range(2):
                        ph.wait("scalar", ev_down)
                        if ev_tr_prev is not None:
                            ph.wait("scalar", ev_tr_prev)
                        ev_st = []
                        for j in range(2):
                            ev_st.append(ph.op(
                                "scalar", lambda h, dh=dh, j=j: h.activation(
                                    out=stage[:, j * 1024: j * 1024 + 1024],
                                    in_=P[2 * dh + j][:, :], func=AF.Identity,
                                    scale=0.5), inc="s"))
                        for j in range(2):
                            ph.wait("tensor", ev_st[j])
                            for sub in range(4):
                                for dl in range(2):
                                    ph.op("tensor",
                                          lambda h, j=j, sub=sub, dl=dl, dh=dh: h.transpose(
                                        P[2 * dh + j][:, (sub * 2 + dl) * 128:
                                                      (sub * 2 + dl) * 128 + 128],
                                        stage[:, j * 1024 + dl * 512 + sub * 128:
                                              j * 1024 + dl * 512 + sub * 128 + 128],
                                        idf32[:, :]),
                                        inc="t" if (sub == 3 and dl == 1) else None)
                            ev_tr = tev()
                            ev_tr_prev = ev_tr
                            ph.wait("vector", ev_tr)
                            for sub in range(4):
                                tile = tb * 4 + sub
                                for dl in range(2):
                                    dch = (2 * dh + j) * 2 + dl
                                    ev_adds_l = ph.op(
                                        "vector", lambda h, tile=tile, dch=dch,
                                        j=j, sub=sub, dl=dl, dh=dh: h.tensor_tensor(
                                            out=xt[:, tile * DM + dch * 128:
                                                   tile * DM + dch * 128 + 128],
                                            in0=xt[:, tile * DM + dch * 128:
                                                   tile * DM + dch * 128 + 128],
                                            in1=P[2 * dh + j][:, (sub * 2 + dl) * 128:
                                                              (sub * 2 + dl) * 128 + 128],
                                            op=ALU.add), inc="v")
                            ph.drain("vector")
                    ev_adds = ev_adds_l
                ph.emit()

        def attn_phase():
            with ExitStack() as ctx:
                ph = Ph(nc, sems, cnt)
                blanket(ph)
                etab_ready = ("dx", b0dx)
                af = SB(ctx, "at_af", [128, NDCH * S], BF16)
                kT = SB(ctx, "at_kT", [128, S], BF16)
                vT = SB(ctx, "at_vT", [128, S], BF16)
                qT = SB(ctx, "at_qT", [128, S], BF16)
                kM = SB(ctx, "at_kM", [128, S], BF16)
                qM = SB(ctx, "at_qM", [128, S], BF16)
                vtm = SB(ctx, "at_vtm", [128, 16 * 65], BF16)
                pt = SB(ctx, "at_pt", [128, 2 * 2048], BF16)
                pexp = SB(ctx, "at_pexp", [128, 2 * 2048], BF16)
                ebuf = [SB(ctx, f"at_e{i}", [128, 2432], F16) for i in range(2)]
                win = [SB(ctx, f"at_w{i}", [128, 3 * 1024], BF16) for i in range(2)]
                wgp = win
                sigb = SB(ctx, "at_sig", [128, 2 * 512], BF16)
                gsg = SB(ctx, "at_gsg", [128, 2 * 512], F32)
                bca = SB(ctx, "at_bca", [128, 1024], F16)
                att_sb = SB(ctx, "at_attsb", [128, 2 * 512], F32)
                stag = SB(ctx, "at_stag", [64, 2 * 512], BF16)
                negmh = SB(ctx, "at_ngm", [128, 16], F32)
                mtmp = SB(ctx, "at_mt", [2, 16], F32)
                srec = SB(ctx, "at_srec", [128, 8], F32)
                rrec = SB(ctx, "at_rrec", [128, 8], F32)
                trow = SB(ctx, "at_tr", [1, 4], F32)
                srow = SB(ctx, "at_sr", [1, 4], F32)
                p_s = PS(ctx, "at_ps", [128, 2048], F32)
                p_pr = PS(ctx, "at_ppr", [128, 1024], F32)
                p_v = PS(ctx, "at_pv", [128, 1024], BF16)
                p_bc = PS(ctx, "at_pbc", [128, 8], F32)

                ev_ones = ph.op("vector", lambda h: h.memset(
                    _ap(vtm, 64, [(16 * 65, 128), (65, 16), (1, 1)]), 1.0),
                    inc="v")

                win_dma = {}
                win_free = [None, None]

                def issue_win(g):
                    buf = g % 2
                    if win_free[buf] is not None:
                        ph.wait("gpsimd", win_free[buf])
                    evs = []
                    for part in range(3):
                        evs.append(ph.dma(
                            "gpsimd",
                            _ap(win[buf], part * 1024,
                                [(3 * 1024, 128), (128, NDCH), (1, 128)]),
                            _ap(w_in, part * 1024 + g * 128,
                                [(3 * DM, 128), (128 * 3 * DM, NDCH), (1, 128)]),
                            "dga" if buf == 0 else "dgb"))
                    win_dma[g] = evs

                EQ = []
                for _g in range(NGRP):
                    for _hl in range(2):
                        for _qg in range(4):
                            EQ.append((_g * 2 + _hl, _qg))
                ESEM = ("dea", "deb")
                EQUEUE = ("sync", "scalar")
                e_dma_ev = [None] * len(EQ)
                e_free_ev = [None] * 2
                eq_ptr = [0]

                def issue_next_equad():
                    i = eq_ptr[0]
                    if i >= len(EQ):
                        return
                    eq_ptr[0] = i + 1
                    h_, qg = EQ[i]
                    buf = i % 2
                    qu = EQUEUE[buf]
                    if e_free_ev[buf] is not None:
                        ph.wait(qu, e_free_ev[buf])
                    ph.wait(qu, etab_ready)
                    e_dma_ev[i] = ph.dma(
                        qu, ebuf[buf][:, :],
                        _ap(etab_d, h_ * TAB + 512 * qg,
                            [(1, 128), (1, 2432)]), ESEM[buf])

                issue_win(0)
                issue_win(1)
                for _ in range(2):
                    issue_next_equad()
                eq_use = [0]
                ev_attn_done = []
                odd_dma_evs = []
                ev_vtmfree = None
                pending = [None]
                pend_b = [None]
                pbc_free = [None, None]

                def flush_a():
                    if pending[0] is None:
                        return
                    (par, fh, fqg, fhl, ev_evc) = pending[0]
                    pending[0] = None
                    ph.wait("tensor", ev_evc)
                    ph.wait("tensor", pbc_free[par])
                    for j in range(4):
                        ph.op("tensor", lambda h, par=par, j=j: h.transpose(
                            p_bc[:, par * 4 + j: par * 4 + j + 1],
                            bass.AP(att_sb, 64 * 1024 + par * 512 + j,
                                    [[1024, 1], [4, 128]]),
                            idf32[64:65, 64:65]),
                            inc="t" if j == 3 else None)
                    ev_tr4 = tev()
                    ph.wait("vector", ev_tr4)
                    ev_rin = ph.op("vector", lambda h, par=par: h.reciprocal(
                        out=rrec[:, par * 4: par * 4 + 4],
                        in_=p_bc[:, par * 4: par * 4 + 4]), inc="v")
                    pbc_free[par] = ev_rin
                    dsem = "dra" if par == 0 else "drb"
                    ph.wait("gpsimd", att_ev[par])
                    ph.wait("gpsimd", ev_rin)
                    ev_st2 = ph.dma(
                        "gpsimd", _ap(rbc_d, par * 512, [(4, 128), (1, 4)]),
                        rrec[:, par * 4: par * 4 + 4], dsem)
                    ph.wait("gpsimd", ev_st2)
                    ev_bc = ph.dma(
                        "gpsimd", bca[:, par * 512: par * 512 + 512],
                        _ap(rbc_d, par * 512, [(0, 128), (1, 512)]), dsem)
                    pend_b[0] = (par, fh, fqg, fhl, ev_evc, ev_bc)

                def flush_b():
                    if pend_b[0] is None:
                        return
                    (par, fh, fqg, fhl, ev_evc, ev_bc) = pend_b[0]
                    pend_b[0] = None
                    ph.wait("vector", ev_bc)
                    ph.wait("vector", ev_evc)
                    if fhl == 0:
                        ev_at = ph.op(
                            "vector", lambda h, fh=fh, fqg=fqg,
                            par=par: h.tensor_tensor(
                                out=af[0:64, (fh // 2) * S + fqg * 512:
                                       (fh // 2) * S + fqg * 512 + 512],
                                in0=att_sb[0:64, par * 512: par * 512 + 512],
                                in1=bca[0:64, par * 512: par * 512 + 512],
                                op=ALU.mult), inc="v")
                        ev_attn_done.append(ev_at)
                    else:
                        ssl = fqg % 2
                        if len(odd_dma_evs) >= 2:
                            ph.wait("vector", odd_dma_evs[-1])
                        ev_at = ph.op(
                            "vector", lambda h, ssl=ssl, par=par: h.tensor_tensor(
                                out=stag[0:64, ssl * 512: ssl * 512 + 512],
                                in0=att_sb[0:64, par * 512: par * 512 + 512],
                                in1=bca[0:64, par * 512: par * 512 + 512],
                                op=ALU.mult), inc="v")
                        ph.wait("gpsimd", ev_at)
                        odd_dma_evs.append(ph.dma(
                            "gpsimd",
                            af[64:128, (fh // 2) * S + fqg * 512:
                               (fh // 2) * S + fqg * 512 + 512],
                            stag[0:64, ssl * 512: ssl * 512 + 512], "dst"))
                    att_ev[par] = ev_at

                def flush_pending():
                    flush_a()
                    flush_b()
                expAB = [None, None]
                mult_ev = [[None, None], [None, None]]
                pvlast = [None, None]
                pat_free = [None, None]
                att_ev = [None, None]

                for g in range(NGRP):
                    wbuf = g % 2
                    flush_pending()
                    ph.waits("tensor", win_dma[g])
                    ph.wait("tensor", pat_free[0])
                    ph.wait("tensor", pat_free[1])
                    ev_evac = [None, None]
                    for part in range(3):
                        for tch in range(4):
                            sl = tch % 2
                            if ev_evac[sl] is not None:
                                ph.wait("tensor", ev_evac[sl])
                            for c in range(NDCH):
                                ph.op("tensor", lambda h, part=part, tch=tch,
                                      c=c, wbuf=wbuf, sl=sl: h.matmul(
                                    p_pr[:, sl * 512: sl * 512 + 512],
                                    win[wbuf][:, part * 1024 + c * 128:
                                              part * 1024 + c * 128 + 128],
                                    hf[:, c * S + tch * 512:
                                       c * S + tch * 512 + 512],
                                    start=(c == 0), stop=(c == NDCH - 1)),
                                    inc="t" if c == NDCH - 1 else None)
                            ev_mm = tev()
                            if part == 0:
                                ph.wait("scalar", ev_mm)
                                ev_evac[sl] = ph.op(
                                    "scalar", lambda h, tch=tch, sl=sl: h.copy(
                                        out=kT[:, tch * 512: tch * 512 + 512],
                                        in_=p_pr[:, sl * 512: sl * 512 + 512]),
                                    inc="s")
                            elif part == 1:
                                ph.wait("scalar", ev_mm)
                                ev_evac[sl] = ph.op(
                                    "scalar", lambda h, tch=tch, sl=sl: h.copy(
                                        out=vT[:, tch * 512: tch * 512 + 512],
                                        in_=p_pr[:, sl * 512: sl * 512 + 512]),
                                    inc="s")
                            else:
                                ph.wait("vector", ev_mm)
                                ev_evac[sl] = ph.op(
                                    "vector", lambda h, tch=tch, sl=sl: h.tensor_scalar(
                                        out=_ap(qT, 2047 - tch * 512,
                                                [(S, 128), (-1, 512)]),
                                        in0=p_pr[:, sl * 512: sl * 512 + 512],
                                        scalar1=0.125, scalar2=None,
                                        op0=ALU.mult), inc="v")
                    ev_proj_s = sev()
                    ev_proj_v = vev()
                    pat_free[0] = ev_evac[0]
                    pat_free[1] = ev_evac[1]
                    win_free[wbuf] = tev()
                    if g + 2 < NGRP:
                        issue_win(g + 2)
                    ph.wait("scalar", tev())
                    ph.wait("scalar", ev_proj_s)
                    ph.wait("scalar", ev_proj_v)
                    for (dst, srct) in ((kM, kT), (qM, qT)):
                        for bnd in range(2):
                            ph.dma("scalar", dst[bnd * 64:(bnd + 1) * 64, :],
                                   srct[(1 - bnd) * 64:(2 - bnd) * 64, :],
                                   "dem")
                    ev_mirror = ("dem", cnt["dem"])

                    ph.wait("scalar", ev_proj_v)
                    ph.wait("scalar", pvlast[0])
                    ph.wait("scalar", pvlast[1])
                    sqb = pt
                    ev_sq = [None, None]
                    ev_sq[0] = ph.op("scalar", lambda h: h.activation(
                        out=sqb[:, 0:S], in_=qT[:, :], func=AF.Square), inc="s")
                    ev_sq[1] = ph.op("scalar", lambda h: h.activation(
                        out=sqb[:, S: 2 * S], in_=kT[:, :], func=AF.Square),
                        inc="s")
                    red_ev = [expAB[0], expAB[0], expAB[1], expAB[1]]
                    for src in range(2):
                        ph.wait("tensor", ev_sq[src])
                        for i in range(4):
                            ph.wait("tensor", red_ev[i])
                            ev_nm = ph.op("tensor", lambda h, i=i, src=src:
                                          h.matmul(
                                p_s[0:2, i * 512: i * 512 + 512],
                                ind2[:, :],
                                sqb[:, src * S + i * 512:
                                    src * S + i * 512 + 512],
                                start=True, stop=True), inc="t")
                            ph.wait("vector", ev_nm)
                            red_ev[i] = ph.op(
                                "vector", lambda h, i=i, src=src: h.reduce_max(
                                    out=mtmp[0:2, src * 4 + i: src * 4 + i + 1],
                                    in_=p_s[0:2, i * 512: i * 512 + 512],
                                    axis=AXX), inc="v")
                    expAB[0] = red_ev[1]
                    expAB[1] = red_ev[3]
                    ph.drain("vector")
                    ph.op("vector", lambda h: h.reduce_max(
                        out=mtmp[0:2, 8:9], in_=mtmp[0:2, 0:4], axis=AXX))
                    ph.drain("vector")
                    ph.op("vector", lambda h: h.reduce_max(
                        out=mtmp[0:2, 9:10], in_=mtmp[0:2, 4:8], axis=AXX))
                    ph.drain("vector")
                    ev_t2 = ph.op("vector", lambda h: h.tensor_mul(
                        mtmp[0:2, 10:11], mtmp[0:2, 8:9], mtmp[0:2, 9:10]),
                        inc="v")
                    ph.wait("tensor", ev_t2)
                    ph.wait("tensor", expAB[0])
                    ev_trp = ph.op("tensor", lambda h: h.transpose(
                        p_s[0:1, 0:2], mtmp[0:2, 10:11], idf32[0:2, 0:2]),
                        inc="t")
                    ph.wait("vector", ev_trp)
                    ev_trc = ph.op("vector", lambda h: h.tensor_copy(
                        out=trow[0:1, 0:2], in_=p_s[0:1, 0:2]), inc="v")
                    ph.wait("scalar", ev_trc)
                    ev_sqr = ph.op("scalar", lambda h: h.activation(
                        out=srow[0:1, 0:2], in_=trow[0:1, 0:2], func=AF.Sqrt),
                        inc="s")
                    ph.wait("vector", ev_sqr)
                    ev_trow = ph.op("vector", lambda h: h.tensor_scalar(
                        out=trow[0:1, 2:4], in0=srow[0:1, 0:2],
                        scalar1=-1.0, scalar2=-0.5,
                        op0=ALU.mult, op1=ALU.add), inc="v")
                    ph.wait("tensor", ev_trow)
                    ph.wait("tensor", ev_trc)
                    ev_bcm = ph.op("tensor", lambda h: h.matmul(
                        p_s[:, 0:2], onesf[0:1, :], trow[0:1, 2:4],
                        start=True, stop=True), inc="t")
                    ph.wait("scalar", ev_bcm)
                    ev_negmh = ph.op("scalar", lambda h, g=g: h.copy(
                        out=negmh[:, g * 2: g * 2 + 2], in_=p_s[:, 0:2]),
                        inc="s")
                    ph.wait("tensor", ev_negmh)
                    expAB[0] = None

                    for hl in range(2):
                        h_ = g * 2 + hl
                        base = hl * 64
                        if ev_vtmfree is not None:
                            ph.wait("tensor", ev_vtmfree)
                        ph.wait("tensor", ev_proj_s)
                        ph.wait("tensor", ev_proj_v)
                        for c in range(16):
                            ph.op("tensor", lambda h, c=c, base=base: h.transpose(
                                p_v[:, c * 64: c * 64 + 64],
                                vT[base: base + 64, c * 128: c * 128 + 128],
                                identb[base: base + 64, base: base + 64]),
                                inc="t" if c == 15 else None)
                        ev_vtr = tev()
                        ph.wait("vector", ev_vtr)
                        ph.wait("vector", ev_ones)
                        ev_vtm = ph.op("vector", lambda h: h.tensor_copy(
                            out=_ap(vtm, 0, [(16 * 65, 128), (65, 16), (1, 64)]),
                            in_=_ap(p_v, 0, [(1024, 128), (64, 16), (1, 64)])),
                            inc="v")
                        ev_vtmfree = ev_vtm

                        for qg in range(4):
                            hgi = h_ * 4 + qg
                            par = hgi % 2
                            ev_s = [[None, None] for _ in range(4)]
                            ev_exp = [[None, None] for _ in range(4)]
                            ev_pv = [None] * 4

                            def emit_pv(q, qg=qg, par=par, ev_pv=ev_pv,
                                        ev_vtm=ev_vtm):
                                for j in range(4):
                                    c = q * 4 + j
                                    ph.wait("tensor", mult_ev[q % 2][j // 2])
                                    if c == 0:
                                        ph.wait("tensor", pat_free[par])
                                        ph.wait("tensor", ev_vtm)
                                    ph.op("tensor", lambda h, c=c, q=q, j=j,
                                          par=par: h.matmul(
                                        p_pr[0:65, par * 512: par * 512 + 512],
                                        vtm[:, c * 65: c * 65 + 65],
                                        pt[:, (q % 2) * 2048 + j * 512:
                                           (q % 2) * 2048 + j * 512 + 512],
                                        start=(c == 0), stop=(c == 15)),
                                        inc="t" if j == 3 else None)
                                ev_pv[q] = tev()
                                pvlast[q % 2] = ev_pv[q]

                            mbase = 64 - base
                            for q in range(4):
                                ph.wait("tensor", expAB[0])
                                ph.wait("tensor", ev_mirror)
                                for j in range(2):
                                    c = q * 4 + j
                                    kk = kT if j % 2 == 0 else kM
                                    qq = qT if j % 2 == 0 else qM
                                    bb = base if j % 2 == 0 else mbase
                                    ph.op("tensor", lambda h, c=c, bb=bb,
                                          kk=kk, qq=qq, qg=qg: h.matmul(
                                        p_s[:, (c % 4) * 512: (c % 4) * 512 + 512],
                                        kk[bb: bb + 64, c * 128: c * 128 + 128],
                                        qq[bb: bb + 64, qg * 512: qg * 512 + 512],
                                        start=True, stop=True),
                                        inc="t" if j == 1 else None)
                                ev_sA = tev()
                                ph.wait("tensor", expAB[1])
                                for j in range(2, 4):
                                    c = q * 4 + j
                                    kk = kT if j % 2 == 0 else kM
                                    qq = qT if j % 2 == 0 else qM
                                    bb = base if j % 2 == 0 else mbase
                                    ph.op("tensor", lambda h, c=c, bb=bb,
                                          kk=kk, qq=qq, qg=qg: h.matmul(
                                        p_s[:, (c % 4) * 512: (c % 4) * 512 + 512],
                                        kk[bb: bb + 64, c * 128: c * 128 + 128],
                                        qq[bb: bb + 64, qg * 512: qg * 512 + 512],
                                        start=True, stop=True),
                                        inc="t" if j == 3 else None)
                                ev_sB = tev()
                                if q == 1:
                                    flush_a()
                                if q >= 1:
                                    emit_pv(q - 1)
                                for half_i, ev_sx in ((0, ev_sA), (1, ev_sB)):
                                    ph.wait("scalar", ev_sx)
                                    ph.wait("scalar", ev_negmh)
                                    ph.wait("scalar", mult_ev[q % 2][half_i])
                                    ev_e_ = ph.op(
                                        "scalar", lambda h, q=q, half_i=half_i,
                                        hh=h_: h.activation(
                                            out=pexp[:, (q % 2) * 2048 +
                                                     half_i * 1024:
                                                     (q % 2) * 2048 +
                                                     half_i * 1024 + 1024],
                                            in_=p_s[:, half_i * 1024:
                                                    half_i * 1024 + 1024],
                                            func=AF.Exp,
                                            bias=negmh[:, hh: hh + 1]),
                                        inc="s")
                                    expAB[half_i] = ev_e_
                                    ev_exp[q][half_i] = ev_e_
                                ei = eq_use[0]
                                ebuf_i = ebuf[ei % 2]
                                for j in range(4):
                                    ph.wait("vector", ev_exp[q][j // 2])
                                    ph.wait("vector", e_dma_ev[ei])
                                    ph.wait("vector", pvlast[q % 2])
                                    mev = ph.op(
                                        "vector", lambda h, q=q, j=j,
                                        ebuf_i=ebuf_i: h.tensor_tensor(
                                            out=pt[:, (q % 2) * 2048 + j * 512:
                                                   (q % 2) * 2048 + j * 512
                                                   + 512],
                                            in0=pexp[:, (q % 2) * 2048 +
                                                     j * 512:
                                                     (q % 2) * 2048 +
                                                     j * 512 + 512],
                                            in1=ebuf_i[:, q * 512 + j * 128:
                                                       q * 512 + j * 128 + 512],
                                            op=ALU.mult), inc="v")
                                    if j % 2 == 1:
                                        mult_ev[q % 2][j // 2] = mev
                                if q == 3:
                                    eq_use[0] = ei + 1
                                    e_free_ev[ei % 2] = mult_ev[q % 2][1]
                                    issue_next_equad()
                            emit_pv(3)
                            flush_b()
                            ph.wait("scalar", ev_pv[3])
                            ph.wait("scalar", att_ev[par])
                            ev_evc = ph.op("scalar", lambda h, par=par: h.copy(
                                out=att_sb[0:65, par * 512: par * 512 + 512],
                                in_=p_pr[0:65, par * 512: par * 512 + 512]),
                                inc="s")
                            pat_free[par] = ev_evc
                            pending[0] = (par, h_, qg, hl, ev_evc)

                flush_pending()
                wg_dma = {}
                wg_free = [None, None]

                def issue_wg(ja):
                    buf = ja % 2
                    if wg_free[buf] is not None:
                        ph.wait("gpsimd", wg_free[buf])
                    if win_free[buf] is not None:
                        ph.wait("gpsimd", win_free[buf])
                    evs = []
                    for i, j in enumerate((ja, ja + 8)):
                        evs.append(ph.dma(
                            "gpsimd",
                            _ap(wgp[buf], i * 1024,
                                [(3 * 1024, 128), (128, NDCH), (1, 128)]),
                            _ap(w_gate, j * 128,
                                [(2 * DM, 128), (128 * 2 * DM, NDCH), (1, 128)]),
                            "dga" if buf == 0 else "dgb"))
                    wg_dma[ja] = evs

                issue_wg(0)
                issue_wg(1)
                ph.waits("tensor", ev_attn_done)
                ph.waits("tensor", odd_dma_evs)
                ph.wait("tensor", expAB[0])
                ph.wait("tensor", expAB[1])
                ph.wait("tensor", pat_free[0])
                ph.wait("tensor", pat_free[1])
                ev_gsl = [None, None]
                ev_ppr_free = None
                for ja in range(8):
                    buf = ja % 2
                    ph.waits("tensor", wg_dma[ja])
                    for tch in range(4):
                        if ev_gsl[0] is not None:
                            ph.wait("tensor", ev_gsl[0])
                        if ev_gsl[1] is not None:
                            ph.wait("tensor", ev_gsl[1])
                        for i in range(2):
                            for c in range(NDCH):
                                ph.op("tensor", lambda h, ja=ja, tch=tch, i=i,
                                      c=c, buf=buf: h.matmul(
                                    p_s[:, i * 512: i * 512 + 512],
                                    wgp[buf][:, i * 1024 + c * 128:
                                             i * 1024 + c * 128 + 128],
                                    af[:, c * S + tch * 512:
                                       c * S + tch * 512 + 512],
                                    start=(c == 0), stop=(c == NDCH - 1)),
                                    inc="t" if c == NDCH - 1 else None)
                        ev_ag = tev()
                        ssl = tch % 2
                        ph.wait("scalar", ev_ag)
                        if ev_gsl[ssl] is not None:
                            ph.wait("scalar", ev_gsl[ssl])
                        ev_sig = ph.op("scalar", lambda h, ja=ja, ssl=ssl: h.activation(
                            out=sigb[:, ssl * 512: ssl * 512 + 512],
                            in_=p_s[:, 512:1024], func=AF.Sigmoid,
                            bias=bg_sb[:, 8 + ja: 9 + ja]), inc="s")
                        ph.wait("vector", ev_sig)
                        ev_gtd = ph.op("vector", lambda h, ja=ja, ssl=ssl: h.scalar_tensor_tensor(
                            out=_ap(gsg, ssl * 512 + 511,
                                    [(2 * 512, 128), (-1, 512)]),
                            in0=p_s[:, 0:512], scalar=bg_sb[:, ja: ja + 1],
                            in1=sigb[:, ssl * 512: ssl * 512 + 512],
                            op0=ALU.add, op1=ALU.mult), inc="v")
                        ph.drain("vector")
                        ev_gsl[ssl] = ev_gtd
                        ph.wait("tensor", ev_gtd)
                        if ev_ppr_free is not None:
                            ph.wait("tensor", ev_ppr_free)
                        for sub in range(4):
                            ph.op("tensor", lambda h, sub=sub, ssl=ssl: h.transpose(
                                p_pr[:, (ssl * 4 + sub) * 128:
                                     (ssl * 4 + sub) * 128 + 128],
                                gsg[:, ssl * 512 + sub * 128:
                                    ssl * 512 + sub * 128 + 128],
                                idf32[:, :]),
                                inc="t" if sub == 3 else None)
                        ev_tr = tev()
                        ph.wait("vector", ev_tr)
                        ev_add = None
                        for sub in range(4):
                            tile = (3 - tch) * 4 + sub
                            ev_add = ph.op("vector", lambda h, tile=tile, ja=ja,
                                           sub=sub, ssl=ssl: h.tensor_tensor(
                                out=xt[:, tile * DM + ja * 128:
                                       tile * DM + ja * 128 + 128],
                                in0=xt[:, tile * DM + ja * 128:
                                       tile * DM + ja * 128 + 128],
                                in1=p_pr[:, (ssl * 4 + sub) * 128:
                                         (ssl * 4 + sub) * 128 + 128],
                                op=ALU.add), inc="v")
                        ph.drain("vector")
                        ev_ppr_free = ev_add
                    wg_free[buf] = tev()
                    if ja + 2 < 8:
                        issue_wg(ja + 2)
                ph.emit()

        ln_phase(0)
        ffn_phase(0, first_call=True)
        if stop != "ffn1":
            ln_phase(1)
            attn_phase()
            if stop != "attn":
                ln_phase(2)
                ffn_phase(1, first_call=False)

        with ExitStack() as ctx:
            ph = Ph(nc, sems, cnt)
            ev = ph.dma("sync",
                        _ap(out_d, 0, [(DM, 128), (128 * DM, NT), (1, DM)]),
                        _ap(xt, 0, [(NT * DM, 128), (DM, NT), (1, DM)]), "dx")
            ph.wait("sync", ev)
            ph.op("sync", lambda h: h.nop())
            ph.emit()

    return nc


_CACHE = {}


def _consts():
    if "c" not in _CACHE:
        rel = (np.arange(TAB, dtype=np.float32) - (S - 1))
        ind2 = np.zeros((128, 2), dtype=ml_dtypes.bfloat16)
        ind2[0:64, 0] = 1
        ind2[64:128, 1] = 1
        reduc = np.zeros((128, H), dtype=np.float32)
        for p in range(128):
            reduc[p, p // 8] = 1.0
        _CACHE["c"] = {
            "identb": np.eye(128, dtype=ml_dtypes.bfloat16),
            "idf32": np.eye(128, dtype=np.float32),
            "onesf": np.ones((128, 128), dtype=np.float32),
            "ind2": ind2,
            "reduc": reduc,
            "relbc": np.broadcast_to(rel, (128, TAB)).copy(),
        }
    return _CACHE["c"]


def kernel(L0_attn_x, L0_attn_w_in, L0_attn_w_gate, L0_attn_b_gate,
           L0_attn_amplitudes, L0_attn_sharpness, L0_attn_offsets,
           L0_ffn_router_w, L0_ffn_wg_r, L0_ffn_wu_r, L0_ffn_wd_r,
           L0_ffn_wg_s, L0_ffn_wu_s, L0_ffn_wd_s):
    if "nc" not in _CACHE:
        _CACHE["nc"] = build_nc()
    nc = _CACHE["nc"]
    c = _consts()
    x = np.ascontiguousarray(np.asarray(L0_attn_x, dtype=np.float32))
    shared = {
        "w_in": np.ascontiguousarray(np.asarray(L0_attn_w_in, np.float32)),
        "w_gate": np.ascontiguousarray(np.asarray(L0_attn_w_gate, np.float32)),
        "b_gate": np.ascontiguousarray(np.asarray(L0_attn_b_gate, np.float32)),
        "amp": np.ascontiguousarray(np.asarray(L0_attn_amplitudes, np.float32)),
        "sharp": np.ascontiguousarray(np.asarray(L0_attn_sharpness, np.float32)),
        "off": np.ascontiguousarray(np.asarray(L0_attn_offsets, np.float32)),
        "router_w": np.ascontiguousarray(np.asarray(L0_ffn_router_w, np.float32)),
        "wg_r": np.ascontiguousarray(np.asarray(L0_ffn_wg_r, np.float32)),
        "wu_r": np.ascontiguousarray(np.asarray(L0_ffn_wu_r, np.float32)),
        "wd_r": np.ascontiguousarray(np.asarray(L0_ffn_wd_r, np.float32)),
        "wg_s": np.ascontiguousarray(np.asarray(L0_ffn_wg_s, np.float32)),
        "wu_s": np.ascontiguousarray(np.asarray(L0_ffn_wu_s, np.float32)),
        "wd_s": np.ascontiguousarray(np.asarray(L0_ffn_wd_s, np.float32)),
    }
    shared.update(c)
    in_maps = []
    for core in range(8):
        m = dict(shared)
        m["x"] = np.ascontiguousarray(x[core])
        in_maps.append(m)
    res = run_bass_kernel_spmd(nc, in_maps, core_ids=list(range(8)))
    return np.stack([r["out"] for r in res.results], axis=0)


# version 19
# speedup vs baseline: 1.0317x; 1.0080x over previous
from contextlib import ExitStack

import numpy as np
import ml_dtypes

import concourse.bass as bass
import concourse.mybir as mybir
from concourse.bass_utils import run_bass_kernel_spmd

F32 = mybir.dt.float32
BF16 = mybir.dt.bfloat16
F16 = mybir.dt.float16
AF = mybir.ActivationFunctionType
ALU = mybir.AluOpType
AXX = mybir.AxisListType.X

S, DM, H, HD = 2048, 1024, 16, 64
NT, NDCH = S // 128, DM // 128
NE, NSH, MIDR, MIDS = 16, 2, 256, 512
TB, NTB = 512, S // 512
NU = NE + 2 * NSH
TAB = 2 * S - 1
NGRP = 8

ENGS = ("tensor", "vector", "scalar", "sync", "gpsimd")


class Ph:

    def __init__(self, nc, sems, cnt):
        self.nc, self.sems, self.cnt = nc, sems, cnt
        self.items = {e: [] for e in ENGS}

    def wait(self, eng, ev):
        if ev is not None:
            self.items[eng].append(("w", ev))

    def waits(self, eng, evs):
        for ev in evs:
            self.wait(eng, ev)

    def op(self, eng, fn, inc=None, amt=1):
        ev = None
        if inc is not None:
            self.cnt[inc] = self.cnt.get(inc, 0) + amt
            ev = (inc, self.cnt[inc])
        self.items[eng].append(("o", fn, inc, amt))
        return ev

    def dma(self, eng, out, in_, inc, slow=False):
        return self.op(eng, lambda h, o=out, i=in_, s=slow: h.dma_start(
            out=o, in_=i, allow_slow_non_contiguous=s), inc=inc, amt=16)

    def drain(self, eng):
        self.items[eng].append(("o", lambda h: h.drain(), None, 1))

    def emit(self):
        with self.nc.Block() as block:
            for eng in ENGS:
                items = self.items[eng]
                if not items:
                    continue

                def run(h, items=items):
                    waited = {}
                    pending = {}

                    def flush():
                        for sname in sorted(pending):
                            v = pending[sname]
                            if waited.get(sname, -1) < v:
                                h.wait_ge(self.sems[sname], v)
                                waited[sname] = v
                        pending.clear()

                    for it in items:
                        if it[0] == "w":
                            sname, v = it[1]
                            if pending.get(sname, -1) < v:
                                pending[sname] = v
                        else:
                            flush()
                            _, fn, inc, amt = it
                            ins = fn(h)
                            if inc is not None:
                                ins.then_inc(self.sems[inc], amt)
                    flush()

                getattr(block, eng)(run)


def _ap(t, off, dims):
    return bass.AP(t, int(off), [[int(a), int(b)] for a, b in dims])


def build_nc(stop=None):
    nc = bass.Bass("TRN2", target_bir_lowering=False)

    x_in = nc.dram_tensor("x", [S, DM], F32, kind="ExternalInput")
    w_in = nc.dram_tensor("w_in", [DM, 3 * DM], F32, kind="ExternalInput")
    w_gate = nc.dram_tensor("w_gate", [DM, 2 * DM], F32, kind="ExternalInput")
    b_gate = nc.dram_tensor("b_gate", [2 * DM], F32, kind="ExternalInput")
    amp_in = nc.dram_tensor("amp", [H, 21], F32, kind="ExternalInput")
    sharp_in = nc.dram_tensor("sharp", [H, 21], F32, kind="ExternalInput")
    off_in = nc.dram_tensor("off", [H, 21], F32, kind="ExternalInput")
    router_in = nc.dram_tensor("router_w", [DM, NE], F32, kind="ExternalInput")
    wg_r = nc.dram_tensor("wg_r", [NE, DM, MIDR], F32, kind="ExternalInput")
    wu_r = nc.dram_tensor("wu_r", [NE, DM, MIDR], F32, kind="ExternalInput")
    wd_r = nc.dram_tensor("wd_r", [NE, MIDR, DM], F32, kind="ExternalInput")
    wg_s = nc.dram_tensor("wg_s", [NSH, DM, MIDS], F32, kind="ExternalInput")
    wu_s = nc.dram_tensor("wu_s", [NSH, DM, MIDS], F32, kind="ExternalInput")
    wd_s = nc.dram_tensor("wd_s", [NSH, MIDS, DM], F32, kind="ExternalInput")
    identb_d = nc.dram_tensor("identb", [128, 128], BF16, kind="ExternalInput")
    idf32_d = nc.dram_tensor("idf32", [128, 128], F32, kind="ExternalInput")
    onesf_d = nc.dram_tensor("onesf", [128, 128], F32, kind="ExternalInput")
    ind2_d = nc.dram_tensor("ind2", [128, 2], BF16, kind="ExternalInput")
    reduc_d = nc.dram_tensor("reduc", [128, H], F32, kind="ExternalInput")
    relbc_d = nc.dram_tensor("relbc", [128, TAB], F32, kind="ExternalInput")
    out_d = nc.dram_tensor("out", [S, DM], F32, kind="ExternalOutput")
    etab_d = nc.dram_tensor("etab_scr", [H, TAB], F16)
    comb_d = nc.dram_tensor("comb_scr", [NE, S], F32)
    rbc_d = nc.dram_tensor("rbc_scr", [2, 512], F32)
    sums_d = nc.dram_tensor("sums_scr", [2, 512], F32)
    wbf_d = nc.dram_tensor("wbf_scr", [NU * 3 * 128, 2048], BF16)

    cnt = {k: 0 for k in ("t", "v", "s", "dx", "dga", "dgb", "dwa", "dwb", "dda", "ddb", "dea", "deb", "dba", "dbb", "dwba", "dwbb", "ddba", "ddbb", "dra", "drb", "dec", "ded", "dst", "dgc", "dwc", "ddc", "dbc", "dwbc", "ddbc", "dxx", "dee", "dem")}

    with ExitStack() as octx:
        sems = {n: octx.enter_context(nc.semaphore(n)) for n in cnt}

        def SB(ctx, name, shape, dt):
            return ctx.enter_context(nc.sbuf_tensor(name, shape, dt))

        def PS(ctx, name, shape, dt):
            return ctx.enter_context(nc.psum_tensor(name, shape, dt))

        xt = SB(octx, "xt", [128, NT * DM], F32)
        hf = SB(octx, "hf", [128, NDCH * S], BF16)
        identb = SB(octx, "identb_s", [128, 128], BF16)
        idf32 = SB(octx, "idf32_s", [128, 128], F32)
        onesf = SB(octx, "onesf_s", [128, 128], F32)
        ind2 = SB(octx, "ind2_s", [128, 2], BF16)
        rw_im = SB(octx, "rw_im", [128, NDCH * NE], BF16)
        bg_sb = SB(octx, "bg_sb", [128, 16], F32)

        def tev():
            return ("t", cnt["t"])

        def vev():
            return ("v", cnt["v"])

        def sev():
            return ("s", cnt["s"])

        with ExitStack() as ctx:
            ph = Ph(nc, sems, cnt)
            acc = SB(ctx, "b0_acc", [128, TAB], F32)
            rel = SB(ctx, "b0_rel", [128, TAB], F32)
            dbuf = SB(ctx, "b0_d", [128, TAB], F32)
            qbuf = SB(ctx, "b0_q", [128, TAB], F32)
            tab = SB(ctx, "b0_tab", [16, TAB], F32)
            etab16 = SB(ctx, "b0_etab", [16, TAB], F16)
            reduc = SB(ctx, "b0_reduc", [128, H], F32)
            amp_pad = SB(ctx, "b0_amp", [16, 24], F32)
            shp_pad = SB(ctx, "b0_shp", [16, 24], F32)
            off_pad = SB(ctx, "b0_off", [16, 24], F32)
            amp_pp = SB(ctx, "b0_amppp", [128, 1], F32)
            shp_pp = SB(ctx, "b0_shppp", [128, 1], F32)
            off_pp = SB(ctx, "b0_offpp", [128, 1], F32)
            nshp = SB(ctx, "b0_nshp", [128, 1], F32)
            p0 = PS(ctx, "b0_p0", [16, 1024], F32)

            ph.dma("scalar",
                   _ap(xt, 0, [(NT * DM, 128), (DM, NT), (1, DM)]),
                   _ap(x_in, 0, [(DM, 128), (128 * DM, NT), (1, DM)]), "dxx")
            ph.dma("sync", identb[:, :], identb_d[:, :], "dx")
            ph.dma("sync", idf32[:, :], idf32_d[:, :], "dx")
            ph.dma("sync", onesf[:, :], onesf_d[:, :], "dx")
            ph.dma("sync", ind2[:, :], ind2_d[:, :], "dx")
            ph.dma("sync", reduc[:, :], reduc_d[:, :], "dx")
            ph.dma("sync", rel[:, :], relbc_d[:, :], "dx")
            ph.dma("sync", bg_sb[:, :],
                   _ap(b_gate, 0, [(1, 128), (128, 16)]), "dx", slow=True)
            ph.dma("gpsimd",
                   _ap(rw_im, 0, [(NDCH * NE, 128), (NE, NDCH), (1, NE)]),
                   _ap(router_in, 0, [(NE, 128), (128 * NE, NDCH), (1, NE)]),
                   "dga")
            ph.op("vector", lambda h: h.memset(amp_pad[:, :], 0.0))
            ph.op("vector", lambda h: h.memset(shp_pad[:, :], 0.0))
            ph.op("vector", lambda h: h.memset(off_pad[:, :], 0.0))
            ev_ms = ph.op("vector", lambda h: h.memset(acc[:, :], 0.0), inc="v")
            ph.wait("sync", ev_ms)
            for (dst, src) in ((amp_pad, amp_in), (shp_pad, sharp_in),
                               (off_pad, off_in)):
                ph.dma("sync", _ap(dst, 0, [(24, 16), (1, 21)]),
                       _ap(src, 0, [(21, 16), (1, 21)]), "dx", slow=True)
            ev_rel = ("dx", cnt["dx"])
            ev_loads = ev_rel
            ph.wait("sync", ev_rel)
            ev_accs = [None, None]
            ev_e = [None, None]
            for jg in range(3):
                for ea in ev_accs:
                    ph.wait("sync", ea)
                evg = []
                for (pp, pad) in ((amp_pp, amp_pad), (shp_pp, shp_pad),
                                  (off_pp, off_pad)):
                    evg.append(ph.dma(
                        "sync", pp[:, :],
                        _ap(pad, jg * 8, [(24, 16), (1, 8)]), "dx"))
                ph.waits("scalar", evg)
                ph.op("scalar", lambda h: h.activation(
                    out=nshp[:, :], in_=shp_pp[:, :], func=AF.Abs))
                ph.drain("scalar")
                ph.op("scalar", lambda h: h.mul(
                    out=nshp[:, :], in_=nshp[:, :], mul=-1.0))
                ph.drain("scalar")
                for hh in range(2):
                    n0 = hh * 2048
                    nn = 2047 if hh else 2048
                    ph.waits("vector", evg)
                    ph.wait("vector", ev_rel)
                    ph.wait("vector", ev_e[hh])
                    ev_d = ph.op("vector", lambda h, n0=n0, nn=nn: h.tensor_scalar(
                        out=dbuf[:, n0: n0 + nn], in0=rel[:, n0: n0 + nn],
                        scalar1=off_pp[:, :],
                        scalar2=None, op0=ALU.subtract), inc="v")
                    ph.wait("scalar", ev_d)
                    ph.op("scalar", lambda h, n0=n0, nn=nn: h.activation(
                        out=qbuf[:, n0: n0 + nn], in_=dbuf[:, n0: n0 + nn],
                        func=AF.Square))
                    ph.drain("scalar")
                    ev_e[hh] = ph.op("scalar", lambda h, n0=n0, nn=nn: h.activation(
                        out=dbuf[:, n0: n0 + nn], in_=qbuf[:, n0: n0 + nn],
                        func=AF.Exp, scale=nshp[:, :]), inc="s")
                    ph.wait("vector", ev_e[hh])
                    ev_accs[hh] = ph.op(
                        "vector", lambda h, n0=n0, nn=nn: h.scalar_tensor_tensor(
                            out=acc[:, n0: n0 + nn], in0=dbuf[:, n0: n0 + nn],
                            scalar=amp_pp[:, :],
                            in1=acc[:, n0: n0 + nn],
                            op0=ALU.mult, op1=ALU.add), inc="v")
            ph.wait("tensor", ev_accs[0])
            ph.wait("tensor", ev_accs[1])
            ph.wait("tensor", ev_loads)
            ev_tabslot = [None, None]
            for i in range(8):
                n = min(512, TAB - i * 512)
                if ev_tabslot[i % 2] is not None:
                    ph.wait("tensor", ev_tabslot[i % 2])
                ev_mm = ph.op("tensor", lambda h, i=i, n=n: h.matmul(
                    p0[0:16, (i % 2) * 512: (i % 2) * 512 + n],
                    reduc[:, 0:16], acc[:, i * 512: i * 512 + n],
                    start=True, stop=True), inc="t")
                ph.wait("vector", ev_mm)
                ev_tabslot[i % 2] = ph.op(
                    "vector", lambda h, i=i, n=n: h.tensor_copy(
                        out=tab[0:16, i * 512: i * 512 + n],
                        in_=p0[0:16, (i % 2) * 512: (i % 2) * 512 + n]),
                    inc="v")
            ph.wait("scalar", ev_tabslot[0])
            ph.wait("scalar", ev_tabslot[1])
            ev_et = ph.op("scalar", lambda h: h.activation(
                out=etab16[0:16, :], in_=tab[0:16, :], func=AF.Exp), inc="s")
            ph.wait("sync", ev_et)
            ph.dma("sync", etab_d[:, :], etab16[0:16, :], "dx")
            ph.emit()

        b0dx, b0dga, b0dxx = cnt["dx"], cnt["dga"], cnt["dxx"]

        def blanket(ph):
            for e in ENGS:
                ph.wait(e, ("dx", b0dx))
                ph.wait(e, ("dga", b0dga))
                ph.wait(e, ("dxx", b0dxx))

        def ln_phase(tag):
            with ExitStack() as ctx:
                ph = Ph(nc, sems, cnt)
                blanket(ph)
                stats = SB(ctx, f"ln{tag}_st", [128, NT * 12], F32)
                mv = SB(ctx, f"ln{tag}_mv", [128, NT * 2], F32)
                vtmp = SB(ctx, f"ln{tag}_vt", [128, NT], F32)
                vrec = SB(ctx, f"ln{tag}_vr", [128, NT], F32)
                rstd = SB(ctx, f"ln{tag}_rs", [128, NT], F32)
                negmr = SB(ctx, f"ln{tag}_nm", [128, NT], F32)
                htb = SB(ctx, f"ln{tag}_ht", [128, 2 * DM], BF16)
                p_h = PS(ctx, f"ln{tag}_ph", [128, 2 * DM], BF16)

                for t in range(NT):
                    for hh in range(2):
                        ph.op("vector", lambda h, t=t, hh=hh: h.bn_stats(
                            out=stats[:, t * 12 + hh * 6: t * 12 + hh * 6 + 6],
                            in_=xt[:, t * DM + hh * 512: t * DM + hh * 512 + 512]))
                ph.drain("vector")
                for t in range(NT):
                    ph.op("vector", lambda h, t=t: h.bn_aggr(
                        out=mv[:, t * 2: t * 2 + 2],
                        in_=stats[:, t * 12: t * 12 + 12]))
                ph.drain("vector")
                ph.op("vector", lambda h: h.tensor_scalar(
                    out=vtmp[:, :], in0=_ap(mv, 1, [(NT * 2, 128), (2, NT)]),
                    scalar1=1e-5, scalar2=None, op0=ALU.add))
                ph.drain("vector")
                ev_rec = ph.op("vector", lambda h: h.reciprocal(
                    out=vrec[:, :], in_=vtmp[:, :]), inc="v")
                ph.wait("scalar", ev_rec)
                ev_rstd = ph.op("scalar", lambda h: h.activation(
                    out=rstd[:, :], in_=vrec[:, :], func=AF.Sqrt), inc="s")
                ph.wait("vector", ev_rstd)
                ev_negmr = ph.op("vector", lambda h: h.scalar_tensor_tensor(
                    out=negmr[:, :], in0=_ap(mv, 0, [(NT * 2, 128), (2, NT)]),
                    scalar=-1.0, in1=rstd[:, :],
                    op0=ALU.mult, op1=ALU.mult), inc="v")
                ph.wait("scalar", ev_negmr)
                ph.drain("scalar")

                ev_slot = [None, None]
                for t in range(NT):
                    sl = t % 2
                    if ev_slot[sl] is not None:
                        ph.wait("scalar", ev_slot[sl])
                    ev_ap = ph.op("scalar", lambda h, t=t, sl=sl: h.activation(
                        out=htb[:, sl * DM: sl * DM + DM],
                        in_=xt[:, t * DM: t * DM + DM], func=AF.Identity,
                        bias=negmr[:, t: t + 1], scale=rstd[:, t: t + 1]),
                        inc="s")
                    ph.wait("tensor", ev_ap)
                    if ev_slot[sl] is not None:
                        ph.wait("tensor", ev_slot[sl])
                    for c in range(NDCH):
                        ph.op("tensor", lambda h, sl=sl, c=c: h.transpose(
                            p_h[:, sl * DM + c * 128: sl * DM + c * 128 + 128],
                            htb[:, sl * DM + c * 128: sl * DM + c * 128 + 128],
                            identb[:, :]),
                            inc="t" if c == NDCH - 1 else None)
                    ev_tr = tev()
                    ph.wait("vector", ev_tr)
                    ev_slot[sl] = ph.op("vector", lambda h, t=t, sl=sl: h.tensor_copy(
                        out=_ap(hf, t * 128,
                                [(NDCH * S, 128), (S, NDCH), (1, 128)]),
                        in_=p_h[:, sl * DM: sl * DM + DM]), inc="v")
                ph.emit()

        wb_ev = {}

        def unit_gu_src(u):
            if u < NE:
                return [(wg_r, u * DM * MIDR,
                         [(MIDR, 128), (128 * MIDR, NDCH), (1, 256)]),
                        (wu_r, u * DM * MIDR,
                         [(MIDR, 128), (128 * MIDR, NDCH), (1, 256)])]
            n, mh = (u - NE) // 2, (u - NE) % 2
            return [(wg_s, n * DM * MIDS + mh * 256,
                     [(MIDS, 128), (128 * MIDS, NDCH), (1, 256)]),
                    (wu_s, n * DM * MIDS + mh * 256,
                     [(MIDS, 128), (128 * MIDS, NDCH), (1, 256)])]

        def unit_d_src(u):
            if u < NE:
                return (wd_r, u * MIDR * DM,
                        [(DM, 128), (128 * DM, 2), (1, 1024)])
            n, mh = (u - NE) // 2, (u - NE) % 2
            return (wd_s, (n * MIDS + mh * 256) * DM,
                    [(DM, 128), (128 * DM, 2), (1, 1024)])

        def ffn_phase(tag, first_call):
            with ExitStack() as ctx:
                ph = Ph(nc, sems, cnt)
                blanket(ph)
                P = [PS(ctx, f"f{tag}_p{i}", [128, 1024], F32) for i in range(4)]
                p_all = SB(ctx, f"f{tag}_pall", [128, NU * 1024], BF16)
                gsil = SB(ctx, f"f{tag}_gs", [128, 2 * 1024], BF16)
                GSEM = ("dga", "dgb", "dgc")
                WSEM = ("dwa", "dwb", "dwc")
                DSEM = ("dda", "ddb", "ddc")
                BSEM = ("dba", "dbb", "dbc")
                WBSEM = ("dwba", "dwbb", "dwbc")
                DBSEM = ("ddba", "ddbb", "ddbc")
                WQUEUE = ("sync", "sync", "sync")
                t1 = SB(ctx, f"f{tag}_t1", [128, 1024], F32)
                bc_sb = SB(ctx, f"f{tag}_bc", [128, 3 * TB], F32)
                stage = SB(ctx, f"f{tag}_stg", [128, 2048], F32)
                lg = SB(ctx, f"f{tag}_lg", [128, NT * 16], F32)
                m8 = SB(ctx, f"f{tag}_m8", [128, NT * 8], F32)
                d21 = SB(ctx, f"f{tag}_d21", [128, NT], F32)
                e2 = SB(ctx, f"f{tag}_e2", [128, NT], F32)
                w1 = SB(ctx, f"f{tag}_w1", [128, NT], F32)
                w2 = SB(ctx, f"f{tag}_w2", [128, NT], F32)
                eqt = SB(ctx, f"f{tag}_eq", [128, 16], F32)
                comb = SB(ctx, f"f{tag}_cmb", [128, NT * 16], F32)
                comb_fm = SB(ctx, f"f{tag}_cfm", [16, S], F32)
                wgu = [SB(ctx, f"f{tag}_wgu{i}", [128, 2 * 2048], BF16)
                       for i in range(3)]
                wdb = [SB(ctx, f"f{tag}_wdb{i}", [128, 2048], BF16)
                       for i in range(3)]

                gu_dma, wd_dma, bc_dma = {}, {}, {}
                gu_free = [None, None, None]
                wd_free = [None, None, None]
                bc_free = [None, None, None]
                pre_issued = first_call

                ev_cp = [None] * 4
                for t in range(NT):
                    bk = t % 4
                    pb = P[bk // 2]
                    sl = (bk % 2) * 512 + (t // 4) * 16
                    ph.wait("tensor", ev_cp[bk])
                    for c in range(NDCH):
                        ph.op("tensor", lambda h, t=t, c=c, sl=sl, pb=pb: h.matmul(
                            pb[:, sl: sl + 16],
                            hf[:, c * S + t * 128: c * S + t * 128 + 128],
                            rw_im[:, c * 16: c * 16 + 16],
                            start=(c == 0), stop=(c == NDCH - 1)),
                            inc="t" if c == NDCH - 1 else None)
                    ev_mm = tev()
                    ph.wait("vector", ev_mm)
                    ev_cp[bk] = ph.op(
                        "vector", lambda h, t=t, sl=sl, pb=pb: h.tensor_copy(
                            out=lg[:, t * 16: t * 16 + 16], in_=pb[:, sl: sl + 16]),
                        inc="v")
                ph.drain("vector")
                for t in range(NT):
                    ph.op("vector", lambda h, t=t: h.max(
                        out=m8[:, t * 8: t * 8 + 8],
                        in_=lg[:, t * 16: t * 16 + 16]))
                ph.drain("vector")
                ev_d21 = ph.op("vector", lambda h: h.tensor_sub(
                    d21[:, :], _ap(m8, 1, [(NT * 8, 128), (8, NT)]),
                    _ap(m8, 0, [(NT * 8, 128), (8, NT)])), inc="v")
                ph.drain("vector")
                ph.wait("scalar", ev_d21)
                ev_e2 = ph.op("scalar", lambda h: h.activation(
                    out=e2[:, :], in_=d21[:, :], func=AF.Exp), inc="s")
                ph.wait("vector", ev_e2)
                ph.op("vector", lambda h: h.tensor_scalar(
                    out=w1[:, :], in0=e2[:, :], scalar1=1.0, scalar2=None,
                    op0=ALU.add))
                ph.drain("vector")
                ph.op("vector", lambda h: h.reciprocal(out=w2[:, :], in_=w1[:, :]))
                ph.drain("vector")
                ph.op("vector", lambda h: h.tensor_scalar(
                    out=w1[:, :], in0=w2[:, :], scalar1=-1.0, scalar2=1.0,
                    op0=ALU.mult, op1=ALU.add))
                ph.drain("vector")
                ev_ct = [None, None]
                ev_cf = None
                for t in range(NT):
                    ph.op("vector", lambda h, t=t: h.tensor_scalar(
                        out=eqt[:, :], in0=lg[:, t * 16: t * 16 + 16],
                        scalar1=m8[:, t * 8: t * 8 + 1], scalar2=None,
                        op0=ALU.is_equal))
                    ph.drain("vector")
                    ph.op("vector", lambda h, t=t: h.tensor_scalar(
                        out=comb[:, t * 16: t * 16 + 16], in0=eqt[:, :],
                        scalar1=w2[:, t: t + 1], scalar2=None, op0=ALU.mult))
                    ph.drain("vector")
                    ph.op("vector", lambda h, t=t: h.tensor_scalar(
                        out=eqt[:, :], in0=lg[:, t * 16: t * 16 + 16],
                        scalar1=m8[:, t * 8 + 1: t * 8 + 2], scalar2=None,
                        op0=ALU.is_equal))
                    ph.drain("vector")
                    ev_c = ph.op("vector", lambda h, t=t: h.scalar_tensor_tensor(
                        out=comb[:, t * 16: t * 16 + 16], in0=eqt[:, :],
                        scalar=w1[:, t: t + 1],
                        in1=comb[:, t * 16: t * 16 + 16],
                        op0=ALU.mult, op1=ALU.add), inc="v")
                    ph.drain("vector")
                    ph.wait("tensor", ev_c)
                    sl = 1 + (t % 2)
                    slot = (t // 2) % 8
                    if ev_ct[t % 2] is not None:
                        ph.wait("tensor", ev_ct[t % 2])
                    ev_t = ph.op("tensor", lambda h, t=t, sl=sl, slot=slot: h.transpose(
                        P[sl][0:16, slot * 128: slot * 128 + 128],
                        comb[:, t * 16: t * 16 + 16], idf32[:, :]), inc="t")
                    ph.wait("vector", ev_t)
                    ev_cf = ph.op("vector", lambda h, t=t, sl=sl, slot=slot: h.tensor_copy(
                        out=comb_fm[0:16, t * 128: t * 128 + 128],
                        in_=P[sl][0:16, slot * 128: slot * 128 + 128]), inc="v")
                    ev_ct[t % 2] = ev_cf
                ph.wait("sync", ev_cf)
                ev_comb_d = ph.dma("sync", comb_d[:, :], comb_fm[0:16, :], "dx")

                def issue_gu(tb, u, wonly=False, skipw=False):
                    buf = u % 3
                    sg = GSEM[buf]
                    sw = WSEM[buf]
                    fresh = first_call and tb == 0
                    if skipw:
                        pass
                    elif fresh:
                        if gu_free[buf] is not None:
                            ph.wait("gpsimd", gu_free[buf])
                        evs = []
                        for i, (ten, off, dims) in enumerate(unit_gu_src(u)):
                            evs.append(ph.dma(
                                "gpsimd",
                                _ap(wgu[buf], i * 2048,
                                    [(2 * 2048, 128), (256, NDCH), (1, 256)]),
                                _ap(ten, off, dims), sg))
                        gu_dma[(tb, u)] = evs
                    else:
                        qw = WQUEUE[buf]
                        if gu_free[buf] is not None:
                            ph.wait(qw, gu_free[buf])
                        ph.wait(qw, wb_ev[(u, 0)])
                        gu_dma[(tb, u)] = [ph.dma(
                            qw, wgu[buf][:, :],
                            _ap(wbf_d, (u * 3) * 128 * 2048,
                                [(2048, 128), (128 * 2048, 2), (1, 2048)]),
                            sw)]
                    if wonly:
                        return
                    if u < NE:
                        if bc_free[buf] is not None:
                            ph.wait("sync", bc_free[buf])
                        ph.wait("sync", ev_comb_d)
                        bc_dma[(tb, u)] = ph.dma(
                            "sync", bc_sb[:, buf * TB: buf * TB + TB],
                            _ap(comb_d, u * S + tb * TB, [(0, 128), (1, TB)]),
                            BSEM[buf])

                def issue_wd(tb, u):
                    buf = u % 3
                    sg = GSEM[buf]
                    sd = DSEM[buf]
                    fresh = first_call and tb == 0
                    if fresh:
                        if wd_free[buf] is not None:
                            ph.wait("gpsimd", wd_free[buf])
                        if gu_free[buf] is not None:
                            ph.wait("gpsimd", gu_free[buf])
                        ten, off, dims = unit_d_src(u)
                        wd_dma[(tb, u)] = [ph.dma(
                            "gpsimd", wdb[buf][:, :], _ap(ten, off, dims), sg)]
                    else:
                        qw = WQUEUE[buf]
                        if wd_free[buf] is not None:
                            ph.wait(qw, wd_free[buf])
                        ph.wait(qw, wb_ev[(u, 2)])
                        wd_dma[(tb, u)] = [ph.dma(
                            qw, wdb[buf][:, :],
                            _ap(wbf_d, (u * 3 + 2) * 128 * 2048,
                                [(2048, 128), (1, 2048)]), sd)]

                gsil_free = [None, None]
                ev_adds = vev()
                if first_call:
                    for u0 in range(3):
                        issue_gu(0, u0, wonly=True)

                for tb in range(NTB):
                    fresh = first_call and tb == 0
                    issue_gu(tb, 0, skipw=pre_issued)
                    issue_gu(tb, 1, skipw=pre_issued)
                    issue_gu(tb, 2, skipw=pre_issued)
                    pre_issued = False
                    last_pmult = None
                    last_sil = None
                    for u in range(NU):
                        buf = u % 2
                        wbuf = u % 3
                        pg, pu = P[2 * buf], P[2 * buf + 1]
                        ph.waits("tensor", gu_dma[(tb, u)])
                        if gsil_free[buf] is not None:
                            ph.wait("tensor", gsil_free[buf])
                        ph.wait("tensor", ev_adds)
                        for mc in range(2):
                            for c in range(NDCH):
                                ph.op("tensor", lambda h, buf=wbuf, mc=mc, c=c,
                                      pg=pg, tb=tb: h.matmul(
                                    pg[:, mc * 512: mc * 512 + 512],
                                    wgu[buf][:, c * 256 + mc * 128:
                                             c * 256 + mc * 128 + 128],
                                    hf[:, c * S + tb * TB: c * S + tb * TB + TB],
                                    start=(c == 0), stop=(c == NDCH - 1)),
                                    inc="t" if (mc == 1 and c == NDCH - 1) else None)
                        ev_g = tev()
                        for mc in range(2):
                            for c in range(NDCH):
                                ph.op("tensor", lambda h, buf=wbuf, mc=mc, c=c,
                                      pu=pu, tb=tb: h.matmul(
                                    pu[:, mc * 512: mc * 512 + 512],
                                    wgu[buf][:, 2048 + c * 256 + mc * 128:
                                             2048 + c * 256 + mc * 128 + 128],
                                    hf[:, c * S + tb * TB: c * S + tb * TB + TB],
                                    start=(c == 0), stop=(c == NDCH - 1)),
                                    inc="t" if (mc == 1 and c == NDCH - 1) else None)
                        ev_u = tev()
                        gu_free[wbuf] = ev_u
                        if fresh:
                            ph.wait("sync", ev_u)
                            wb_ev[(u, 0)] = ph.dma(
                                "sync",
                                _ap(wbf_d, (u * 3) * 128 * 2048,
                                    [(2048, 128), (128 * 2048, 2), (1, 2048)]),
                                wgu[wbuf][:, :], WBSEM[wbuf])
                            gu_free[wbuf] = wb_ev[(u, 0)]
                        ph.wait("scalar", ev_g)
                        if gsil_free[buf] is not None:
                            ph.wait("scalar", gsil_free[buf])
                        ev_sil = ph.op("scalar", lambda h, pg=pg, buf=buf: h.activation(
                            out=gsil[:, buf * 1024: buf * 1024 + 1024],
                            in_=pg[:, :], func=AF.Silu), inc="s")
                        last_sil = ev_sil
                        ph.wait("vector", ev_u)
                        ph.wait("vector", ev_sil)
                        if u < NE:
                            ph.wait("vector", bc_dma[(tb, u)])
                            ev_t1 = ph.op("vector", lambda h, pu=pu, buf=wbuf: h.tensor_tensor(
                                out=t1[:, :].rearrange("p (a b) -> p a b", a=2),
                                in0=pu[:, :].rearrange("p (a b) -> p a b", a=2),
                                in1=bc_sb[:, buf * TB: buf * TB + TB]
                                    .unsqueeze(1).to_broadcast([128, 2, TB]),
                                op=ALU.mult), inc="v")
                            bc_free[wbuf] = ev_t1
                            ph.drain("vector")
                            ev_p = ph.op("vector", lambda h, u=u, buf=buf: h.tensor_tensor(
                                out=p_all[:, u * 1024: u * 1024 + 1024],
                                in0=t1[:, :],
                                in1=gsil[:, buf * 1024: buf * 1024 + 1024],
                                op=ALU.mult), inc="v")
                        else:
                            ev_p = ph.op("vector", lambda h, u=u, pu=pu, buf=buf: h.tensor_tensor(
                                out=p_all[:, u * 1024: u * 1024 + 1024],
                                in0=pu[:, :],
                                in1=gsil[:, buf * 1024: buf * 1024 + 1024],
                                op=ALU.mult), inc="v")
                        ph.drain("vector")
                        gsil_free[buf] = ev_p
                        last_pmult = ev_p
                        if u + 3 < NU:
                            issue_gu(tb, u + 3)

                    issue_wd(tb, 0)
                    issue_wd(tb, 1)
                    issue_wd(tb, 2)
                    ph.wait("tensor", last_pmult)
                    ph.wait("tensor", last_sil)
                    for u in range(NU):
                        wbuf = u % 3
                        ph.waits("tensor", wd_dma[(tb, u)])
                        for dch in range(NDCH):
                            po = P[dch // 2]
                            col = (dch % 2) * 512
                            for mc in range(2):
                                ph.op("tensor", lambda h, buf=wbuf, dch=dch,
                                      mc=mc, po=po, col=col, u=u: h.matmul(
                                    po[:, col: col + 512],
                                    wdb[buf][:, mc * 1024 + dch * 128:
                                             mc * 1024 + dch * 128 + 128],
                                    p_all[:, u * 1024 + mc * 512:
                                          u * 1024 + mc * 512 + 512],
                                    start=(u == 0 and mc == 0),
                                    stop=(u == NU - 1 and mc == 1)),
                                    inc="t" if (dch == NDCH - 1 and mc == 1) else None)
                        ev_wdused = tev()
                        wd_free[wbuf] = ev_wdused
                        if fresh:
                            ph.wait("sync", ev_wdused)
                            wb_ev[(u, 2)] = ph.dma(
                                "sync",
                                _ap(wbf_d, (u * 3 + 2) * 128 * 2048,
                                    [(2048, 128), (1, 2048)]),
                                wdb[wbuf][:, :], DBSEM[wbuf])
                            wd_free[wbuf] = wb_ev[(u, 2)]
                        if u + 3 < NU:
                            issue_wd(tb, u + 3)
                    ev_down = tev()

                    ev_adds_l = None
                    ev_tr_prev = None
                    for dh in range(2):
                        ph.wait("scalar", ev_down)
                        if ev_tr_prev is not None:
                            ph.wait("scalar", ev_tr_prev)
                        ev_st = []
                        for j in range(2):
                            ev_st.append(ph.op(
                                "scalar", lambda h, dh=dh, j=j: h.activation(
                                    out=stage[:, j * 1024: j * 1024 + 1024],
                                    in_=P[2 * dh + j][:, :], func=AF.Identity,
                                    scale=0.5), inc="s"))
                        for j in range(2):
                            ph.wait("tensor", ev_st[j])
                            for sub in range(4):
                                for dl in range(2):
                                    ph.op("tensor",
                                          lambda h, j=j, sub=sub, dl=dl, dh=dh: h.transpose(
                                        P[2 * dh + j][:, (sub * 2 + dl) * 128:
                                                      (sub * 2 + dl) * 128 + 128],
                                        stage[:, j * 1024 + dl * 512 + sub * 128:
                                              j * 1024 + dl * 512 + sub * 128 + 128],
                                        idf32[:, :]),
                                        inc="t" if (sub == 3 and dl == 1) else None)
                            ev_tr = tev()
                            ev_tr_prev = ev_tr
                            ph.wait("vector", ev_tr)
                            for sub in range(4):
                                tile = tb * 4 + sub
                                for dl in range(2):
                                    dch = (2 * dh + j) * 2 + dl
                                    ev_adds_l = ph.op(
                                        "vector", lambda h, tile=tile, dch=dch,
                                        j=j, sub=sub, dl=dl, dh=dh: h.tensor_tensor(
                                            out=xt[:, tile * DM + dch * 128:
                                                   tile * DM + dch * 128 + 128],
                                            in0=xt[:, tile * DM + dch * 128:
                                                   tile * DM + dch * 128 + 128],
                                            in1=P[2 * dh + j][:, (sub * 2 + dl) * 128:
                                                              (sub * 2 + dl) * 128 + 128],
                                            op=ALU.add), inc="v")
                            ph.drain("vector")
                    ev_adds = ev_adds_l
                ph.emit()

        def attn_phase():
            with ExitStack() as ctx:
                ph = Ph(nc, sems, cnt)
                blanket(ph)
                etab_ready = ("dx", b0dx)
                af = SB(ctx, "at_af", [128, NDCH * S], BF16)
                kT = SB(ctx, "at_kT", [128, S], BF16)
                vT = SB(ctx, "at_vT", [128, S], BF16)
                qT = SB(ctx, "at_qT", [128, S], BF16)
                kM = SB(ctx, "at_kM", [128, S], BF16)
                qM = SB(ctx, "at_qM", [128, S], BF16)
                vtm = SB(ctx, "at_vtm", [128, 16 * 65], BF16)
                pt = SB(ctx, "at_pt", [128, 2 * 2048], BF16)
                pexp = SB(ctx, "at_pexp", [128, 2 * 2048], BF16)
                ebuf = [SB(ctx, f"at_e{i}", [128, 2432], F16) for i in range(2)]
                win = [SB(ctx, f"at_w{i}", [128, 3 * 1024], BF16) for i in range(2)]
                wgp = win
                sigb = SB(ctx, "at_sig", [128, 2 * 512], BF16)
                gsg = SB(ctx, "at_gsg", [128, 2 * 512], F32)
                bca = SB(ctx, "at_bca", [128, 1024], F16)
                att_sb = SB(ctx, "at_attsb", [128, 2 * 512], F32)
                stag = SB(ctx, "at_stag", [64, 2 * 512], BF16)
                negmh = SB(ctx, "at_ngm", [128, 16], F32)
                mtmp = SB(ctx, "at_mt", [2, 16], F32)
                srec = SB(ctx, "at_srec", [128, 8], F32)
                rrec = SB(ctx, "at_rrec", [128, 8], F32)
                trow = SB(ctx, "at_tr", [1, 4], F32)
                srow = SB(ctx, "at_sr", [1, 4], F32)
                p_s = PS(ctx, "at_ps", [128, 2048], F32)
                p_pr = PS(ctx, "at_ppr", [128, 1024], F32)
                p_v = PS(ctx, "at_pv", [128, 1024], BF16)
                p_bc = PS(ctx, "at_pbc", [128, 8], F32)

                ev_ones = ph.op("vector", lambda h: h.memset(
                    _ap(vtm, 64, [(16 * 65, 128), (65, 16), (1, 1)]), 1.0),
                    inc="v")

                win_dma = {}
                win_free = [None, None]

                def issue_win(g):
                    buf = g % 2
                    if win_free[buf] is not None:
                        ph.wait("gpsimd", win_free[buf])
                    evs = []
                    for part in range(3):
                        evs.append(ph.dma(
                            "gpsimd",
                            _ap(win[buf], part * 1024,
                                [(3 * 1024, 128), (128, NDCH), (1, 128)]),
                            _ap(w_in, part * 1024 + g * 128,
                                [(3 * DM, 128), (128 * 3 * DM, NDCH), (1, 128)]),
                            "dga" if buf == 0 else "dgb"))
                    win_dma[g] = evs

                EQ = []
                for _g in range(NGRP):
                    for _hl in range(2):
                        for _qg in range(4):
                            EQ.append((_g * 2 + _hl, _qg))
                ESEM = ("dea", "deb")
                EQUEUE = ("sync", "scalar")
                e_dma_ev = [None] * len(EQ)
                e_free_ev = [None] * 2
                eq_ptr = [0]

                def issue_next_equad():
                    i = eq_ptr[0]
                    if i >= len(EQ):
                        return
                    eq_ptr[0] = i + 1
                    h_, qg = EQ[i]
                    buf = i % 2
                    qu = EQUEUE[buf]
                    if e_free_ev[buf] is not None:
                        ph.wait(qu, e_free_ev[buf])
                    ph.wait(qu, etab_ready)
                    e_dma_ev[i] = ph.dma(
                        qu, ebuf[buf][:, :],
                        _ap(etab_d, h_ * TAB + 512 * qg,
                            [(1, 128), (1, 2432)]), ESEM[buf])

                issue_win(0)
                issue_win(1)
                for _ in range(2):
                    issue_next_equad()
                eq_use = [0]
                ev_attn_done = []
                odd_dma_evs = []
                ev_vtmfree = None
                pending = [None]
                pend_b = [None]
                pbc_free = [None, None]

                def flush_a():
                    if pending[0] is None:
                        return
                    (par, fh, fqg, fhl, ev_evc) = pending[0]
                    pending[0] = None
                    ph.wait("tensor", ev_evc)
                    ph.wait("tensor", pbc_free[par])
                    for j in range(4):
                        ph.op("tensor", lambda h, par=par, j=j: h.transpose(
                            p_bc[:, par * 4 + j: par * 4 + j + 1],
                            bass.AP(att_sb, 64 * 1024 + par * 512 + j,
                                    [[1024, 1], [4, 128]]),
                            idf32[64:65, 64:65]),
                            inc="t" if j == 3 else None)
                    ev_tr4 = tev()
                    ph.wait("vector", ev_tr4)
                    ev_rin = ph.op("vector", lambda h, par=par: h.reciprocal(
                        out=rrec[:, par * 4: par * 4 + 4],
                        in_=p_bc[:, par * 4: par * 4 + 4]), inc="v")
                    pbc_free[par] = ev_rin
                    dsem = "dra" if par == 0 else "drb"
                    ph.wait("gpsimd", att_ev[par])
                    ph.wait("gpsimd", ev_rin)
                    ev_st2 = ph.dma(
                        "gpsimd", _ap(rbc_d, par * 512, [(4, 128), (1, 4)]),
                        rrec[:, par * 4: par * 4 + 4], dsem)
                    ph.wait("gpsimd", ev_st2)
                    ev_bc = ph.dma(
                        "gpsimd", bca[:, par * 512: par * 512 + 512],
                        _ap(rbc_d, par * 512, [(0, 128), (1, 512)]), dsem)
                    pend_b[0] = (par, fh, fqg, fhl, ev_evc, ev_bc)

                def flush_b():
                    if pend_b[0] is None:
                        return
                    (par, fh, fqg, fhl, ev_evc, ev_bc) = pend_b[0]
                    pend_b[0] = None
                    ph.wait("vector", ev_bc)
                    ph.wait("vector", ev_evc)
                    if fhl == 0:
                        ev_at = ph.op(
                            "vector", lambda h, fh=fh, fqg=fqg,
                            par=par: h.tensor_tensor(
                                out=af[0:64, (fh // 2) * S + fqg * 512:
                                       (fh // 2) * S + fqg * 512 + 512],
                                in0=att_sb[0:64, par * 512: par * 512 + 512],
                                in1=bca[0:64, par * 512: par * 512 + 512],
                                op=ALU.mult), inc="v")
                        ev_attn_done.append(ev_at)
                    else:
                        ssl = fqg % 2
                        if len(odd_dma_evs) >= 2:
                            ph.wait("vector", odd_dma_evs[-1])
                        ev_at = ph.op(
                            "vector", lambda h, ssl=ssl, par=par: h.tensor_tensor(
                                out=stag[0:64, ssl * 512: ssl * 512 + 512],
                                in0=att_sb[0:64, par * 512: par * 512 + 512],
                                in1=bca[0:64, par * 512: par * 512 + 512],
                                op=ALU.mult), inc="v")
                        ph.wait("gpsimd", ev_at)
                        odd_dma_evs.append(ph.dma(
                            "gpsimd",
                            af[64:128, (fh // 2) * S + fqg * 512:
                               (fh // 2) * S + fqg * 512 + 512],
                            stag[0:64, ssl * 512: ssl * 512 + 512], "dst"))
                    att_ev[par] = ev_at

                def flush_pending():
                    flush_a()
                    flush_b()
                expAB = [None, None]
                mult_ev = [[None, None], [None, None]]
                pvlast = [None, None]
                pat_free = [None, None]
                att_ev = [None, None]

                for g in range(NGRP):
                    wbuf = g % 2
                    flush_pending()
                    ph.waits("tensor", win_dma[g])
                    ph.wait("tensor", pat_free[0])
                    ph.wait("tensor", pat_free[1])
                    ev_evac = [None, None]
                    for part in range(3):
                        for tch in range(4):
                            sl = tch % 2
                            if ev_evac[sl] is not None:
                                ph.wait("tensor", ev_evac[sl])
                            for c in range(NDCH):
                                ph.op("tensor", lambda h, part=part, tch=tch,
                                      c=c, wbuf=wbuf, sl=sl: h.matmul(
                                    p_pr[:, sl * 512: sl * 512 + 512],
                                    win[wbuf][:, part * 1024 + c * 128:
                                              part * 1024 + c * 128 + 128],
                                    hf[:, c * S + tch * 512:
                                       c * S + tch * 512 + 512],
                                    start=(c == 0), stop=(c == NDCH - 1)),
                                    inc="t" if c == NDCH - 1 else None)
                            ev_mm = tev()
                            if part == 0:
                                ph.wait("scalar", ev_mm)
                                ev_evac[sl] = ph.op(
                                    "scalar", lambda h, tch=tch, sl=sl: h.copy(
                                        out=kT[:, tch * 512: tch * 512 + 512],
                                        in_=p_pr[:, sl * 512: sl * 512 + 512]),
                                    inc="s")
                            elif part == 1:
                                ph.wait("scalar", ev_mm)
                                ev_evac[sl] = ph.op(
                                    "scalar", lambda h, tch=tch, sl=sl: h.copy(
                                        out=vT[:, tch * 512: tch * 512 + 512],
                                        in_=p_pr[:, sl * 512: sl * 512 + 512]),
                                    inc="s")
                            else:
                                ph.wait("vector", ev_mm)
                                ev_evac[sl] = ph.op(
                                    "vector", lambda h, tch=tch, sl=sl: h.tensor_scalar(
                                        out=_ap(qT, 2047 - tch * 512,
                                                [(S, 128), (-1, 512)]),
                                        in0=p_pr[:, sl * 512: sl * 512 + 512],
                                        scalar1=0.125, scalar2=None,
                                        op0=ALU.mult), inc="v")
                    ev_proj_s = sev()
                    ev_proj_v = vev()
                    pat_free[0] = ev_evac[0]
                    pat_free[1] = ev_evac[1]
                    win_free[wbuf] = tev()
                    if g + 2 < NGRP:
                        issue_win(g + 2)
                    ph.wait("scalar", tev())
                    ph.wait("scalar", ev_proj_s)
                    ph.wait("scalar", ev_proj_v)
                    for (dst, srct) in ((kM, kT), (qM, qT)):
                        for bnd in range(2):
                            ph.dma("scalar", dst[bnd * 64:(bnd + 1) * 64, :],
                                   srct[(1 - bnd) * 64:(2 - bnd) * 64, :],
                                   "dem")
                    ev_mirror = ("dem", cnt["dem"])

                    ph.wait("scalar", ev_proj_v)
                    ph.wait("scalar", pvlast[0])
                    ph.wait("scalar", pvlast[1])
                    sqb = pt
                    ev_sq = [None, None]
                    ev_sq[0] = ph.op("scalar", lambda h: h.activation(
                        out=sqb[:, 0:S], in_=qT[:, :], func=AF.Square), inc="s")
                    ev_sq[1] = ph.op("scalar", lambda h: h.activation(
                        out=sqb[:, S: 2 * S], in_=kT[:, :], func=AF.Square),
                        inc="s")
                    red_ev = [expAB[0], expAB[0], expAB[1], expAB[1]]
                    for src in range(2):
                        ph.wait("tensor", ev_sq[src])
                        for i in range(4):
                            ph.wait("tensor", red_ev[i])
                            ev_nm = ph.op("tensor", lambda h, i=i, src=src:
                                          h.matmul(
                                p_s[0:2, i * 512: i * 512 + 512],
                                ind2[:, :],
                                sqb[:, src * S + i * 512:
                                    src * S + i * 512 + 512],
                                start=True, stop=True), inc="t")
                            ph.wait("vector", ev_nm)
                            red_ev[i] = ph.op(
                                "vector", lambda h, i=i, src=src: h.reduce_max(
                                    out=mtmp[0:2, src * 4 + i: src * 4 + i + 1],
                                    in_=p_s[0:2, i * 512: i * 512 + 512],
                                    axis=AXX), inc="v")
                    expAB[0] = red_ev[1]
                    expAB[1] = red_ev[3]
                    ph.drain("vector")
                    ph.op("vector", lambda h: h.reduce_max(
                        out=mtmp[0:2, 8:9], in_=mtmp[0:2, 0:4], axis=AXX))
                    ph.drain("vector")
                    ph.op("vector", lambda h: h.reduce_max(
                        out=mtmp[0:2, 9:10], in_=mtmp[0:2, 4:8], axis=AXX))
                    ph.drain("vector")
                    ev_t2 = ph.op("vector", lambda h: h.tensor_mul(
                        mtmp[0:2, 10:11], mtmp[0:2, 8:9], mtmp[0:2, 9:10]),
                        inc="v")
                    ph.wait("tensor", ev_t2)
                    ph.wait("tensor", expAB[0])
                    ev_trp = ph.op("tensor", lambda h: h.transpose(
                        p_s[0:1, 0:2], mtmp[0:2, 10:11], idf32[0:2, 0:2]),
                        inc="t")
                    ph.wait("vector", ev_trp)
                    ev_trc = ph.op("vector", lambda h: h.tensor_copy(
                        out=trow[0:1, 0:2], in_=p_s[0:1, 0:2]), inc="v")
                    ph.wait("scalar", ev_trc)
                    ev_sqr = ph.op("scalar", lambda h: h.activation(
                        out=srow[0:1, 0:2], in_=trow[0:1, 0:2], func=AF.Sqrt),
                        inc="s")
                    ph.wait("vector", ev_sqr)
                    ev_trow = ph.op("vector", lambda h: h.tensor_scalar(
                        out=trow[0:1, 2:4], in0=srow[0:1, 0:2],
                        scalar1=-1.0, scalar2=-0.5,
                        op0=ALU.mult, op1=ALU.add), inc="v")
                    ph.wait("tensor", ev_trow)
                    ph.wait("tensor", ev_trc)
                    ev_bcm = ph.op("tensor", lambda h: h.matmul(
                        p_s[:, 0:2], onesf[0:1, :], trow[0:1, 2:4],
                        start=True, stop=True), inc="t")
                    ph.wait("scalar", ev_bcm)
                    ev_negmh = ph.op("scalar", lambda h, g=g: h.copy(
                        out=negmh[:, g * 2: g * 2 + 2], in_=p_s[:, 0:2]),
                        inc="s")
                    ph.wait("tensor", ev_negmh)
                    expAB[0] = None

                    for hl in range(2):
                        h_ = g * 2 + hl
                        base = hl * 64
                        if ev_vtmfree is not None:
                            ph.wait("tensor", ev_vtmfree)
                        ph.wait("tensor", ev_proj_s)
                        ph.wait("tensor", ev_proj_v)
                        for c in range(16):
                            ph.op("tensor", lambda h, c=c, base=base: h.transpose(
                                p_v[:, c * 64: c * 64 + 64],
                                vT[base: base + 64, c * 128: c * 128 + 128],
                                identb[base: base + 64, base: base + 64]),
                                inc="t" if c == 15 else None)
                        ev_vtr = tev()
                        ph.wait("vector", ev_vtr)
                        ph.wait("vector", ev_ones)
                        ev_vtm = ph.op("vector", lambda h: h.tensor_copy(
                            out=_ap(vtm, 0, [(16 * 65, 128), (65, 16), (1, 64)]),
                            in_=_ap(p_v, 0, [(1024, 128), (64, 16), (1, 64)])),
                            inc="v")
                        ev_vtmfree = ev_vtm

                        for qg in range(4):
                            hgi = h_ * 4 + qg
                            par = hgi % 2
                            ev_s = [[None, None] for _ in range(4)]
                            ev_exp = [[None, None] for _ in range(4)]
                            ev_pv = [None] * 4

                            def emit_pv(q, qg=qg, par=par, ev_pv=ev_pv,
                                        ev_vtm=ev_vtm):
                                for j in range(4):
                                    c = q * 4 + j
                                    ph.wait("tensor", mult_ev[q % 2][j // 2])
                                    if c == 0:
                                        ph.wait("tensor", pat_free[par])
                                        ph.wait("tensor", ev_vtm)
                                    ph.op("tensor", lambda h, c=c, q=q, j=j,
                                          par=par: h.matmul(
                                        p_pr[0:65, par * 512: par * 512 + 512],
                                        vtm[:, c * 65: c * 65 + 65],
                                        pt[:, (q % 2) * 2048 + j * 512:
                                           (q % 2) * 2048 + j * 512 + 512],
                                        start=(c == 0), stop=(c == 15)),
                                        inc="t" if j == 3 else None)
                                ev_pv[q] = tev()
                                pvlast[q % 2] = ev_pv[q]

                            mbase = 64 - base
                            for q in range(4):
                                ph.wait("tensor", expAB[0])
                                ph.wait("tensor", ev_mirror)
                                for j in range(2):
                                    c = q * 4 + j
                                    kk = kT if j % 2 == 0 else kM
                                    qq = qT if j % 2 == 0 else qM
                                    bb = base if j % 2 == 0 else mbase
                                    ph.op("tensor", lambda h, c=c, bb=bb,
                                          kk=kk, qq=qq, qg=qg: h.matmul(
                                        p_s[:, (c % 4) * 512: (c % 4) * 512 + 512],
                                        kk[bb: bb + 64, c * 128: c * 128 + 128],
                                        qq[bb: bb + 64, qg * 512: qg * 512 + 512],
                                        start=True, stop=True),
                                        inc="t" if j == 1 else None)
                                ev_sA = tev()
                                ph.wait("tensor", expAB[1])
                                for j in range(2, 4):
                                    c = q * 4 + j
                                    kk = kT if j % 2 == 0 else kM
                                    qq = qT if j % 2 == 0 else qM
                                    bb = base if j % 2 == 0 else mbase
                                    ph.op("tensor", lambda h, c=c, bb=bb,
                                          kk=kk, qq=qq, qg=qg: h.matmul(
                                        p_s[:, (c % 4) * 512: (c % 4) * 512 + 512],
                                        kk[bb: bb + 64, c * 128: c * 128 + 128],
                                        qq[bb: bb + 64, qg * 512: qg * 512 + 512],
                                        start=True, stop=True),
                                        inc="t" if j == 3 else None)
                                ev_sB = tev()
                                if q == 1:
                                    flush_a()
                                if q >= 1:
                                    emit_pv(q - 1)
                                for half_i, ev_sx in ((0, ev_sA), (1, ev_sB)):
                                    ph.wait("scalar", ev_sx)
                                    ph.wait("scalar", ev_negmh)
                                    ph.wait("scalar", mult_ev[q % 2][half_i])
                                    ev_e_ = ph.op(
                                        "scalar", lambda h, q=q, half_i=half_i,
                                        hh=h_: h.activation(
                                            out=pexp[:, (q % 2) * 2048 +
                                                     half_i * 1024:
                                                     (q % 2) * 2048 +
                                                     half_i * 1024 + 1024],
                                            in_=p_s[:, half_i * 1024:
                                                    half_i * 1024 + 1024],
                                            func=AF.Exp,
                                            bias=negmh[:, hh: hh + 1]),
                                        inc="s")
                                    expAB[half_i] = ev_e_
                                    ev_exp[q][half_i] = ev_e_
                                ei = eq_use[0]
                                ebuf_i = ebuf[ei % 2]
                                for j in range(4):
                                    ph.wait("vector", ev_exp[q][j // 2])
                                    ph.wait("vector", e_dma_ev[ei])
                                    ph.wait("vector", pvlast[q % 2])
                                    mev = ph.op(
                                        "vector", lambda h, q=q, j=j,
                                        ebuf_i=ebuf_i: h.tensor_tensor(
                                            out=pt[:, (q % 2) * 2048 + j * 512:
                                                   (q % 2) * 2048 + j * 512
                                                   + 512],
                                            in0=pexp[:, (q % 2) * 2048 +
                                                     j * 512:
                                                     (q % 2) * 2048 +
                                                     j * 512 + 512],
                                            in1=ebuf_i[:, q * 512 + j * 128:
                                                       q * 512 + j * 128 + 512],
                                            op=ALU.mult), inc="v")
                                    if j % 2 == 1:
                                        mult_ev[q % 2][j // 2] = mev
                                if q == 3:
                                    eq_use[0] = ei + 1
                                    e_free_ev[ei % 2] = mult_ev[q % 2][1]
                                    issue_next_equad()
                            emit_pv(3)
                            flush_b()
                            ph.wait("scalar", ev_pv[3])
                            ph.wait("scalar", att_ev[par])
                            ev_evc = ph.op("scalar", lambda h, par=par: h.copy(
                                out=att_sb[0:65, par * 512: par * 512 + 512],
                                in_=p_pr[0:65, par * 512: par * 512 + 512]),
                                inc="s")
                            pat_free[par] = ev_evc
                            pending[0] = (par, h_, qg, hl, ev_evc)

                flush_pending()
                wg_dma = {}
                wg_free = [None, None]

                def issue_wg(ja):
                    buf = ja % 2
                    if wg_free[buf] is not None:
                        ph.wait("gpsimd", wg_free[buf])
                    if win_free[buf] is not None:
                        ph.wait("gpsimd", win_free[buf])
                    evs = []
                    for i, j in enumerate((ja, ja + 8)):
                        evs.append(ph.dma(
                            "gpsimd",
                            _ap(wgp[buf], i * 1024,
                                [(3 * 1024, 128), (128, NDCH), (1, 128)]),
                            _ap(w_gate, j * 128,
                                [(2 * DM, 128), (128 * 2 * DM, NDCH), (1, 128)]),
                            "dga" if buf == 0 else "dgb"))
                    wg_dma[ja] = evs

                issue_wg(0)
                issue_wg(1)
                ph.waits("tensor", ev_attn_done)
                ph.waits("tensor", odd_dma_evs)
                ph.wait("tensor", expAB[0])
                ph.wait("tensor", expAB[1])
                ph.wait("tensor", pat_free[0])
                ph.wait("tensor", pat_free[1])
                ev_gsl = [None, None]
                ev_ppr_free = None
                for ja in range(8):
                    buf = ja % 2
                    ph.waits("tensor", wg_dma[ja])
                    for tch in range(4):
                        if ev_gsl[0] is not None:
                            ph.wait("tensor", ev_gsl[0])
                        if ev_gsl[1] is not None:
                            ph.wait("tensor", ev_gsl[1])
                        for i in range(2):
                            for c in range(NDCH):
                                ph.op("tensor", lambda h, ja=ja, tch=tch, i=i,
                                      c=c, buf=buf: h.matmul(
                                    p_s[:, i * 512: i * 512 + 512],
                                    wgp[buf][:, i * 1024 + c * 128:
                                             i * 1024 + c * 128 + 128],
                                    af[:, c * S + tch * 512:
                                       c * S + tch * 512 + 512],
                                    start=(c == 0), stop=(c == NDCH - 1)),
                                    inc="t" if c == NDCH - 1 else None)
                        ev_ag = tev()
                        ssl = tch % 2
                        ph.wait("scalar", ev_ag)
                        if ev_gsl[ssl] is not None:
                            ph.wait("scalar", ev_gsl[ssl])
                        ev_sig = ph.op("scalar", lambda h, ja=ja, ssl=ssl: h.activation(
                            out=sigb[:, ssl * 512: ssl * 512 + 512],
                            in_=p_s[:, 512:1024], func=AF.Sigmoid,
                            bias=bg_sb[:, 8 + ja: 9 + ja]), inc="s")
                        ph.wait("vector", ev_sig)
                        ev_gtd = ph.op("vector", lambda h, ja=ja, ssl=ssl: h.scalar_tensor_tensor(
                            out=_ap(gsg, ssl * 512 + 511,
                                    [(2 * 512, 128), (-1, 512)]),
                            in0=p_s[:, 0:512], scalar=bg_sb[:, ja: ja + 1],
                            in1=sigb[:, ssl * 512: ssl * 512 + 512],
                            op0=ALU.add, op1=ALU.mult), inc="v")
                        ph.drain("vector")
                        ev_gsl[ssl] = ev_gtd
                        ph.wait("tensor", ev_gtd)
                        if ev_ppr_free is not None:
                            ph.wait("tensor", ev_ppr_free)
                        for sub in range(4):
                            ph.op("tensor", lambda h, sub=sub, ssl=ssl: h.transpose(
                                p_pr[:, (ssl * 4 + sub) * 128:
                                     (ssl * 4 + sub) * 128 + 128],
                                gsg[:, ssl * 512 + sub * 128:
                                    ssl * 512 + sub * 128 + 128],
                                idf32[:, :]),
                                inc="t" if sub == 3 else None)
                        ev_tr = tev()
                        ph.wait("vector", ev_tr)
                        ev_add = None
                        for sub in range(4):
                            tile = (3 - tch) * 4 + sub
                            ev_add = ph.op("vector", lambda h, tile=tile, ja=ja,
                                           sub=sub, ssl=ssl: h.tensor_tensor(
                                out=xt[:, tile * DM + ja * 128:
                                       tile * DM + ja * 128 + 128],
                                in0=xt[:, tile * DM + ja * 128:
                                       tile * DM + ja * 128 + 128],
                                in1=p_pr[:, (ssl * 4 + sub) * 128:
                                         (ssl * 4 + sub) * 128 + 128],
                                op=ALU.add), inc="v")
                        ph.drain("vector")
                        ev_ppr_free = ev_add
                    wg_free[buf] = tev()
                    if ja + 2 < 8:
                        issue_wg(ja + 2)
                ph.emit()

        ln_phase(0)
        ffn_phase(0, first_call=True)
        if stop != "ffn1":
            ln_phase(1)
            attn_phase()
            if stop != "attn":
                ln_phase(2)
                ffn_phase(1, first_call=False)

        with ExitStack() as ctx:
            ph = Ph(nc, sems, cnt)
            ev = ph.dma("sync",
                        _ap(out_d, 0, [(DM, 128), (128 * DM, NT), (1, DM)]),
                        _ap(xt, 0, [(NT * DM, 128), (DM, NT), (1, DM)]), "dx")
            ph.wait("sync", ev)
            ph.op("sync", lambda h: h.nop())
            ph.emit()

    return nc


_CACHE = {}


def _consts():
    if "c" not in _CACHE:
        rel = (np.arange(TAB, dtype=np.float32) - (S - 1))
        ind2 = np.zeros((128, 2), dtype=ml_dtypes.bfloat16)
        ind2[0:64, 0] = 1
        ind2[64:128, 1] = 1
        reduc = np.zeros((128, H), dtype=np.float32)
        for p in range(128):
            reduc[p, p // 8] = 1.0
        _CACHE["c"] = {
            "identb": np.eye(128, dtype=ml_dtypes.bfloat16),
            "idf32": np.eye(128, dtype=np.float32),
            "onesf": np.ones((128, 128), dtype=np.float32),
            "ind2": ind2,
            "reduc": reduc,
            "relbc": np.broadcast_to(rel, (128, TAB)).copy(),
        }
    return _CACHE["c"]


def kernel(L0_attn_x, L0_attn_w_in, L0_attn_w_gate, L0_attn_b_gate,
           L0_attn_amplitudes, L0_attn_sharpness, L0_attn_offsets,
           L0_ffn_router_w, L0_ffn_wg_r, L0_ffn_wu_r, L0_ffn_wd_r,
           L0_ffn_wg_s, L0_ffn_wu_s, L0_ffn_wd_s):
    if "nc" not in _CACHE:
        _CACHE["nc"] = build_nc()
    nc = _CACHE["nc"]
    c = _consts()
    x = np.ascontiguousarray(np.asarray(L0_attn_x, dtype=np.float32))
    shared = {
        "w_in": np.ascontiguousarray(np.asarray(L0_attn_w_in, np.float32)),
        "w_gate": np.ascontiguousarray(np.asarray(L0_attn_w_gate, np.float32)),
        "b_gate": np.ascontiguousarray(np.asarray(L0_attn_b_gate, np.float32)),
        "amp": np.ascontiguousarray(np.asarray(L0_attn_amplitudes, np.float32)),
        "sharp": np.ascontiguousarray(np.asarray(L0_attn_sharpness, np.float32)),
        "off": np.ascontiguousarray(np.asarray(L0_attn_offsets, np.float32)),
        "router_w": np.ascontiguousarray(np.asarray(L0_ffn_router_w, np.float32)),
        "wg_r": np.ascontiguousarray(np.asarray(L0_ffn_wg_r, np.float32)),
        "wu_r": np.ascontiguousarray(np.asarray(L0_ffn_wu_r, np.float32)),
        "wd_r": np.ascontiguousarray(np.asarray(L0_ffn_wd_r, np.float32)),
        "wg_s": np.ascontiguousarray(np.asarray(L0_ffn_wg_s, np.float32)),
        "wu_s": np.ascontiguousarray(np.asarray(L0_ffn_wu_s, np.float32)),
        "wd_s": np.ascontiguousarray(np.asarray(L0_ffn_wd_s, np.float32)),
    }
    shared.update(c)
    in_maps = []
    for core in range(8):
        m = dict(shared)
        m["x"] = np.ascontiguousarray(x[core])
        in_maps.append(m)
    res = run_bass_kernel_spmd(nc, in_maps, core_ids=list(range(8)))
    return np.stack([r["out"] for r in res.results], axis=0)


# version 20
# speedup vs baseline: 1.0407x; 1.0088x over previous
from contextlib import ExitStack

import numpy as np
import ml_dtypes

import concourse.bass as bass
import concourse.mybir as mybir
from concourse.bass_utils import run_bass_kernel_spmd

F32 = mybir.dt.float32
BF16 = mybir.dt.bfloat16
F16 = mybir.dt.float16
AF = mybir.ActivationFunctionType
ALU = mybir.AluOpType
AXX = mybir.AxisListType.X

S, DM, H, HD = 2048, 1024, 16, 64
NT, NDCH = S // 128, DM // 128
NE, NSH, MIDR, MIDS = 16, 2, 256, 512
TB, NTB = 512, S // 512
NU = NE + 2 * NSH
TAB = 2 * S - 1
NGRP = 8

ENGS = ("tensor", "vector", "scalar", "sync", "gpsimd")


class Ph:

    def __init__(self, nc, sems, cnt):
        self.nc, self.sems, self.cnt = nc, sems, cnt
        self.items = {e: [] for e in ENGS}

    def wait(self, eng, ev):
        if ev is not None:
            self.items[eng].append(("w", ev))

    def waits(self, eng, evs):
        for ev in evs:
            self.wait(eng, ev)

    def op(self, eng, fn, inc=None, amt=1):
        ev = None
        if inc is not None:
            self.cnt[inc] = self.cnt.get(inc, 0) + amt
            ev = (inc, self.cnt[inc])
        self.items[eng].append(("o", fn, inc, amt))
        return ev

    def dma(self, eng, out, in_, inc, slow=False):
        return self.op(eng, lambda h, o=out, i=in_, s=slow: h.dma_start(
            out=o, in_=i, allow_slow_non_contiguous=s), inc=inc, amt=16)

    def drain(self, eng):
        self.items[eng].append(("o", lambda h: h.drain(), None, 1))

    def emit(self):
        with self.nc.Block() as block:
            for eng in ENGS:
                items = self.items[eng]
                if not items:
                    continue

                def run(h, items=items):
                    waited = {}
                    pending = {}

                    def flush():
                        for sname in sorted(pending):
                            v = pending[sname]
                            if waited.get(sname, -1) < v:
                                h.wait_ge(self.sems[sname], v)
                                waited[sname] = v
                        pending.clear()

                    for it in items:
                        if it[0] == "w":
                            sname, v = it[1]
                            if pending.get(sname, -1) < v:
                                pending[sname] = v
                        else:
                            flush()
                            _, fn, inc, amt = it
                            ins = fn(h)
                            if inc is not None:
                                ins.then_inc(self.sems[inc], amt)
                    flush()

                getattr(block, eng)(run)


def _ap(t, off, dims):
    return bass.AP(t, int(off), [[int(a), int(b)] for a, b in dims])


def build_nc(stop=None):
    nc = bass.Bass("TRN2", target_bir_lowering=False)

    x_in = nc.dram_tensor("x", [S, DM], F32, kind="ExternalInput")
    w_in = nc.dram_tensor("w_in", [DM, 3 * DM], F32, kind="ExternalInput")
    w_gate = nc.dram_tensor("w_gate", [DM, 2 * DM], F32, kind="ExternalInput")
    b_gate = nc.dram_tensor("b_gate", [2 * DM], F32, kind="ExternalInput")
    amp_in = nc.dram_tensor("amp", [H, 21], F32, kind="ExternalInput")
    sharp_in = nc.dram_tensor("sharp", [H, 21], F32, kind="ExternalInput")
    off_in = nc.dram_tensor("off", [H, 21], F32, kind="ExternalInput")
    router_in = nc.dram_tensor("router_w", [DM, NE], F32, kind="ExternalInput")
    wg_r = nc.dram_tensor("wg_r", [NE, DM, MIDR], F32, kind="ExternalInput")
    wu_r = nc.dram_tensor("wu_r", [NE, DM, MIDR], F32, kind="ExternalInput")
    wd_r = nc.dram_tensor("wd_r", [NE, MIDR, DM], F32, kind="ExternalInput")
    wg_s = nc.dram_tensor("wg_s", [NSH, DM, MIDS], F32, kind="ExternalInput")
    wu_s = nc.dram_tensor("wu_s", [NSH, DM, MIDS], F32, kind="ExternalInput")
    wd_s = nc.dram_tensor("wd_s", [NSH, MIDS, DM], F32, kind="ExternalInput")
    identb_d = nc.dram_tensor("identb", [128, 128], BF16, kind="ExternalInput")
    idf32_d = nc.dram_tensor("idf32", [128, 128], F32, kind="ExternalInput")
    onesf_d = nc.dram_tensor("onesf", [128, 128], F32, kind="ExternalInput")
    ind2_d = nc.dram_tensor("ind2", [128, 2], BF16, kind="ExternalInput")
    reduc_d = nc.dram_tensor("reduc", [128, H], F32, kind="ExternalInput")
    relbc_d = nc.dram_tensor("relbc", [128, TAB], F32, kind="ExternalInput")
    out_d = nc.dram_tensor("out", [S, DM], F32, kind="ExternalOutput")
    etab_d = nc.dram_tensor("etab_scr", [H, TAB], F16)
    comb_d = nc.dram_tensor("comb_scr", [NE, S], F32)
    rbc_d = nc.dram_tensor("rbc_scr", [2, 512], F32)
    sums_d = nc.dram_tensor("sums_scr", [2, 512], F32)
    wbf_d = nc.dram_tensor("wbf_scr", [NU * 3 * 128, 2048], BF16)

    cnt = {k: 0 for k in ("t", "v", "s", "dx", "dga", "dgb", "dwa", "dwb", "dda", "ddb", "dea", "deb", "dba", "dbb", "dwba", "dwbb", "ddba", "ddbb", "dra", "drb", "dec", "ded", "dst", "dgc", "dwc", "ddc", "dbc", "dwbc", "ddbc", "dxx", "dee", "dem")}

    with ExitStack() as octx:
        sems = {n: octx.enter_context(nc.semaphore(n)) for n in cnt}

        def SB(ctx, name, shape, dt):
            return ctx.enter_context(nc.sbuf_tensor(name, shape, dt))

        def PS(ctx, name, shape, dt):
            return ctx.enter_context(nc.psum_tensor(name, shape, dt))

        xt = SB(octx, "xt", [128, NT * DM], F32)
        hf = SB(octx, "hf", [128, NDCH * S], BF16)
        identb = SB(octx, "identb_s", [128, 128], BF16)
        idf32 = SB(octx, "idf32_s", [128, 128], F32)
        onesf = SB(octx, "onesf_s", [128, 128], F32)
        ind2 = SB(octx, "ind2_s", [128, 2], BF16)
        rw_im = SB(octx, "rw_im", [128, NDCH * NE], BF16)
        bg_sb = SB(octx, "bg_sb", [128, 16], F32)

        def tev():
            return ("t", cnt["t"])

        def vev():
            return ("v", cnt["v"])

        def sev():
            return ("s", cnt["s"])

        with ExitStack() as ctx:
            ph = Ph(nc, sems, cnt)
            acc = SB(ctx, "b0_acc", [128, TAB], F32)
            rel = SB(ctx, "b0_rel", [128, TAB], F32)
            dbuf = SB(ctx, "b0_d", [128, TAB], F32)
            qbuf = SB(ctx, "b0_q", [128, TAB], F32)
            tab = SB(ctx, "b0_tab", [16, TAB], F32)
            etab16 = SB(ctx, "b0_etab", [16, TAB], F16)
            reduc = SB(ctx, "b0_reduc", [128, H], F32)
            amp_pad = SB(ctx, "b0_amp", [16, 24], F32)
            shp_pad = SB(ctx, "b0_shp", [16, 24], F32)
            off_pad = SB(ctx, "b0_off", [16, 24], F32)
            amp_pp = SB(ctx, "b0_amppp", [128, 1], F32)
            shp_pp = SB(ctx, "b0_shppp", [128, 1], F32)
            off_pp = SB(ctx, "b0_offpp", [128, 1], F32)
            nshp = SB(ctx, "b0_nshp", [128, 1], F32)
            p0 = PS(ctx, "b0_p0", [16, 1024], F32)

            ph.dma("scalar",
                   _ap(xt, 0, [(NT * DM, 128), (DM, NT), (1, DM)]),
                   _ap(x_in, 0, [(DM, 128), (128 * DM, NT), (1, DM)]), "dxx")
            ph.dma("sync", identb[:, :], identb_d[:, :], "dx")
            ph.dma("sync", idf32[:, :], idf32_d[:, :], "dx")
            ph.dma("sync", onesf[:, :], onesf_d[:, :], "dx")
            ph.dma("sync", ind2[:, :], ind2_d[:, :], "dx")
            ph.dma("sync", reduc[:, :], reduc_d[:, :], "dx")
            ph.dma("sync", rel[:, :], relbc_d[:, :], "dx")
            ph.dma("sync", bg_sb[:, :],
                   _ap(b_gate, 0, [(1, 128), (128, 16)]), "dx", slow=True)
            ph.dma("gpsimd",
                   _ap(rw_im, 0, [(NDCH * NE, 128), (NE, NDCH), (1, NE)]),
                   _ap(router_in, 0, [(NE, 128), (128 * NE, NDCH), (1, NE)]),
                   "dga")
            ph.op("vector", lambda h: h.memset(amp_pad[:, :], 0.0))
            ph.op("vector", lambda h: h.memset(shp_pad[:, :], 0.0))
            ph.op("vector", lambda h: h.memset(off_pad[:, :], 0.0))
            ev_ms = ph.op("vector", lambda h: h.memset(acc[:, :], 0.0), inc="v")
            ph.wait("sync", ev_ms)
            for (dst, src) in ((amp_pad, amp_in), (shp_pad, sharp_in),
                               (off_pad, off_in)):
                ph.dma("sync", _ap(dst, 0, [(24, 16), (1, 21)]),
                       _ap(src, 0, [(21, 16), (1, 21)]), "dx", slow=True)
            ev_rel = ("dx", cnt["dx"])
            ev_loads = ev_rel
            ph.wait("sync", ev_rel)
            ev_accs = [None, None]
            ev_e = [None, None]
            for jg in range(3):
                for ea in ev_accs:
                    ph.wait("sync", ea)
                evg = []
                for (pp, pad) in ((amp_pp, amp_pad), (shp_pp, shp_pad),
                                  (off_pp, off_pad)):
                    evg.append(ph.dma(
                        "sync", pp[:, :],
                        _ap(pad, jg * 8, [(24, 16), (1, 8)]), "dx"))
                ph.waits("scalar", evg)
                ph.op("scalar", lambda h: h.activation(
                    out=nshp[:, :], in_=shp_pp[:, :], func=AF.Abs))
                ph.drain("scalar")
                ph.op("scalar", lambda h: h.mul(
                    out=nshp[:, :], in_=nshp[:, :], mul=-1.0))
                ph.drain("scalar")
                for hh in range(2):
                    n0 = hh * 2048
                    nn = 2047 if hh else 2048
                    ph.waits("vector", evg)
                    ph.wait("vector", ev_rel)
                    ph.wait("vector", ev_e[hh])
                    ev_d = ph.op("vector", lambda h, n0=n0, nn=nn: h.tensor_scalar(
                        out=dbuf[:, n0: n0 + nn], in0=rel[:, n0: n0 + nn],
                        scalar1=off_pp[:, :],
                        scalar2=None, op0=ALU.subtract), inc="v")
                    ph.wait("scalar", ev_d)
                    ph.op("scalar", lambda h, n0=n0, nn=nn: h.activation(
                        out=qbuf[:, n0: n0 + nn], in_=dbuf[:, n0: n0 + nn],
                        func=AF.Square))
                    ph.drain("scalar")
                    ev_e[hh] = ph.op("scalar", lambda h, n0=n0, nn=nn: h.activation(
                        out=dbuf[:, n0: n0 + nn], in_=qbuf[:, n0: n0 + nn],
                        func=AF.Exp, scale=nshp[:, :]), inc="s")
                    ph.wait("vector", ev_e[hh])
                    ev_accs[hh] = ph.op(
                        "vector", lambda h, n0=n0, nn=nn: h.scalar_tensor_tensor(
                            out=acc[:, n0: n0 + nn], in0=dbuf[:, n0: n0 + nn],
                            scalar=amp_pp[:, :],
                            in1=acc[:, n0: n0 + nn],
                            op0=ALU.mult, op1=ALU.add), inc="v")
            ph.wait("tensor", ev_accs[0])
            ph.wait("tensor", ev_accs[1])
            ph.wait("tensor", ev_loads)
            ev_tabslot = [None, None]
            for i in range(8):
                n = min(512, TAB - i * 512)
                if ev_tabslot[i % 2] is not None:
                    ph.wait("tensor", ev_tabslot[i % 2])
                ev_mm = ph.op("tensor", lambda h, i=i, n=n: h.matmul(
                    p0[0:16, (i % 2) * 512: (i % 2) * 512 + n],
                    reduc[:, 0:16], acc[:, i * 512: i * 512 + n],
                    start=True, stop=True), inc="t")
                ph.wait("vector", ev_mm)
                ev_tabslot[i % 2] = ph.op(
                    "vector", lambda h, i=i, n=n: h.tensor_copy(
                        out=tab[0:16, i * 512: i * 512 + n],
                        in_=p0[0:16, (i % 2) * 512: (i % 2) * 512 + n]),
                    inc="v")
            ph.wait("scalar", ev_tabslot[0])
            ph.wait("scalar", ev_tabslot[1])
            ev_et = ph.op("scalar", lambda h: h.activation(
                out=etab16[0:16, :], in_=tab[0:16, :], func=AF.Exp), inc="s")
            ph.wait("sync", ev_et)
            ph.dma("sync", etab_d[:, :], etab16[0:16, :], "dx")
            ph.emit()

        b0dx, b0dga, b0dxx = cnt["dx"], cnt["dga"], cnt["dxx"]

        def blanket(ph):
            for e in ENGS:
                ph.wait(e, ("dx", b0dx))
                ph.wait(e, ("dga", b0dga))
                ph.wait(e, ("dxx", b0dxx))

        def ln_phase(tag):
            with ExitStack() as ctx:
                ph = Ph(nc, sems, cnt)
                blanket(ph)
                stats = SB(ctx, f"ln{tag}_st", [128, NT * 12], F32)
                mv = SB(ctx, f"ln{tag}_mv", [128, NT * 2], F32)
                vtmp = SB(ctx, f"ln{tag}_vt", [128, NT], F32)
                vrec = SB(ctx, f"ln{tag}_vr", [128, NT], F32)
                rstd = SB(ctx, f"ln{tag}_rs", [128, NT], F32)
                negmr = SB(ctx, f"ln{tag}_nm", [128, NT], F32)
                htb = SB(ctx, f"ln{tag}_ht", [128, 2 * DM], BF16)
                p_h = PS(ctx, f"ln{tag}_ph", [128, 2 * DM], BF16)

                for t in range(NT):
                    for hh in range(2):
                        ph.op("vector", lambda h, t=t, hh=hh: h.bn_stats(
                            out=stats[:, t * 12 + hh * 6: t * 12 + hh * 6 + 6],
                            in_=xt[:, t * DM + hh * 512: t * DM + hh * 512 + 512]))
                ph.drain("vector")
                for t in range(NT):
                    ph.op("vector", lambda h, t=t: h.bn_aggr(
                        out=mv[:, t * 2: t * 2 + 2],
                        in_=stats[:, t * 12: t * 12 + 12]))
                ph.drain("vector")
                ph.op("vector", lambda h: h.tensor_scalar(
                    out=vtmp[:, :], in0=_ap(mv, 1, [(NT * 2, 128), (2, NT)]),
                    scalar1=1e-5, scalar2=None, op0=ALU.add))
                ph.drain("vector")
                ev_rec = ph.op("vector", lambda h: h.reciprocal(
                    out=vrec[:, :], in_=vtmp[:, :]), inc="v")
                ph.wait("scalar", ev_rec)
                ev_rstd = ph.op("scalar", lambda h: h.activation(
                    out=rstd[:, :], in_=vrec[:, :], func=AF.Sqrt), inc="s")
                ph.wait("vector", ev_rstd)
                ev_negmr = ph.op("vector", lambda h: h.scalar_tensor_tensor(
                    out=negmr[:, :], in0=_ap(mv, 0, [(NT * 2, 128), (2, NT)]),
                    scalar=-1.0, in1=rstd[:, :],
                    op0=ALU.mult, op1=ALU.mult), inc="v")
                ph.wait("scalar", ev_negmr)
                ph.drain("scalar")

                ev_slot = [None, None]
                for t in range(NT):
                    sl = t % 2
                    if ev_slot[sl] is not None:
                        ph.wait("scalar", ev_slot[sl])
                    ev_ap = ph.op("scalar", lambda h, t=t, sl=sl: h.activation(
                        out=htb[:, sl * DM: sl * DM + DM],
                        in_=xt[:, t * DM: t * DM + DM], func=AF.Identity,
                        bias=negmr[:, t: t + 1], scale=rstd[:, t: t + 1]),
                        inc="s")
                    ph.wait("tensor", ev_ap)
                    if ev_slot[sl] is not None:
                        ph.wait("tensor", ev_slot[sl])
                    for c in range(NDCH):
                        ph.op("tensor", lambda h, sl=sl, c=c: h.transpose(
                            p_h[:, sl * DM + c * 128: sl * DM + c * 128 + 128],
                            htb[:, sl * DM + c * 128: sl * DM + c * 128 + 128],
                            identb[:, :]),
                            inc="t" if c == NDCH - 1 else None)
                    ev_tr = tev()
                    ph.wait("vector", ev_tr)
                    ev_slot[sl] = ph.op("vector", lambda h, t=t, sl=sl: h.tensor_copy(
                        out=_ap(hf, t * 128,
                                [(NDCH * S, 128), (S, NDCH), (1, 128)]),
                        in_=p_h[:, sl * DM: sl * DM + DM]), inc="v")
                ph.emit()

        wb_ev = {}

        def unit_gu_src(u):
            if u < NE:
                return [(wg_r, u * DM * MIDR,
                         [(MIDR, 128), (128 * MIDR, NDCH), (1, 256)]),
                        (wu_r, u * DM * MIDR,
                         [(MIDR, 128), (128 * MIDR, NDCH), (1, 256)])]
            n, mh = (u - NE) // 2, (u - NE) % 2
            return [(wg_s, n * DM * MIDS + mh * 256,
                     [(MIDS, 128), (128 * MIDS, NDCH), (1, 256)]),
                    (wu_s, n * DM * MIDS + mh * 256,
                     [(MIDS, 128), (128 * MIDS, NDCH), (1, 256)])]

        def unit_d_src(u):
            if u < NE:
                return (wd_r, u * MIDR * DM,
                        [(DM, 128), (128 * DM, 2), (1, 1024)])
            n, mh = (u - NE) // 2, (u - NE) % 2
            return (wd_s, (n * MIDS + mh * 256) * DM,
                    [(DM, 128), (128 * DM, 2), (1, 1024)])

        def ffn_phase(tag, first_call):
            with ExitStack() as ctx:
                ph = Ph(nc, sems, cnt)
                blanket(ph)
                P = [PS(ctx, f"f{tag}_p{i}", [128, 1024], F32) for i in range(4)]
                p_all = SB(ctx, f"f{tag}_pall", [128, NU * 1024], BF16)
                gsil = SB(ctx, f"f{tag}_gs", [128, 2 * 1024], BF16)
                GSEM = ("dga", "dgb", "dgc")
                WSEM = ("dwa", "dwb", "dwc")
                DSEM = ("dda", "ddb", "ddc")
                BSEM = ("dba", "dbb", "dbc")
                WBSEM = ("dwba", "dwbb", "dwbc")
                DBSEM = ("ddba", "ddbb", "ddbc")
                WQUEUE = ("sync", "sync", "sync")
                t1 = SB(ctx, f"f{tag}_t1", [128, 1024], F32)
                bc_sb = SB(ctx, f"f{tag}_bc", [128, 3 * TB], F32)
                stage = SB(ctx, f"f{tag}_stg", [128, 2048], F32)
                lg = SB(ctx, f"f{tag}_lg", [128, NT * 16], F32)
                m8 = SB(ctx, f"f{tag}_m8", [128, NT * 8], F32)
                d21 = SB(ctx, f"f{tag}_d21", [128, NT], F32)
                e2 = SB(ctx, f"f{tag}_e2", [128, NT], F32)
                w1 = SB(ctx, f"f{tag}_w1", [128, NT], F32)
                w2 = SB(ctx, f"f{tag}_w2", [128, NT], F32)
                eqt = SB(ctx, f"f{tag}_eq", [128, 16], F32)
                comb = SB(ctx, f"f{tag}_cmb", [128, NT * 16], F32)
                comb_fm = SB(ctx, f"f{tag}_cfm", [16, S], F32)
                wgu = [SB(ctx, f"f{tag}_wgu{i}", [128, 2 * 2048], BF16)
                       for i in range(3)]
                wdb = [SB(ctx, f"f{tag}_wdb{i}", [128, 2048], BF16)
                       for i in range(3)]

                gu_dma, wd_dma, bc_dma = {}, {}, {}
                gu_free = [None, None, None]
                wd_free = [None, None, None]
                bc_free = [None, None, None]
                pre_issued = first_call

                ev_cp = [None] * 4
                for t in range(NT):
                    bk = t % 4
                    pb = P[bk // 2]
                    sl = (bk % 2) * 512 + (t // 4) * 16
                    ph.wait("tensor", ev_cp[bk])
                    for c in range(NDCH):
                        ph.op("tensor", lambda h, t=t, c=c, sl=sl, pb=pb: h.matmul(
                            pb[:, sl: sl + 16],
                            hf[:, c * S + t * 128: c * S + t * 128 + 128],
                            rw_im[:, c * 16: c * 16 + 16],
                            start=(c == 0), stop=(c == NDCH - 1)),
                            inc="t" if c == NDCH - 1 else None)
                    ev_mm = tev()
                    ph.wait("vector", ev_mm)
                    ev_cp[bk] = ph.op(
                        "vector", lambda h, t=t, sl=sl, pb=pb: h.tensor_copy(
                            out=lg[:, t * 16: t * 16 + 16], in_=pb[:, sl: sl + 16]),
                        inc="v")
                ph.drain("vector")
                for t in range(NT):
                    ph.op("vector", lambda h, t=t: h.max(
                        out=m8[:, t * 8: t * 8 + 8],
                        in_=lg[:, t * 16: t * 16 + 16]))
                ph.drain("vector")
                ev_d21 = ph.op("vector", lambda h: h.tensor_sub(
                    d21[:, :], _ap(m8, 1, [(NT * 8, 128), (8, NT)]),
                    _ap(m8, 0, [(NT * 8, 128), (8, NT)])), inc="v")
                ph.drain("vector")
                ph.wait("scalar", ev_d21)
                ev_e2 = ph.op("scalar", lambda h: h.activation(
                    out=e2[:, :], in_=d21[:, :], func=AF.Exp), inc="s")
                ph.wait("vector", ev_e2)
                ph.op("vector", lambda h: h.tensor_scalar(
                    out=w1[:, :], in0=e2[:, :], scalar1=1.0, scalar2=None,
                    op0=ALU.add))
                ph.drain("vector")
                ph.op("vector", lambda h: h.reciprocal(out=w2[:, :], in_=w1[:, :]))
                ph.drain("vector")
                ph.op("vector", lambda h: h.tensor_scalar(
                    out=w1[:, :], in0=w2[:, :], scalar1=-1.0, scalar2=1.0,
                    op0=ALU.mult, op1=ALU.add))
                ph.drain("vector")
                ev_ct = [None, None]
                ev_cf = None
                for t in range(NT):
                    ph.op("vector", lambda h, t=t: h.tensor_scalar(
                        out=eqt[:, :], in0=lg[:, t * 16: t * 16 + 16],
                        scalar1=m8[:, t * 8: t * 8 + 1], scalar2=None,
                        op0=ALU.is_equal))
                    ph.drain("vector")
                    ph.op("vector", lambda h, t=t: h.tensor_scalar(
                        out=comb[:, t * 16: t * 16 + 16], in0=eqt[:, :],
                        scalar1=w2[:, t: t + 1], scalar2=None, op0=ALU.mult))
                    ph.drain("vector")
                    ph.op("vector", lambda h, t=t: h.tensor_scalar(
                        out=eqt[:, :], in0=lg[:, t * 16: t * 16 + 16],
                        scalar1=m8[:, t * 8 + 1: t * 8 + 2], scalar2=None,
                        op0=ALU.is_equal))
                    ph.drain("vector")
                    ev_c = ph.op("vector", lambda h, t=t: h.scalar_tensor_tensor(
                        out=comb[:, t * 16: t * 16 + 16], in0=eqt[:, :],
                        scalar=w1[:, t: t + 1],
                        in1=comb[:, t * 16: t * 16 + 16],
                        op0=ALU.mult, op1=ALU.add), inc="v")
                    ph.drain("vector")
                    ph.wait("tensor", ev_c)
                    sl = 1 + (t % 2)
                    slot = (t // 2) % 8
                    if ev_ct[t % 2] is not None:
                        ph.wait("tensor", ev_ct[t % 2])
                    ev_t = ph.op("tensor", lambda h, t=t, sl=sl, slot=slot: h.transpose(
                        P[sl][0:16, slot * 128: slot * 128 + 128],
                        comb[:, t * 16: t * 16 + 16], idf32[:, :]), inc="t")
                    ph.wait("vector", ev_t)
                    ev_cf = ph.op("vector", lambda h, t=t, sl=sl, slot=slot: h.tensor_copy(
                        out=comb_fm[0:16, t * 128: t * 128 + 128],
                        in_=P[sl][0:16, slot * 128: slot * 128 + 128]), inc="v")
                    ev_ct[t % 2] = ev_cf
                ph.wait("sync", ev_cf)
                ev_comb_d = ph.dma("sync", comb_d[:, :], comb_fm[0:16, :], "dx")

                def issue_gu(tb, u, wonly=False, skipw=False):
                    buf = u % 3
                    sg = GSEM[buf]
                    sw = WSEM[buf]
                    fresh = first_call and tb == 0
                    if skipw:
                        pass
                    elif fresh:
                        if gu_free[buf] is not None:
                            ph.wait("gpsimd", gu_free[buf])
                        evs = []
                        for i, (ten, off, dims) in enumerate(unit_gu_src(u)):
                            evs.append(ph.dma(
                                "gpsimd",
                                _ap(wgu[buf], i * 2048,
                                    [(2 * 2048, 128), (256, NDCH), (1, 256)]),
                                _ap(ten, off, dims), sg))
                        gu_dma[(tb, u)] = evs
                    else:
                        qw = WQUEUE[buf]
                        if gu_free[buf] is not None:
                            ph.wait(qw, gu_free[buf])
                        ph.wait(qw, wb_ev[(u, 0)])
                        gu_dma[(tb, u)] = [ph.dma(
                            qw, wgu[buf][:, :],
                            _ap(wbf_d, (u * 3) * 128 * 2048,
                                [(2048, 128), (128 * 2048, 2), (1, 2048)]),
                            sw)]
                    if wonly:
                        return
                    if u < NE:
                        if bc_free[buf] is not None:
                            ph.wait("sync", bc_free[buf])
                        ph.wait("sync", ev_comb_d)
                        bc_dma[(tb, u)] = ph.dma(
                            "sync", bc_sb[:, buf * TB: buf * TB + TB],
                            _ap(comb_d, u * S + tb * TB, [(0, 128), (1, TB)]),
                            BSEM[buf])

                def issue_wd(tb, u):
                    buf = u % 3
                    sg = GSEM[buf]
                    sd = DSEM[buf]
                    fresh = first_call and tb == 0
                    if fresh:
                        if wd_free[buf] is not None:
                            ph.wait("gpsimd", wd_free[buf])
                        if gu_free[buf] is not None:
                            ph.wait("gpsimd", gu_free[buf])
                        ten, off, dims = unit_d_src(u)
                        wd_dma[(tb, u)] = [ph.dma(
                            "gpsimd", wdb[buf][:, :], _ap(ten, off, dims), sg)]
                    else:
                        qw = WQUEUE[buf]
                        if wd_free[buf] is not None:
                            ph.wait(qw, wd_free[buf])
                        ph.wait(qw, wb_ev[(u, 2)])
                        wd_dma[(tb, u)] = [ph.dma(
                            qw, wdb[buf][:, :],
                            _ap(wbf_d, (u * 3 + 2) * 128 * 2048,
                                [(2048, 128), (1, 2048)]), sd)]

                gsil_free = [None, None]
                ev_adds = vev()
                if first_call:
                    for u0 in range(3):
                        issue_gu(0, u0, wonly=True)

                for tb in range(NTB):
                    fresh = first_call and tb == 0
                    issue_gu(tb, 0, skipw=pre_issued)
                    issue_gu(tb, 1, skipw=pre_issued)
                    issue_gu(tb, 2, skipw=pre_issued)
                    pre_issued = False
                    last_pmult = None
                    last_sil = None
                    for u in range(NU):
                        buf = u % 2
                        wbuf = u % 3
                        pg, pu = P[2 * buf], P[2 * buf + 1]
                        ph.waits("tensor", gu_dma[(tb, u)])
                        if gsil_free[buf] is not None:
                            ph.wait("tensor", gsil_free[buf])
                        ph.wait("tensor", ev_adds)
                        for mc in range(2):
                            for c in range(NDCH):
                                ph.op("tensor", lambda h, buf=wbuf, mc=mc, c=c,
                                      pg=pg, tb=tb: h.matmul(
                                    pg[:, mc * 512: mc * 512 + 512],
                                    wgu[buf][:, c * 256 + mc * 128:
                                             c * 256 + mc * 128 + 128],
                                    hf[:, c * S + tb * TB: c * S + tb * TB + TB],
                                    start=(c == 0), stop=(c == NDCH - 1)),
                                    inc="t" if (mc == 1 and c == NDCH - 1) else None)
                        ev_g = tev()
                        for mc in range(2):
                            for c in range(NDCH):
                                ph.op("tensor", lambda h, buf=wbuf, mc=mc, c=c,
                                      pu=pu, tb=tb: h.matmul(
                                    pu[:, mc * 512: mc * 512 + 512],
                                    wgu[buf][:, 2048 + c * 256 + mc * 128:
                                             2048 + c * 256 + mc * 128 + 128],
                                    hf[:, c * S + tb * TB: c * S + tb * TB + TB],
                                    start=(c == 0), stop=(c == NDCH - 1)),
                                    inc="t" if (mc == 1 and c == NDCH - 1) else None)
                        ev_u = tev()
                        gu_free[wbuf] = ev_u
                        if fresh:
                            ph.wait("sync", ev_u)
                            wb_ev[(u, 0)] = ph.dma(
                                "sync",
                                _ap(wbf_d, (u * 3) * 128 * 2048,
                                    [(2048, 128), (128 * 2048, 2), (1, 2048)]),
                                wgu[wbuf][:, :], WBSEM[wbuf])
                            gu_free[wbuf] = wb_ev[(u, 0)]
                        ph.wait("scalar", ev_g)
                        if gsil_free[buf] is not None:
                            ph.wait("scalar", gsil_free[buf])
                        ev_sil = ph.op("scalar", lambda h, pg=pg, buf=buf: h.activation(
                            out=gsil[:, buf * 1024: buf * 1024 + 1024],
                            in_=pg[:, :], func=AF.Silu), inc="s")
                        last_sil = ev_sil
                        ph.wait("vector", ev_u)
                        ph.wait("vector", ev_sil)
                        if u < NE:
                            ph.wait("vector", bc_dma[(tb, u)])
                            ev_t1 = ph.op("vector", lambda h, pu=pu, buf=wbuf: h.tensor_tensor(
                                out=t1[:, :].rearrange("p (a b) -> p a b", a=2),
                                in0=pu[:, :].rearrange("p (a b) -> p a b", a=2),
                                in1=bc_sb[:, buf * TB: buf * TB + TB]
                                    .unsqueeze(1).to_broadcast([128, 2, TB]),
                                op=ALU.mult), inc="v")
                            bc_free[wbuf] = ev_t1
                            ph.drain("vector")
                            ev_p = ph.op("vector", lambda h, u=u, buf=buf: h.tensor_tensor(
                                out=p_all[:, u * 1024: u * 1024 + 1024],
                                in0=t1[:, :],
                                in1=gsil[:, buf * 1024: buf * 1024 + 1024],
                                op=ALU.mult), inc="v")
                        else:
                            ev_p = ph.op("vector", lambda h, u=u, pu=pu, buf=buf: h.tensor_tensor(
                                out=p_all[:, u * 1024: u * 1024 + 1024],
                                in0=pu[:, :],
                                in1=gsil[:, buf * 1024: buf * 1024 + 1024],
                                op=ALU.mult), inc="v")
                        ph.drain("vector")
                        gsil_free[buf] = ev_p
                        last_pmult = ev_p
                        if u + 3 < NU:
                            issue_gu(tb, u + 3)

                    issue_wd(tb, 0)
                    issue_wd(tb, 1)
                    issue_wd(tb, 2)
                    ph.wait("tensor", last_pmult)
                    ph.wait("tensor", last_sil)
                    for u in range(NU):
                        wbuf = u % 3
                        ph.waits("tensor", wd_dma[(tb, u)])
                        for dch in range(NDCH):
                            po = P[dch // 2]
                            col = (dch % 2) * 512
                            for mc in range(2):
                                ph.op("tensor", lambda h, buf=wbuf, dch=dch,
                                      mc=mc, po=po, col=col, u=u: h.matmul(
                                    po[:, col: col + 512],
                                    wdb[buf][:, mc * 1024 + dch * 128:
                                             mc * 1024 + dch * 128 + 128],
                                    p_all[:, u * 1024 + mc * 512:
                                          u * 1024 + mc * 512 + 512],
                                    start=(u == 0 and mc == 0),
                                    stop=(u == NU - 1 and mc == 1)),
                                    inc="t" if (dch == NDCH - 1 and mc == 1) else None)
                        ev_wdused = tev()
                        wd_free[wbuf] = ev_wdused
                        if fresh:
                            ph.wait("sync", ev_wdused)
                            wb_ev[(u, 2)] = ph.dma(
                                "sync",
                                _ap(wbf_d, (u * 3 + 2) * 128 * 2048,
                                    [(2048, 128), (1, 2048)]),
                                wdb[wbuf][:, :], DBSEM[wbuf])
                            wd_free[wbuf] = wb_ev[(u, 2)]
                        if u + 3 < NU:
                            issue_wd(tb, u + 3)
                    ev_down = tev()

                    ev_adds_l = None
                    ev_tr_prev = None
                    for dh in range(2):
                        ph.wait("scalar", ev_down)
                        if ev_tr_prev is not None:
                            ph.wait("scalar", ev_tr_prev)
                        ev_st = []
                        for j in range(2):
                            ev_st.append(ph.op(
                                "scalar", lambda h, dh=dh, j=j: h.activation(
                                    out=stage[:, j * 1024: j * 1024 + 1024],
                                    in_=P[2 * dh + j][:, :], func=AF.Identity,
                                    scale=0.5), inc="s"))
                        for j in range(2):
                            ph.wait("tensor", ev_st[j])
                            for sub in range(4):
                                for dl in range(2):
                                    ph.op("tensor",
                                          lambda h, j=j, sub=sub, dl=dl, dh=dh: h.transpose(
                                        P[2 * dh + j][:, (sub * 2 + dl) * 128:
                                                      (sub * 2 + dl) * 128 + 128],
                                        stage[:, j * 1024 + dl * 512 + sub * 128:
                                              j * 1024 + dl * 512 + sub * 128 + 128],
                                        idf32[:, :]),
                                        inc="t" if (sub == 3 and dl == 1) else None)
                            ev_tr = tev()
                            ev_tr_prev = ev_tr
                            ph.wait("vector", ev_tr)
                            for sub in range(4):
                                tile = tb * 4 + sub
                                for dl in range(2):
                                    dch = (2 * dh + j) * 2 + dl
                                    ev_adds_l = ph.op(
                                        "vector", lambda h, tile=tile, dch=dch,
                                        j=j, sub=sub, dl=dl, dh=dh: h.tensor_tensor(
                                            out=xt[:, tile * DM + dch * 128:
                                                   tile * DM + dch * 128 + 128],
                                            in0=xt[:, tile * DM + dch * 128:
                                                   tile * DM + dch * 128 + 128],
                                            in1=P[2 * dh + j][:, (sub * 2 + dl) * 128:
                                                              (sub * 2 + dl) * 128 + 128],
                                            op=ALU.add), inc="v")
                            ph.drain("vector")
                    ev_adds = ev_adds_l
                ph.emit()

        def attn_phase():
            with ExitStack() as ctx:
                ph = Ph(nc, sems, cnt)
                blanket(ph)
                etab_ready = ("dx", b0dx)
                af = SB(ctx, "at_af", [128, NDCH * S], BF16)
                kT = SB(ctx, "at_kT", [128, S], BF16)
                vT = SB(ctx, "at_vT", [128, S], BF16)
                qT = SB(ctx, "at_qT", [128, S], BF16)
                kM = SB(ctx, "at_kM", [128, S], BF16)
                qM = SB(ctx, "at_qM", [128, S], BF16)
                vtm = SB(ctx, "at_vtm", [128, 16 * 65], BF16)
                pt = SB(ctx, "at_pt", [128, 2 * 2048], BF16)
                pexp = SB(ctx, "at_pexp", [128, 2 * 2048], BF16)
                ebuf = [SB(ctx, f"at_e{i}", [128, 3968], F16) for i in range(2)]
                win = [SB(ctx, f"at_w{i}", [128, 3 * 1024], BF16) for i in range(2)]
                wgp = win
                sigb = pt
                gsg = SB(ctx, "at_gsg", [128, 2 * 512], F32)
                bca = SB(ctx, "at_bca", [128, 1024], F16)
                att_sb = SB(ctx, "at_attsb", [128, 2 * 512], F32)
                stag = SB(ctx, "at_stag", [64, 2 * 512], BF16)
                negmh = SB(ctx, "at_ngm", [128, 16], F32)
                mtmp = SB(ctx, "at_mt", [2, 16], F32)
                srec = SB(ctx, "at_srec", [128, 8], F32)
                rrec = SB(ctx, "at_rrec", [128, 8], F32)
                trow = SB(ctx, "at_tr", [1, 4], F32)
                srow = SB(ctx, "at_sr", [1, 4], F32)
                p_s = PS(ctx, "at_ps", [128, 2048], F32)
                p_pr = PS(ctx, "at_ppr", [128, 1024], F32)
                p_v = PS(ctx, "at_pv", [128, 1024], BF16)
                p_bc = PS(ctx, "at_pbc", [128, 8], F32)

                ev_ones = ph.op("vector", lambda h: h.memset(
                    _ap(vtm, 64, [(16 * 65, 128), (65, 16), (1, 1)]), 1.0),
                    inc="v")

                win_dma = {}
                win_free = [None, None]

                def issue_win(g):
                    buf = g % 2
                    if win_free[buf] is not None:
                        ph.wait("gpsimd", win_free[buf])
                    evs = []
                    for part in range(3):
                        evs.append(ph.dma(
                            "gpsimd",
                            _ap(win[buf], part * 1024,
                                [(3 * 1024, 128), (128, NDCH), (1, 128)]),
                            _ap(w_in, part * 1024 + g * 128,
                                [(3 * DM, 128), (128 * 3 * DM, NDCH), (1, 128)]),
                            "dga" if buf == 0 else "dgb"))
                    win_dma[g] = evs

                EQ = []
                for _g in range(NGRP):
                    for _hl in range(2):
                        EQ.append(_g * 2 + _hl)
                ESEM = ("dea", "deb")
                EQUEUE = ("sync", "scalar")
                e_dma_ev = [None] * len(EQ)
                e_free_ev = [None] * 2
                eq_ptr = [0]

                def issue_next_equad():
                    i = eq_ptr[0]
                    if i >= len(EQ):
                        return
                    eq_ptr[0] = i + 1
                    h_ = EQ[i]
                    buf = i % 2
                    qu = EQUEUE[buf]
                    if e_free_ev[buf] is not None:
                        ph.wait(qu, e_free_ev[buf])
                    ph.wait(qu, etab_ready)
                    e_dma_ev[i] = ph.dma(
                        qu, ebuf[buf][:, :],
                        _ap(etab_d, h_ * TAB,
                            [(1, 128), (1, 3968)]), ESEM[buf])

                issue_win(0)
                issue_win(1)
                for _ in range(2):
                    issue_next_equad()
                eq_use = [0]
                ev_attn_done = []
                odd_dma_evs = []
                ev_vtmfree = None
                pending = [None]
                pend_b = [None]
                pbc_free = [None, None]

                def flush_a():
                    if pending[0] is None:
                        return
                    (par, fh, fqg, fhl, ev_evc) = pending[0]
                    pending[0] = None
                    ph.wait("tensor", ev_evc)
                    ph.wait("tensor", pbc_free[par])
                    for j in range(4):
                        ph.op("tensor", lambda h, par=par, j=j: h.transpose(
                            p_bc[:, par * 4 + j: par * 4 + j + 1],
                            bass.AP(att_sb, 64 * 1024 + par * 512 + j,
                                    [[1024, 1], [4, 128]]),
                            idf32[64:65, 64:65]),
                            inc="t" if j == 3 else None)
                    ev_tr4 = tev()
                    ph.wait("vector", ev_tr4)
                    ev_rin = ph.op("vector", lambda h, par=par: h.reciprocal(
                        out=rrec[:, par * 4: par * 4 + 4],
                        in_=p_bc[:, par * 4: par * 4 + 4]), inc="v")
                    pbc_free[par] = ev_rin
                    dsem = "dra" if par == 0 else "drb"
                    ph.wait("gpsimd", att_ev[par])
                    ph.wait("gpsimd", ev_rin)
                    ev_st2 = ph.dma(
                        "gpsimd", _ap(rbc_d, par * 512, [(4, 128), (1, 4)]),
                        rrec[:, par * 4: par * 4 + 4], dsem)
                    ph.wait("gpsimd", ev_st2)
                    ev_bc = ph.dma(
                        "gpsimd", bca[:, par * 512: par * 512 + 512],
                        _ap(rbc_d, par * 512, [(0, 128), (1, 512)]), dsem)
                    pend_b[0] = (par, fh, fqg, fhl, ev_evc, ev_bc)

                def flush_b():
                    if pend_b[0] is None:
                        return
                    (par, fh, fqg, fhl, ev_evc, ev_bc) = pend_b[0]
                    pend_b[0] = None
                    ph.wait("vector", ev_bc)
                    ph.wait("vector", ev_evc)
                    if fhl == 0:
                        ev_at = ph.op(
                            "vector", lambda h, fh=fh, fqg=fqg,
                            par=par: h.tensor_tensor(
                                out=af[0:64, (fh // 2) * S + fqg * 512:
                                       (fh // 2) * S + fqg * 512 + 512],
                                in0=att_sb[0:64, par * 512: par * 512 + 512],
                                in1=bca[0:64, par * 512: par * 512 + 512],
                                op=ALU.mult), inc="v")
                        ev_attn_done.append(ev_at)
                    else:
                        ssl = fqg % 2
                        if len(odd_dma_evs) >= 2:
                            ph.wait("vector", odd_dma_evs[-1])
                        ev_at = ph.op(
                            "vector", lambda h, ssl=ssl, par=par: h.tensor_tensor(
                                out=stag[0:64, ssl * 512: ssl * 512 + 512],
                                in0=att_sb[0:64, par * 512: par * 512 + 512],
                                in1=bca[0:64, par * 512: par * 512 + 512],
                                op=ALU.mult), inc="v")
                        ph.wait("gpsimd", ev_at)
                        odd_dma_evs.append(ph.dma(
                            "gpsimd",
                            af[64:128, (fh // 2) * S + fqg * 512:
                               (fh // 2) * S + fqg * 512 + 512],
                            stag[0:64, ssl * 512: ssl * 512 + 512], "dst"))
                    att_ev[par] = ev_at

                def flush_pending():
                    flush_a()
                    flush_b()
                expAB = [None, None]
                mult_ev = [[None, None], [None, None]]
                pvlast = [None, None]
                pat_free = [None, None]
                att_ev = [None, None]

                for g in range(NGRP):
                    wbuf = g % 2
                    flush_pending()
                    ph.waits("tensor", win_dma[g])
                    ph.wait("tensor", pat_free[0])
                    ph.wait("tensor", pat_free[1])
                    ev_evac = [None, None]
                    for part in range(3):
                        for tch in range(4):
                            sl = tch % 2
                            if ev_evac[sl] is not None:
                                ph.wait("tensor", ev_evac[sl])
                            for c in range(NDCH):
                                ph.op("tensor", lambda h, part=part, tch=tch,
                                      c=c, wbuf=wbuf, sl=sl: h.matmul(
                                    p_pr[:, sl * 512: sl * 512 + 512],
                                    win[wbuf][:, part * 1024 + c * 128:
                                              part * 1024 + c * 128 + 128],
                                    hf[:, c * S + tch * 512:
                                       c * S + tch * 512 + 512],
                                    start=(c == 0), stop=(c == NDCH - 1)),
                                    inc="t" if c == NDCH - 1 else None)
                            ev_mm = tev()
                            if part == 0:
                                ph.wait("scalar", ev_mm)
                                ev_evac[sl] = ph.op(
                                    "scalar", lambda h, tch=tch, sl=sl: h.copy(
                                        out=kT[:, tch * 512: tch * 512 + 512],
                                        in_=p_pr[:, sl * 512: sl * 512 + 512]),
                                    inc="s")
                            elif part == 1:
                                ph.wait("scalar", ev_mm)
                                ev_evac[sl] = ph.op(
                                    "scalar", lambda h, tch=tch, sl=sl: h.copy(
                                        out=vT[:, tch * 512: tch * 512 + 512],
                                        in_=p_pr[:, sl * 512: sl * 512 + 512]),
                                    inc="s")
                            else:
                                ph.wait("vector", ev_mm)
                                ev_evac[sl] = ph.op(
                                    "vector", lambda h, tch=tch, sl=sl: h.tensor_scalar(
                                        out=_ap(qT, 2047 - tch * 512,
                                                [(S, 128), (-1, 512)]),
                                        in0=p_pr[:, sl * 512: sl * 512 + 512],
                                        scalar1=0.125, scalar2=None,
                                        op0=ALU.mult), inc="v")
                    ev_proj_s = sev()
                    ev_proj_v = vev()
                    pat_free[0] = ev_evac[0]
                    pat_free[1] = ev_evac[1]
                    win_free[wbuf] = tev()
                    if g + 2 < NGRP:
                        issue_win(g + 2)
                    ph.wait("scalar", tev())
                    ph.wait("scalar", ev_proj_s)
                    ph.wait("scalar", ev_proj_v)
                    for (dst, srct) in ((kM, kT), (qM, qT)):
                        for bnd in range(2):
                            ph.dma("scalar", dst[bnd * 64:(bnd + 1) * 64, :],
                                   srct[(1 - bnd) * 64:(2 - bnd) * 64, :],
                                   "dem")
                    ev_mirror = ("dem", cnt["dem"])

                    ph.wait("scalar", ev_proj_v)
                    ph.wait("scalar", pvlast[0])
                    ph.wait("scalar", pvlast[1])
                    sqb = pt
                    ev_sq = [None, None]
                    ev_sq[0] = ph.op("scalar", lambda h: h.activation(
                        out=sqb[:, 0:S], in_=qT[:, :], func=AF.Square), inc="s")
                    ev_sq[1] = ph.op("scalar", lambda h: h.activation(
                        out=sqb[:, S: 2 * S], in_=kT[:, :], func=AF.Square),
                        inc="s")
                    red_ev = [expAB[0], expAB[0], expAB[1], expAB[1]]
                    for src in range(2):
                        ph.wait("tensor", ev_sq[src])
                        for i in range(4):
                            ph.wait("tensor", red_ev[i])
                            ev_nm = ph.op("tensor", lambda h, i=i, src=src:
                                          h.matmul(
                                p_s[0:2, i * 512: i * 512 + 512],
                                ind2[:, :],
                                sqb[:, src * S + i * 512:
                                    src * S + i * 512 + 512],
                                start=True, stop=True), inc="t")
                            ph.wait("vector", ev_nm)
                            red_ev[i] = ph.op(
                                "vector", lambda h, i=i, src=src: h.reduce_max(
                                    out=mtmp[0:2, src * 4 + i: src * 4 + i + 1],
                                    in_=p_s[0:2, i * 512: i * 512 + 512],
                                    axis=AXX), inc="v")
                    expAB[0] = red_ev[1]
                    expAB[1] = red_ev[3]
                    ph.drain("vector")
                    ph.op("vector", lambda h: h.reduce_max(
                        out=mtmp[0:2, 8:9], in_=mtmp[0:2, 0:4], axis=AXX))
                    ph.drain("vector")
                    ph.op("vector", lambda h: h.reduce_max(
                        out=mtmp[0:2, 9:10], in_=mtmp[0:2, 4:8], axis=AXX))
                    ph.drain("vector")
                    ev_t2 = ph.op("vector", lambda h: h.tensor_mul(
                        mtmp[0:2, 10:11], mtmp[0:2, 8:9], mtmp[0:2, 9:10]),
                        inc="v")
                    ph.wait("tensor", ev_t2)
                    ph.wait("tensor", expAB[0])
                    ev_trp = ph.op("tensor", lambda h: h.transpose(
                        p_s[0:1, 0:2], mtmp[0:2, 10:11], idf32[0:2, 0:2]),
                        inc="t")
                    ph.wait("vector", ev_trp)
                    ev_trc = ph.op("vector", lambda h: h.tensor_copy(
                        out=trow[0:1, 0:2], in_=p_s[0:1, 0:2]), inc="v")
                    ph.wait("scalar", ev_trc)
                    ev_sqr = ph.op("scalar", lambda h: h.activation(
                        out=srow[0:1, 0:2], in_=trow[0:1, 0:2], func=AF.Sqrt),
                        inc="s")
                    ph.wait("vector", ev_sqr)
                    ev_trow = ph.op("vector", lambda h: h.tensor_scalar(
                        out=trow[0:1, 2:4], in0=srow[0:1, 0:2],
                        scalar1=-1.0, scalar2=-0.5,
                        op0=ALU.mult, op1=ALU.add), inc="v")
                    ph.wait("tensor", ev_trow)
                    ph.wait("tensor", ev_trc)
                    ev_bcm = ph.op("tensor", lambda h: h.matmul(
                        p_s[:, 0:2], onesf[0:1, :], trow[0:1, 2:4],
                        start=True, stop=True), inc="t")
                    ph.wait("scalar", ev_bcm)
                    ev_negmh = ph.op("scalar", lambda h, g=g: h.copy(
                        out=negmh[:, g * 2: g * 2 + 2], in_=p_s[:, 0:2]),
                        inc="s")
                    ph.wait("tensor", ev_negmh)
                    expAB[0] = None

                    for hl in range(2):
                        h_ = g * 2 + hl
                        base = hl * 64
                        if ev_vtmfree is not None:
                            ph.wait("tensor", ev_vtmfree)
                        ph.wait("tensor", ev_proj_s)
                        ph.wait("tensor", ev_proj_v)
                        for c in range(16):
                            ph.op("tensor", lambda h, c=c, base=base: h.transpose(
                                p_v[:, c * 64: c * 64 + 64],
                                vT[base: base + 64, c * 128: c * 128 + 128],
                                identb[base: base + 64, base: base + 64]),
                                inc="t" if c == 15 else None)
                        ev_vtr = tev()
                        ph.wait("vector", ev_vtr)
                        ph.wait("vector", ev_ones)
                        ev_vtm = ph.op("vector", lambda h: h.tensor_copy(
                            out=_ap(vtm, 0, [(16 * 65, 128), (65, 16), (1, 64)]),
                            in_=_ap(p_v, 0, [(1024, 128), (64, 16), (1, 64)])),
                            inc="v")
                        ev_vtmfree = ev_vtm

                        for qg in range(4):
                            hgi = h_ * 4 + qg
                            par = hgi % 2
                            ev_s = [[None, None] for _ in range(4)]
                            ev_exp = [[None, None] for _ in range(4)]
                            ev_pv = [None] * 4

                            def emit_pv(q, qg=qg, par=par, ev_pv=ev_pv,
                                        ev_vtm=ev_vtm):
                                for j in range(4):
                                    c = q * 4 + j
                                    ph.wait("tensor", mult_ev[q % 2][j // 2])
                                    if c == 0:
                                        ph.wait("tensor", pat_free[par])
                                        ph.wait("tensor", ev_vtm)
                                    ph.op("tensor", lambda h, c=c, q=q, j=j,
                                          par=par: h.matmul(
                                        p_pr[0:65, par * 512: par * 512 + 512],
                                        vtm[:, c * 65: c * 65 + 65],
                                        pt[:, (q % 2) * 2048 + j * 512:
                                           (q % 2) * 2048 + j * 512 + 512],
                                        start=(c == 0), stop=(c == 15)),
                                        inc="t" if j == 3 else None)
                                ev_pv[q] = tev()
                                pvlast[q % 2] = ev_pv[q]

                            mbase = 64 - base
                            for q in range(4):
                                ph.wait("tensor", expAB[0])
                                ph.wait("tensor", ev_mirror)
                                for j in range(2):
                                    c = q * 4 + j
                                    kk = kT if j % 2 == 0 else kM
                                    qq = qT if j % 2 == 0 else qM
                                    bb = base if j % 2 == 0 else mbase
                                    ph.op("tensor", lambda h, c=c, bb=bb,
                                          kk=kk, qq=qq, qg=qg: h.matmul(
                                        p_s[:, (c % 4) * 512: (c % 4) * 512 + 512],
                                        kk[bb: bb + 64, c * 128: c * 128 + 128],
                                        qq[bb: bb + 64, qg * 512: qg * 512 + 512],
                                        start=True, stop=True),
                                        inc="t" if j == 1 else None)
                                ev_sA = tev()
                                ph.wait("tensor", expAB[1])
                                for j in range(2, 4):
                                    c = q * 4 + j
                                    kk = kT if j % 2 == 0 else kM
                                    qq = qT if j % 2 == 0 else qM
                                    bb = base if j % 2 == 0 else mbase
                                    ph.op("tensor", lambda h, c=c, bb=bb,
                                          kk=kk, qq=qq, qg=qg: h.matmul(
                                        p_s[:, (c % 4) * 512: (c % 4) * 512 + 512],
                                        kk[bb: bb + 64, c * 128: c * 128 + 128],
                                        qq[bb: bb + 64, qg * 512: qg * 512 + 512],
                                        start=True, stop=True),
                                        inc="t" if j == 3 else None)
                                ev_sB = tev()
                                if q == 1:
                                    flush_a()
                                if q >= 1:
                                    emit_pv(q - 1)
                                for half_i, ev_sx in ((0, ev_sA), (1, ev_sB)):
                                    ph.wait("scalar", ev_sx)
                                    ph.wait("scalar", ev_negmh)
                                    ph.wait("scalar", mult_ev[q % 2][half_i])
                                    ev_e_ = ph.op(
                                        "scalar", lambda h, q=q, half_i=half_i,
                                        hh=h_: h.activation(
                                            out=pexp[:, (q % 2) * 2048 +
                                                     half_i * 1024:
                                                     (q % 2) * 2048 +
                                                     half_i * 1024 + 1024],
                                            in_=p_s[:, half_i * 1024:
                                                    half_i * 1024 + 1024],
                                            func=AF.Exp,
                                            bias=negmh[:, hh: hh + 1]),
                                        inc="s")
                                    expAB[half_i] = ev_e_
                                    ev_exp[q][half_i] = ev_e_
                                ei = eq_use[0]
                                ebuf_i = ebuf[ei % 2]
                                eoff = 512 * qg
                                for j in range(4):
                                    ph.wait("vector", ev_exp[q][j // 2])
                                    ph.wait("vector", e_dma_ev[ei])
                                    ph.wait("vector", pvlast[q % 2])
                                    mev = ph.op(
                                        "vector", lambda h, q=q, j=j,
                                        ebuf_i=ebuf_i: h.tensor_tensor(
                                            out=pt[:, (q % 2) * 2048 + j * 512:
                                                   (q % 2) * 2048 + j * 512
                                                   + 512],
                                            in0=pexp[:, (q % 2) * 2048 +
                                                     j * 512:
                                                     (q % 2) * 2048 +
                                                     j * 512 + 512],
                                            in1=ebuf_i[:, eoff + q * 512
                                                       + j * 128:
                                                       eoff + q * 512
                                                       + j * 128 + 512],
                                            op=ALU.mult), inc="v")
                                    if j % 2 == 1:
                                        mult_ev[q % 2][j // 2] = mev
                                if q == 3 and qg == 3:
                                    eq_use[0] = ei + 1
                                    e_free_ev[ei % 2] = mult_ev[q % 2][1]
                                    issue_next_equad()
                            emit_pv(3)
                            flush_b()
                            ph.wait("scalar", ev_pv[3])
                            ph.wait("scalar", att_ev[par])
                            ev_evc = ph.op("scalar", lambda h, par=par: h.copy(
                                out=att_sb[0:65, par * 512: par * 512 + 512],
                                in_=p_pr[0:65, par * 512: par * 512 + 512]),
                                inc="s")
                            pat_free[par] = ev_evc
                            pending[0] = (par, h_, qg, hl, ev_evc)

                flush_pending()
                wg_dma = {}
                wg_free = [None, None]

                def issue_wg(ja):
                    buf = ja % 2
                    if wg_free[buf] is not None:
                        ph.wait("gpsimd", wg_free[buf])
                    if win_free[buf] is not None:
                        ph.wait("gpsimd", win_free[buf])
                    evs = []
                    for i, j in enumerate((ja, ja + 8)):
                        evs.append(ph.dma(
                            "gpsimd",
                            _ap(wgp[buf], i * 1024,
                                [(3 * 1024, 128), (128, NDCH), (1, 128)]),
                            _ap(w_gate, j * 128,
                                [(2 * DM, 128), (128 * 2 * DM, NDCH), (1, 128)]),
                            "dga" if buf == 0 else "dgb"))
                    wg_dma[ja] = evs

                issue_wg(0)
                issue_wg(1)
                ph.waits("tensor", ev_attn_done)
                ph.waits("tensor", odd_dma_evs)
                ph.wait("tensor", expAB[0])
                ph.wait("tensor", expAB[1])
                ph.wait("tensor", pat_free[0])
                ph.wait("tensor", pat_free[1])
                ev_gsl = [None, None]
                ev_ppr_free = None
                for ja in range(8):
                    buf = ja % 2
                    ph.waits("tensor", wg_dma[ja])
                    for tch in range(4):
                        if ev_gsl[0] is not None:
                            ph.wait("tensor", ev_gsl[0])
                        if ev_gsl[1] is not None:
                            ph.wait("tensor", ev_gsl[1])
                        for i in range(2):
                            for c in range(NDCH):
                                ph.op("tensor", lambda h, ja=ja, tch=tch, i=i,
                                      c=c, buf=buf: h.matmul(
                                    p_s[:, i * 512: i * 512 + 512],
                                    wgp[buf][:, i * 1024 + c * 128:
                                             i * 1024 + c * 128 + 128],
                                    af[:, c * S + tch * 512:
                                       c * S + tch * 512 + 512],
                                    start=(c == 0), stop=(c == NDCH - 1)),
                                    inc="t" if c == NDCH - 1 else None)
                        ev_ag = tev()
                        ssl = tch % 2
                        ph.wait("scalar", ev_ag)
                        if ev_gsl[ssl] is not None:
                            ph.wait("scalar", ev_gsl[ssl])
                        ev_sig = ph.op("scalar", lambda h, ja=ja, ssl=ssl: h.activation(
                            out=sigb[:, ssl * 512: ssl * 512 + 512],
                            in_=p_s[:, 512:1024], func=AF.Sigmoid,
                            bias=bg_sb[:, 8 + ja: 9 + ja]), inc="s")
                        ph.wait("vector", ev_sig)
                        ev_gtd = ph.op("vector", lambda h, ja=ja, ssl=ssl: h.scalar_tensor_tensor(
                            out=_ap(gsg, ssl * 512 + 511,
                                    [(2 * 512, 128), (-1, 512)]),
                            in0=p_s[:, 0:512], scalar=bg_sb[:, ja: ja + 1],
                            in1=sigb[:, ssl * 512: ssl * 512 + 512],
                            op0=ALU.add, op1=ALU.mult), inc="v")
                        ph.drain("vector")
                        ev_gsl[ssl] = ev_gtd
                        ph.wait("tensor", ev_gtd)
                        if ev_ppr_free is not None:
                            ph.wait("tensor", ev_ppr_free)
                        for sub in range(4):
                            ph.op("tensor", lambda h, sub=sub, ssl=ssl: h.transpose(
                                p_pr[:, (ssl * 4 + sub) * 128:
                                     (ssl * 4 + sub) * 128 + 128],
                                gsg[:, ssl * 512 + sub * 128:
                                    ssl * 512 + sub * 128 + 128],
                                idf32[:, :]),
                                inc="t" if sub == 3 else None)
                        ev_tr = tev()
                        ph.wait("vector", ev_tr)
                        ev_add = None
                        for sub in range(4):
                            tile = (3 - tch) * 4 + sub
                            ev_add = ph.op("vector", lambda h, tile=tile, ja=ja,
                                           sub=sub, ssl=ssl: h.tensor_tensor(
                                out=xt[:, tile * DM + ja * 128:
                                       tile * DM + ja * 128 + 128],
                                in0=xt[:, tile * DM + ja * 128:
                                       tile * DM + ja * 128 + 128],
                                in1=p_pr[:, (ssl * 4 + sub) * 128:
                                         (ssl * 4 + sub) * 128 + 128],
                                op=ALU.add), inc="v")
                        ph.drain("vector")
                        ev_ppr_free = ev_add
                    wg_free[buf] = tev()
                    if ja + 2 < 8:
                        issue_wg(ja + 2)
                ph.emit()

        ln_phase(0)
        ffn_phase(0, first_call=True)
        if stop != "ffn1":
            ln_phase(1)
            attn_phase()
            if stop != "attn":
                ln_phase(2)
                ffn_phase(1, first_call=False)

        with ExitStack() as ctx:
            ph = Ph(nc, sems, cnt)
            ev = ph.dma("sync",
                        _ap(out_d, 0, [(DM, 128), (128 * DM, NT), (1, DM)]),
                        _ap(xt, 0, [(NT * DM, 128), (DM, NT), (1, DM)]), "dx")
            ph.wait("sync", ev)
            ph.op("sync", lambda h: h.nop())
            ph.emit()

    return nc


_CACHE = {}


def _consts():
    if "c" not in _CACHE:
        rel = (np.arange(TAB, dtype=np.float32) - (S - 1))
        ind2 = np.zeros((128, 2), dtype=ml_dtypes.bfloat16)
        ind2[0:64, 0] = 1
        ind2[64:128, 1] = 1
        reduc = np.zeros((128, H), dtype=np.float32)
        for p in range(128):
            reduc[p, p // 8] = 1.0
        _CACHE["c"] = {
            "identb": np.eye(128, dtype=ml_dtypes.bfloat16),
            "idf32": np.eye(128, dtype=np.float32),
            "onesf": np.ones((128, 128), dtype=np.float32),
            "ind2": ind2,
            "reduc": reduc,
            "relbc": np.broadcast_to(rel, (128, TAB)).copy(),
        }
    return _CACHE["c"]


def kernel(L0_attn_x, L0_attn_w_in, L0_attn_w_gate, L0_attn_b_gate,
           L0_attn_amplitudes, L0_attn_sharpness, L0_attn_offsets,
           L0_ffn_router_w, L0_ffn_wg_r, L0_ffn_wu_r, L0_ffn_wd_r,
           L0_ffn_wg_s, L0_ffn_wu_s, L0_ffn_wd_s):
    if "nc" not in _CACHE:
        _CACHE["nc"] = build_nc()
    nc = _CACHE["nc"]
    c = _consts()
    x = np.ascontiguousarray(np.asarray(L0_attn_x, dtype=np.float32))
    shared = {
        "w_in": np.ascontiguousarray(np.asarray(L0_attn_w_in, np.float32)),
        "w_gate": np.ascontiguousarray(np.asarray(L0_attn_w_gate, np.float32)),
        "b_gate": np.ascontiguousarray(np.asarray(L0_attn_b_gate, np.float32)),
        "amp": np.ascontiguousarray(np.asarray(L0_attn_amplitudes, np.float32)),
        "sharp": np.ascontiguousarray(np.asarray(L0_attn_sharpness, np.float32)),
        "off": np.ascontiguousarray(np.asarray(L0_attn_offsets, np.float32)),
        "router_w": np.ascontiguousarray(np.asarray(L0_ffn_router_w, np.float32)),
        "wg_r": np.ascontiguousarray(np.asarray(L0_ffn_wg_r, np.float32)),
        "wu_r": np.ascontiguousarray(np.asarray(L0_ffn_wu_r, np.float32)),
        "wd_r": np.ascontiguousarray(np.asarray(L0_ffn_wd_r, np.float32)),
        "wg_s": np.ascontiguousarray(np.asarray(L0_ffn_wg_s, np.float32)),
        "wu_s": np.ascontiguousarray(np.asarray(L0_ffn_wu_s, np.float32)),
        "wd_s": np.ascontiguousarray(np.asarray(L0_ffn_wd_s, np.float32)),
    }
    shared.update(c)
    in_maps = []
    for core in range(8):
        m = dict(shared)
        m["x"] = np.ascontiguousarray(x[core])
        in_maps.append(m)
    res = run_bass_kernel_spmd(nc, in_maps, core_ids=list(range(8)))
    return np.stack([r["out"] for r in res.results], axis=0)
